# version 02
# baseline (speedup 1.0000x reference)
import numpy as np
import ml_dtypes
import concourse.bass as bass
import concourse.mybir as mybir
from concourse.bass_utils import run_bass_kernel_spmd

F32 = mybir.dt.float32
BF16 = mybir.dt.bfloat16
AF = mybir.ActivationFunctionType
ALU = mybir.AluOpType
AX = mybir.AxisListType

NCORES = 8
D = 1024
DC = 8
T = 2048
NT = 16
TK = 4096
NTK = 32
NH = 16
HD = 64
E0 = 8
F0 = 4096
E1 = 16
I1 = 512
SH = 1024
EPS = 1e-5
ISQRT_D = 1.0 / 32.0
L1_SCALE = HD ** -0.5
RG = [[0, 1], [2, 3], [4, 5], [6, 7]]

NEG = -1.0e30
DEBUG_ATTN = False

_UID = [0]


def uid():
    _UID[0] += 1
    return _UID[0]


class Sem:
    def __init__(self, h):
        self.h = h
        self.v = 0

    def inc(self, n=1):
        self.v += n
        return self.v


def plan(sem, n, step=1):
    base = sem.v
    sem.v += n * step
    return [base + (i + 1) * step for i in range(n)]


class Ctx:

    def __init__(self, nc, block, sems):
        self.nc = nc
        self.block = block
        for k, v in sems.items():
            setattr(self, k, v)

    def barrier_vals(self):
        return [(s, s.v) for s in (self.sT, self.sV, self.sA, self.sG, self.sW,
                                   self.sX, self.sW0, self.sW1, self.sW2,
                                   self.sW3, self.sP)]


def emit_barrier(ctx):
    vals = ctx.barrier_vals()
    b = ctx.block

    def body(eng):
        for s, v in vals:
            if v > 0:
                eng.wait_ge(s.h, v)

    b.tensor(body)
    b.vector(body)
    b.scalar(body)
    b.gpsimd(body)
    b.sync(body)



def emit_ln(ctx, r, lnT):
    nc, b = ctx.nc, ctx.block
    with (nc.psum_tensor(f"tr_ps0_{uid()}", [128, 128], BF16) as tr_ps0,
          nc.psum_tensor(f"tr_ps1_{uid()}", [128, 128], BF16) as tr_ps1):
        _emit_ln_inner(ctx, r, lnT, [tr_ps0[:, :], tr_ps1[:, :]])
    emit_barrier(ctx)


def _emit_ln_inner(ctx, r, lnT, tr_ps):
    nc, b = ctx.nc, ctx.block
    r = dict(r)
    r["tr_ps"] = tr_ps
    x = r["x"]
    mean, rstd = r["mean"], r["rstd"]
    sq_a = plan(ctx.sA, NT)
    rs_a = plan(ctx.sA, NT)
    ev_a = plan(ctx.sA, NT * DC)
    st_v = plan(ctx.sV, NT)
    ln_v = plan(ctx.sV, NT)
    tr_t = plan(ctx.sT, NT * DC)

    def scalar1(sc):
        for t in range(NT):
            sc.activation(r["dump"][:, :], x[:, t * D:(t + 1) * D], AF.Square,
                          accum_out=r["s2"][:, t:t + 1]).then_inc(ctx.sA.h, 1)
        for t in range(NT):
            sc.wait_ge(ctx.sV.h, st_v[t])
            sc.activation(r["tc2"][:, 0:1], r["var"][:, t:t + 1], AF.Ln)
            sc.drain()
            sc.activation(rstd[:, t:t + 1], r["tc2"][:, 0:1], AF.Exp,
                          scale=-0.5).then_inc(ctx.sA.h, 1)
        for t in range(NT):
            for dc in range(DC):
                idx = t * DC + dc
                sc.wait_ge(ctx.sT.h, tr_t[idx])
                sc.copy(lnT[:, dc * T + t * 128: dc * T + (t + 1) * 128],
                        r["tr_ps"][idx % 2][:, :]).then_inc(ctx.sA.h, 1)

    b.scalar(scalar1)

    def vector1(v):
        for t in range(NT):
            v.wait_ge(ctx.sA.h, sq_a[t])
            v.reduce_sum(r["s1"][:, t:t + 1], x[:, t * D:(t + 1) * D], axis=AX.X)
            v.drain()
            v.tensor_scalar_mul(mean[:, t:t + 1], r["s1"][:, t:t + 1], 1.0 / D)
            v.drain()
            v.tensor_mul(r["tc"][:, 0:1], mean[:, t:t + 1], mean[:, t:t + 1])
            v.tensor_scalar(r["var"][:, t:t + 1], r["s2"][:, t:t + 1], 1.0 / D,
                            EPS, ALU.mult, ALU.add)
            v.drain()
            v.tensor_sub(r["var"][:, t:t + 1], r["var"][:, t:t + 1],
                         r["tc"][:, 0:1]).then_inc(ctx.sV.h, 1)
        for t in range(NT):
            v.wait_ge(ctx.sA.h, rs_a[t])
            if t >= 1:
                v.wait_ge(ctx.sT.h, tr_t[8 * t - 1])
            v.tensor_scalar(r["ln_tok"][:, 0:D], x[:, t * D:(t + 1) * D],
                            mean[:, t:t + 1], rstd[:, t:t + 1],
                            ALU.subtract, ALU.mult).then_inc(ctx.sV.h, 1)

    b.vector(vector1)

    def tensor1(te):
        for t in range(NT):
            te.wait_ge(ctx.sV.h, ln_v[t])
            for dc in range(DC):
                idx = t * DC + dc
                if idx >= 2:
                    te.wait_ge(ctx.sA.h, ev_a[idx - 2])
                te.transpose(r["tr_ps"][idx % 2][:, :],
                             r["ln_tok"][:, dc * 128:(dc + 1) * 128],
                             r["ident"][:, :]).then_inc(ctx.sT.h, 1)

    b.tensor(tensor1)



def emit_l0ffn(ctx, r, io):
    nc, b = ctx.nc, ctx.block
    x = r["x"]

    from contextlib import ExitStack
    with ExitStack() as es:
        ec = es.enter_context
        arena = ec(nc.sbuf_tensor(f"ffn0_arena_{uid()}", [128, (DC * T) + 32 * 1024], BF16))
        w1buf = ec(nc.sbuf_tensor(f"w1buf_{uid()}", [128, 2, DC, 128], BF16))
        w2buf = ec(nc.sbuf_tensor(f"w2buf_{uid()}", [128, 2, 8, 1024], BF16))
        rw_sb = ec(nc.sbuf_tensor(f"rw_sb_{uid()}", [128, DC, 9], BF16))
        rb_sb = ec(nc.sbuf_tensor(f"rb_sb_{uid()}", [1, 9], BF16))
        probs = ec(nc.sbuf_tensor(f"probs_{uid()}", [128, 16], F32))
        lnT = arena[:, 0:DC * T]
        h_sb = arena[:, DC * T:DC * T + 32 * 1024]

        emit_ln(ctx, r, lnT)
        router_es = ExitStack()
        r_ps = router_es.enter_context(nc.psum_tensor(f"r_ps_{uid()}", [128, 9], F32))

        coef = r["coef"]

        ctx.sP.inc(32)

        def g_pre(g):
            g.dma_start(out=rw_sb[:, :, :],
                        in_=io["l0_rw"].rearrange("(dc p) n -> p dc n", p=128)
                        ).then_inc(ctx.sP.h, 16)
            g.dma_start(out=rb_sb[:, :],
                        in_=io["l0_rb_row"]).then_inc(ctx.sP.h, 16)

        b.gpsimd(g_pre)
        w_pre = ctx.sP.v

        rt_t = plan(ctx.sT, NT)

        def t_router(te):
            te.wait_ge(ctx.sP.h, w_pre)
            for t in range(NT):
                if t >= 1:
                    te.wait_ge(ctx.sA.h, rt_ev[t - 1])
                for dc in range(DC):
                    te.matmul(r_ps[:, :],
                              lnT[:, dc * T + t * 128: dc * T + (t + 1) * 128],
                              rw_sb[:, dc, :],
                              start=(dc == 0), stop=False)
                te.matmul(r_ps[:, :], r["ones"][0:1, 0:128], rb_sb[0:1, :],
                          start=False, stop=True).then_inc(ctx.sT.h, 1)

        rt_ev = plan(ctx.sA, NT)
        gate_sched = [plan(ctx.sV, 3) for _ in range(NT)]

        b.tensor(t_router)

        def s_gate(sc):
            for t in range(NT):
                sc.wait_ge(ctx.sV.h, gate_sched[t][0])
                if t >= 1:
                    sc.wait_ge(ctx.sV.h, gate_sched[t - 1][1])
                sc.activation(probs[:, 0:9], r_ps[:, :], AF.Exp,
                              bias=r["tc2"][:, 0:1]).then_inc(ctx.sA.h, 1)

        b.scalar(s_gate)

        def v_gate(v):
            for t in range(NT):
                v.wait_ge(ctx.sT.h, rt_t[t])
                v.reduce_max(r["tc"][:, 0:1], r_ps[:, :], axis=AX.X)
                v.drain()
                v.tensor_scalar_mul(r["tc2"][:, 0:1], r["tc"][:, 0:1],
                                    -1.0).then_inc(ctx.sV.h, 1)
                v.wait_ge(ctx.sA.h, rt_ev[t])
                v.reduce_max(r["tc"][:, 0:1], probs[:, 0:9], axis=AX.X)
                v.drain()
                v.tensor_scalar(r["m9"][:, 0:9], probs[:, 0:9], r["tc"][:, 0:1],
                                NEG, ALU.is_ge, ALU.mult)
                v.drain()
                v.tensor_add(r["m9"][:, 0:9], r["m9"][:, 0:9], probs[:, 0:9])
                v.drain()
                v.reduce_max(r["tc"][:, 0:1], r["m9"][:, 0:9], axis=AX.X)
                v.drain()
                v.tensor_scalar(r["m9"][:, 0:9], probs[:, 0:9], r["tc"][:, 0:1],
                                None, ALU.is_ge, ALU.bypass)
                v.drain()
                v.tensor_mul(r["m9"][:, 0:9], r["m9"][:, 0:9], probs[:, 0:9])
                v.reduce_sum(r["tc"][:, 0:1], probs[:, 0:9], axis=AX.X)
                v.drain()
                v.reciprocal(r["tc2"][:, 0:1], r["tc"][:, 0:1])
                v.drain()
                v.tensor_scalar(coef[:, t * 16:t * 16 + 9], r["m9"][:, 0:9],
                                r["tc2"][:, 0:1], 0.5, ALU.mult,
                                ALU.mult).then_inc(ctx.sV.h, 1)
                v.drain()
                v.tensor_scalar(r["tc"][:, 0:1], coef[:, t * 16 + 8:t * 16 + 9],
                                r["rstd"][:, t:t + 1], None, ALU.mult, ALU.bypass)
                v.drain()
                v.tensor_scalar(r["dump"][:, 0:D], x[:, t * D:(t + 1) * D],
                                r["mean"][:, t:t + 1], r["tc"][:, 0:1],
                                ALU.subtract, ALU.mult)
                v.drain()
                v.tensor_add(x[:, t * D:(t + 1) * D], x[:, t * D:(t + 1) * D],
                             r["dump"][:, 0:D]).then_inc(ctx.sV.h, 1)

        b.vector(v_gate)

        emit_barrier(ctx)
        router_es.close()
        h_ps0 = ec(nc.psum_tensor(f"h_ps0_{uid()}", [128, 1024], F32))
        h_ps1 = ec(nc.psum_tensor(f"h_ps1_{uid()}", [128, 1024], F32))
        y_ps0 = ec(nc.psum_tensor(f"y_ps0_{uid()}", [128, 1024], F32))
        y_ps1 = ec(nc.psum_tensor(f"y_ps1_{uid()}", [128, 1024], F32))
        h_ps = [h_ps0, h_ps1]
        y_ps = [y_ps0, y_ps1]

        n_eh = E0 * 2
        w1_dma, w2_dma = [], []
        w1_sem = [ctx.sW0, ctx.sW1]
        w2_sem = [ctx.sW2, ctx.sW3]
        for i in range(n_eh * 32):
            w1_dma += plan(w1_sem[i % 2], 1, step=16)
        for j in range(n_eh * 4):
            w2_dma += plan(w2_sem[j % 2], 1, step=16)
        h_mm, y_mm = [], []
        for _eh in range(n_eh):
            h_mm += plan(ctx.sT, 32)
            y_mm += plan(ctx.sT, 32)
        h_ev = plan(ctx.sA, n_eh * 32)
        y_add = plan(ctx.sV, n_eh * 32)

        def g_weights(g):
            for e in range(E0):
                for half in range(2):
                    i_eh = e * 2 + half
                    for ft in range(32):
                        i = i_eh * 32 + ft
                        if i >= 2:
                            g.wait_ge(ctx.sT.h, h_mm[i - 2])
                        g.dma_start(
                            out=w1buf[:, i % 2, :, :],
                            in_=io["l0_w1"][e].rearrange("(dc p) f -> p dc f", p=128)
                            [:, :, ft * 128:(ft + 1) * 128],
                        ).then_inc(w1_sem[i % 2].h, 16)
                    for fcg in range(4):
                        j = i_eh * 4 + fcg
                        if j >= 2:
                            g.wait_ge(ctx.sT.h, y_mm[(j - 2) * 8 + 7])
                        g.dma_start(
                            out=w2buf[:, j % 2, :, :],
                            in_=io["l0_w2"][e][fcg * 1024:(fcg + 1) * 1024, :]
                            .rearrange("(c p) n -> p c n", p=128),
                        ).then_inc(w2_sem[j % 2].h, 16)

        b.gpsimd(g_weights)

        def t_exp(te):
            for e in range(E0):
                for half in range(2):
                    i_eh = e * 2 + half
                    for ft in range(32):
                        i = i_eh * 32 + ft
                        te.wait_ge(w1_sem[i % 2].h, w1_dma[i])
                        if i >= 2:
                            te.wait_ge(ctx.sA.h, h_ev[i - 2])
                        for dc in range(DC):
                            for nb in range(2):
                                ins = te.matmul(
                                    h_ps[i % 2][:, nb * 512:(nb + 1) * 512],
                                    w1buf[:, i % 2, dc, :],
                                    lnT[:, dc * T + half * 1024 + nb * 512:
                                        dc * T + half * 1024 + (nb + 1) * 512],
                                    start=(dc == 0), stop=(dc == DC - 1))
                                if dc == DC - 1 and nb == 1:
                                    ins.then_inc(ctx.sT.h, 1)
                    for fcg in range(4):
                        j = i_eh * 4 + fcg
                        te.wait_ge(w2_sem[j % 2].h, w2_dma[j])
                        te.wait_ge(ctx.sA.h, h_ev[i_eh * 32 + 31])
                        for tt in range(8):
                            jj = j * 8 + tt
                            if jj >= 2:
                                te.wait_ge(ctx.sV.h, y_add[jj - 2])
                            for fc in range(8):
                                fcol = (fcg * 8 + fc) * 1024 + tt * 128
                                for nb in range(2):
                                    ins = te.matmul(
                                        y_ps[jj % 2][:, nb * 512:(nb + 1) * 512],
                                        h_sb[:, fcol:fcol + 128],
                                        w2buf[:, j % 2, fc, nb * 512:(nb + 1) * 512],
                                        start=(fc == 0), stop=(fc == 7))
                                    if fc == 7 and nb == 1:
                                        ins.then_inc(ctx.sT.h, 1)

        b.tensor(t_exp)

        def s_gelu(sc):
            for e in range(E0):
                for half in range(2):
                    i_eh = e * 2 + half
                    for ft in range(32):
                        i = i_eh * 32 + ft
                        sc.wait_ge(ctx.sT.h, h_mm[i])
                        sc.activation(h_sb[:, ft * 1024:(ft + 1) * 1024],
                                      h_ps[i % 2][:, :], AF.Gelu
                                      ).then_inc(ctx.sA.h, 1)

        b.scalar(s_gelu)

        def v_acc(v):
            for e in range(E0):
                for half in range(2):
                    i_eh = e * 2 + half
                    for fcg in range(4):
                        j = i_eh * 4 + fcg
                        for tt in range(8):
                            jj = j * 8 + tt
                            v.wait_ge(ctx.sT.h, y_mm[jj])
                            g = half * 8 + tt
                            v.scalar_tensor_tensor(
                                x[:, g * D:(g + 1) * D], y_ps[jj % 2][:, :],
                                coef[:, g * 16 + e:g * 16 + e + 1],
                                x[:, g * D:(g + 1) * D],
                                ALU.mult, ALU.add).then_inc(ctx.sV.h, 1)

        b.vector(v_acc)

        emit_barrier(ctx)



def emit_l0attn(ctx, r, io, cc):
    nc, b = ctx.nc, ctx.block
    from contextlib import ExitStack
    with ExitStack() as es:
        ec = es.enter_context
        C0 = 32768 + 16384
        VO = 32 * 130
        arena = ec(nc.sbuf_tensor(f"attn_arena_{uid()}", [128, C0 + VO + 4096 + 4096], BF16))
        lnT = arena[:, 0:DC * T]
        kvT = arena[:, 0:32768].rearrange("p (dc t) -> p dc t", dc=DC)
        qT = arena[:, 32768:32768 + 16384]
        Vo = arena[:, C0:C0 + VO]
        vT = arena[:, C0 + VO:C0 + VO + 4096]
        kT = vT
        E2 = [arena[:, C0 + VO + 4096 + i * 2048:C0 + VO + 4096 + (i + 1) * 2048]
              for i in range(2)]
        o_sb = ec(nc.sbuf_tensor(f"o_sb_{uid()}", [128, 2048], BF16))
        wq_sb = ec(nc.sbuf_tensor(f"wq_sb_{uid()}", [128, 2, DC, 128], BF16))
        wv_sb = ec(nc.sbuf_tensor(f"wv_sb_{uid()}", [128, DC, 128], BF16))
        wk_sb = ec(nc.sbuf_tensor(f"wk_sb_{uid()}", [128, DC, 128], BF16))
        wo_sb = ec(nc.sbuf_tensor(f"wo_sb_{uid()}", [128, 1024], BF16))
        bq_sb = ec(nc.sbuf_tensor(f"bq_sb_{uid()}", [128, 8], F32))
        bk_sb = ec(nc.sbuf_tensor(f"bk_sb_{uid()}", [128, 8], F32))
        bv_sb = ec(nc.sbuf_tensor(f"bv_sb_{uid()}", [128, 8], F32))
        o_f = r["dump"][0:64, 0:512]
        dsc = r["dump"][0:64, 512:768].bitcast(BF16)[0:1, :]
        bo_sb = r["ln_tok"][0:1, 0:1024]

        emit_ln(ctx, r, lnT)

        snd_x = plan(ctx.sX, 4, step=16)

        def q_send(g):
            g.dma_start(out=cc["in"].ap().rearrange("(dc p) t -> p dc t", p=128),
                        in_=lnT.rearrange("p (dc t) -> p dc t", dc=DC)
                        ).then_inc(ctx.sX.h, 16)
            g.dma_start(out=bq_sb[:, :], in_=io["l0_bq"]).then_inc(ctx.sX.h, 16)
            g.dma_start(out=bk_sb[:, :], in_=io["l0_bk"]).then_inc(ctx.sX.h, 16)
            g.dma_start(out=bv_sb[:, :], in_=io["l0_bv"]).then_inc(ctx.sX.h, 16)

        b.sync(q_send)

        cc_done = plan(ctx.sG, 1)
        bo_dma = plan(ctx.sP, 1, step=16)
        wq_sem = [ctx.sW0, ctx.sW1]
        wq_dma = []
        for i in range(8):
            wq_dma += plan(wq_sem[i % 2], 1, step=16)
        q_mm = plan(ctx.sT, 8 * 4)
        q_ev = plan(ctx.sA, 8 * 4)

        def g_cc(g):
            g.wait_ge(ctx.sX.h, snd_x[3])
            g.collective_compute(
                "AllGather", ALU.bypass, replica_groups=cc["rg"],
                ins=[cc["in"].ap().opt()], outs=[cc["out"].ap().opt()],
            ).then_inc(ctx.sG.h, 1)
            g.dma_start(out=bo_sb[:, :], in_=io["l0_bo_row"]).then_inc(ctx.sP.h, 16)
            for pp in range(8):
                if pp >= 2:
                    g.wait_ge(ctx.sT.h, q_mm[(pp - 2) * 4 + 3])
                g.dma_start(out=wq_sb[:, pp % 2, :, :],
                            in_=io["l0_wqT"].rearrange("(dc p) f -> p dc f", p=128)
                            [:, :, pp * 128:(pp + 1) * 128]
                            ).then_inc(wq_sem[pp % 2].h, 16)

        b.gpsimd(g_cc)

        with (nc.psum_tensor(f"q_ps0_{uid()}", [128, 512], F32) as q_ps0,
              nc.psum_tensor(f"q_ps1_{uid()}", [128, 512], F32) as q_ps1):
            q_ps = [q_ps0[:, :], q_ps1[:, :]]

            def t_q(te):
                for pp in range(8):
                    te.wait_ge(wq_sem[pp % 2].h, wq_dma[pp])
                    for tb in range(4):
                        i = pp * 4 + tb
                        if i >= 2:
                            te.wait_ge(ctx.sA.h, q_ev[i - 2])
                        for dc in range(DC):
                            ins = te.matmul(q_ps[i % 2], wq_sb[:, pp % 2, dc, :],
                                            lnT[:, dc * T + tb * 512:dc * T + (tb + 1) * 512],
                                            start=(dc == 0), stop=(dc == DC - 1))
                        ins.then_inc(ctx.sT.h, 1)

            b.tensor(t_q)

            def s_q(sc):
                sc.wait_ge(ctx.sX.h, snd_x[3])
                for pp in range(8):
                    for tb in range(4):
                        i = pp * 4 + tb
                        sc.wait_ge(ctx.sT.h, q_mm[i])
                        sc.activation(qT[:, pp * 2048 + tb * 512:pp * 2048 + (tb + 1) * 512],
                                      q_ps[i % 2], AF.Identity,
                                      bias=bq_sb[:, pp:pp + 1]).then_inc(ctx.sA.h, 1)

            b.scalar(s_q)

        kv_x = plan(ctx.sW, 2, step=16)

        def g_kv(g):
            g.wait_ge(ctx.sG.h, cc_done[0])
            g.wait_ge(ctx.sA.h, q_ev[-1])
            for rk in range(2):
                g.dma_start(
                    out=kvT[:, :, rk * 2048:(rk + 1) * 2048],
                    in_=cc["out"].ap()[rk * 1024:(rk + 1) * 1024, :]
                    .rearrange("(dc p) t -> p dc t", p=128),
                ).then_inc(ctx.sW.h, 16)

        b.gpsimd(g_kv)

        vo_init = plan(ctx.sV, 1)

        def v_voinit(v):
            v.memset(Vo.rearrange("p (kc h j) -> p kc h j", h=2, j=65)[:, :, :, 64:65],
                     1.0).then_inc(ctx.sV.h, 1)

        b.vector(v_voinit)

        st = dict(kvT=kvT, qT=qT, Vo=Vo, vT=vT, kT=kT, E2=E2, o_sb=o_sb,
                  wv_sb=wv_sb, wk_sb=wk_sb, wo_sb=wo_sb, bk_sb=bk_sb, bv_sb=bv_sb,
                  bo_sb=bo_sb, dsc=dsc, o_f=o_f, kv_x=kv_x, vo_init=vo_init,
                  bo_dma=bo_dma)
        prev_last_v, prev_last_t = None, None
        for pp in range(8):
            prev_last_v, prev_last_t = _l0attn_pair(ctx, r, io, pp, st,
                                                    prev_last_v, prev_last_t)
            if DEBUG_ATTN and pp == 0:
                dbg3 = nc.dram_tensor("dbg3", [128, 2048], BF16,
                                      kind="ExternalOutput").ap()
                dbg4 = nc.dram_tensor("dbg4", [128, 4096], BF16,
                                      kind="ExternalOutput").ap()
                dbg5 = nc.dram_tensor("dbg5", [128, 4160], BF16,
                                      kind="ExternalOutput").ap()
                dx = plan(ctx.sX, 3, step=16)

                def q_dbg(g):
                    g.wait_ge(ctx.sV.h, prev_last_v)
                    g.dma_start(out=dbg3, in_=o_sb[:, :]).then_inc(ctx.sX.h, 16)
                    g.dma_start(out=dbg4, in_=kT[:, :]).then_inc(ctx.sX.h, 16)
                    g.dma_start(out=dbg5, in_=Vo[:, :]).then_inc(ctx.sX.h, 16)

                b.sync(q_dbg)
                emit_barrier(ctx)

    emit_barrier(ctx)


def _l0attn_pair(ctx, r, io, pp, s, prev_last_v, prev_last_t):
    nc, b = ctx.nc, ctx.block
    from contextlib import ExitStack
    x = r["x"]
    kvT, qT, Vo, vT, kT, E2, o_sb = (s["kvT"], s["qT"], s["Vo"], s["vT"], s["kT"],
                                     s["E2"], s["o_sb"])

    wdma = plan(ctx.sW, 3, step=16)

    def g_w(g):
        if prev_last_t is not None:
            g.wait_ge(ctx.sT.h, prev_last_t)
        g.dma_start(out=s["wv_sb"][:, :, :],
                    in_=io["l0_wvT"].rearrange("(dc p) f -> p dc f", p=128)
                    [:, :, pp * 128:(pp + 1) * 128]).then_inc(ctx.sW.h, 16)
        g.dma_start(out=s["wk_sb"][:, :, :],
                    in_=io["l0_wkT"].rearrange("(dc p) f -> p dc f", p=128)
                    [:, :, pp * 128:(pp + 1) * 128]).then_inc(ctx.sW.h, 16)
        g.dma_start(out=s["wo_sb"][:, :],
                    in_=io["l0_woT"][pp * 128:(pp + 1) * 128, :]).then_inc(ctx.sW.h, 16)

    b.gpsimd(g_w)

    v_mm = plan(ctx.sT, 8)
    tr_mm = plan(ctx.sT, 32)
    k_mm = plan(ctx.sT, 8)
    v_ev = plan(ctx.sA, 8)
    k_ev = plan(ctx.sA, 8)
    tr_ev = plan(ctx.sV, 32)

    with ExitStack() as es:
        ec = es.enter_context
        vk_ps = [ec(nc.psum_tensor(f"vk_ps{i}_{pp}_{uid()}", [128, 512], F32))[:, :]
                 for i in range(4)]
        t_ps = [ec(nc.psum_tensor(f"t_ps{i}_{pp}_{uid()}", [128, 128], BF16))[:, :]
                for i in range(2)]

        def t_a(te):
            te.wait_ge(ctx.sW.h, wdma[2])
            if prev_last_v is not None:
                te.wait_ge(ctx.sV.h, prev_last_v)
            for tbg in range(2):
                for dc in range(DC):
                    for tb4 in range(4):
                        i = tbg * 4 + tb4
                        if dc == 0 and tbg == 1:
                            te.wait_ge(ctx.sA.h, v_ev[i - 4])
                        ins = te.matmul(vk_ps[tb4], s["wv_sb"][:, dc, :],
                                        kvT[:, dc, tbg * 2048 + tb4 * 512:
                                            tbg * 2048 + (tb4 + 1) * 512],
                                        start=(dc == 0), stop=(dc == DC - 1))
                        if dc == DC - 1:
                            ins.then_inc(ctx.sT.h, 1)
            for kc in range(32):
                te.wait_ge(ctx.sA.h, v_ev[kc // 4])
                if kc >= 2:
                    te.wait_ge(ctx.sV.h, tr_ev[kc - 2])
                if kc == 0:
                    te.wait_ge(ctx.sV.h, s["vo_init"][0])
                te.transpose(t_ps[kc % 2], vT[:, kc * 128:(kc + 1) * 128],
                             r["ident"][:, :]).then_inc(ctx.sT.h, 1)
            for tbg in range(2):
                for dc in range(DC):
                    for tb4 in range(4):
                        i = tbg * 4 + tb4
                        if dc == 0:
                            te.wait_ge(ctx.sA.h, v_ev[7] if tbg == 0 else k_ev[i - 4])
                        ins = te.matmul(vk_ps[tb4], s["wk_sb"][:, dc, :],
                                        kvT[:, dc, tbg * 2048 + tb4 * 512:
                                            tbg * 2048 + (tb4 + 1) * 512],
                                        start=(dc == 0), stop=(dc == DC - 1))
                        if dc == DC - 1:
                            ins.then_inc(ctx.sT.h, 1)

        b.tensor(t_a)

        def s_a(sc):
            for i in range(8):
                sc.wait_ge(ctx.sT.h, v_mm[i])
                sc.activation(vT[:, i * 512:(i + 1) * 512], vk_ps[i % 4],
                              AF.Identity, bias=s["bv_sb"][:, pp:pp + 1]
                              ).then_inc(ctx.sA.h, 1)
            for i in range(8):
                sc.wait_ge(ctx.sT.h, k_mm[i])
                if i == 0:
                    sc.wait_ge(ctx.sT.h, tr_mm[31])
                sc.activation(kT[:, i * 512:(i + 1) * 512], vk_ps[i % 4],
                              AF.Identity, bias=s["bk_sb"][:, pp:pp + 1]
                              ).then_inc(ctx.sA.h, 1)

        b.scalar(s_a)

        def v_a(v):
            for kc in range(32):
                v.wait_ge(ctx.sT.h, tr_mm[kc])
                dst = Vo[:, kc * 130:(kc + 1) * 130].rearrange(
                    "p (h j) -> p h j", h=2)[:, :, 0:64]
                src = t_ps[kc % 2].rearrange("p (h j) -> p h j", h=2)
                v.tensor_copy(dst, src).then_inc(ctx.sV.h, 1)

        b.vector(v_a)

    sc_mm, pv_mm, den_mm, ex_ev, den_v = [], [], [], [], []
    for qb in range(4):
        sq, pq, xq = [], [], []
        for kc in range(32):
            sq += plan(ctx.sT, 1)
            pq += plan(ctx.sT, 1)
        sc_mm.append(sq)
        pv_mm.append(pq)
        den_mm.append(plan(ctx.sT, 2))
        ex_ev.append(plan(ctx.sA, 32))
        den_v.append(plan(ctx.sV, 6))
    p_mm = plan(ctx.sT, NT)
    p_add = plan(ctx.sV, NT)

    with ExitStack() as es:
        ec = es.enter_context
        s_ps = [ec(nc.psum_tensor(f"s_ps{i}_{pp}_{uid()}", [128, 1024], F32))[:, :]
                for i in range(2)]
        o_ps = [ec(nc.psum_tensor(f"o_ps{i}_{pp}_{uid()}", [65, 512], F32))[:, :]
                for i in range(2)]

        def t_b(te):
            for qb in range(4):
                for kc in range(32):
                    if kc >= 2:
                        te.wait_ge(ctx.sA.h, ex_ev[qb][kc - 2])
                    elif qb >= 1:
                        te.wait_ge(ctx.sA.h, ex_ev[qb - 1][30 + kc])
                        if kc == 0:
                            te.wait_ge(ctx.sV.h, den_v[qb - 1][5])
                    else:
                        if kc == 0:
                            te.wait_ge(ctx.sA.h, k_ev[7])
                            te.wait_ge(ctx.sV.h, tr_ev[31])
                    for hp in range(2):
                        ins = te.matmul(
                            s_ps[kc % 2][:, hp * 512:(hp + 1) * 512],
                            kT[hp * 64:(hp + 1) * 64, kc * 128:(kc + 1) * 128],
                            qT[hp * 64:(hp + 1) * 64,
                               pp * 2048 + qb * 512:pp * 2048 + (qb + 1) * 512],
                            start=True, stop=True)
                    ins.then_inc(ctx.sT.h, 1)
                    te.wait_ge(ctx.sA.h, ex_ev[qb][kc])
                    for hp in range(2):
                        ins = te.matmul(
                            o_ps[hp],
                            Vo[:, kc * 130 + hp * 65:kc * 130 + hp * 65 + 65],
                            E2[kc % 2][:, hp * 512:(hp + 1) * 512],
                            start=(kc == 0), stop=(kc == 31))
                    ins.then_inc(ctx.sT.h, 1)
                for hp in range(2):
                    te.wait_ge(ctx.sV.h, den_v[qb][hp * 3 + 0])
                    te.matmul(s_ps[0][0:64, 0:512], r["ones"][0:1, 0:64],
                              s["dsc"][0:1, :], start=True, stop=True
                              ).then_inc(ctx.sT.h, 1)
            te.wait_ge(ctx.sW.h, wdma[2])
            if pp == 0:
                te.wait_ge(ctx.sP.h, s["bo_dma"][0])
            for tt in range(NT):
                if tt == 0:
                    te.wait_ge(ctx.sV.h, den_v[3][5])
                if tt >= 2:
                    te.wait_ge(ctx.sV.h, p_add[tt - 2])
                for nb in range(2):
                    ins = te.matmul(s_ps[tt % 2][:, nb * 512:(nb + 1) * 512],
                                    o_sb[:, tt * 128:(tt + 1) * 128],
                                    s["wo_sb"][:, nb * 512:(nb + 1) * 512],
                                    start=True, stop=(pp != 0))
                    if pp == 0:
                        ins = te.matmul(s_ps[tt % 2][:, nb * 512:(nb + 1) * 512],
                                        r["ones"][0:1, 0:128],
                                        s["bo_sb"][0:1, nb * 512:(nb + 1) * 512],
                                        start=False, stop=True)
                ins.then_inc(ctx.sT.h, 1)

        b.tensor(t_b)

        def s_b(sc):
            for qb in range(4):
                for kc in range(32):
                    sc.wait_ge(ctx.sT.h, sc_mm[qb][kc])
                    if not (qb == 0 and kc < 2):
                        pq2, pkc = (qb, kc - 2) if kc >= 2 else (qb - 1, 30 + kc)
                        sc.wait_ge(ctx.sT.h, pv_mm[pq2][pkc])
                    sc.activation(E2[kc % 2][:, 0:1024], s_ps[kc % 2], AF.Exp,
                                  scale=ISQRT_D).then_inc(ctx.sA.h, 1)

        b.scalar(s_b)

        def v_b(v):
            for qb in range(4):
                for hp in range(2):
                    v.wait_ge(ctx.sT.h, pv_mm[qb][31])
                    with nc.allow_low_precision(reason="bf16 softmax denom"):
                        v.reciprocal(s["dsc"][0:1, :], o_ps[hp][64:65, :]
                                     ).then_inc(ctx.sV.h, 1)
                    v.tensor_copy(s["o_f"][:, :], o_ps[hp][0:64, :]
                                  ).then_inc(ctx.sV.h, 1)
                    v.wait_ge(ctx.sT.h, den_mm[qb][hp])
                    v.drain()
                    v.tensor_mul(
                        o_sb[hp * 64:(hp + 1) * 64, qb * 512:(qb + 1) * 512],
                        s["o_f"][:, :], s_ps[0][0:64, 0:512]).then_inc(ctx.sV.h, 1)
            for tt in range(NT):
                v.wait_ge(ctx.sT.h, p_mm[tt])
                v.tensor_add(x[:, tt * D:(tt + 1) * D], x[:, tt * D:(tt + 1) * D],
                             s_ps[tt % 2]).then_inc(ctx.sV.h, 1)

        b.vector(v_b)

    return p_add[-1], p_mm[-1]



def emit_l1ffn(ctx, r, io):
    nc, b = ctx.nc, ctx.block
    x = r["x"]
    from contextlib import ExitStack
    with ExitStack() as es:
        ec = es.enter_context
        arena = ec(nc.sbuf_tensor(f"ffn1_arena_{uid()}", [128, 32768], BF16))
        lnT = arena[:, 0:16384]
        hgs = arena[:, 16384:24576]
        act = arena[:, 24576:32768]
        gw_sb = ec(nc.sbuf_tensor(f"gw_{uid()}", [128, DC, 16], BF16))
        gbr_sb = ec(nc.sbuf_tensor(f"gbr_{uid()}", [1, 16], BF16))
        gb_sb = ec(nc.sbuf_tensor(f"gb_{uid()}", [128, 16], F32))
        scores = ec(nc.sbuf_tensor(f"scor_{uid()}", [128, NT * 16], F32))
        sfc = ec(nc.sbuf_tensor(f"sfc_{uid()}", [128, 16], F32))
        msk = ec(nc.sbuf_tensor(f"msk_{uid()}", [128, 16], F32))
        mskc = ec(nc.sbuf_tensor(f"mskc_{uid()}", [128, 16], F32))
        gs4 = ec(nc.sbuf_tensor(f"gs4_{uid()}", [128, 4], F32))
        t4 = ec(nc.sbuf_tensor(f"t4_{uid()}", [128, 4], F32))
        coef = r["coef"]

        emit_ln(ctx, r, lnT)

        gpre = plan(ctx.sP, 2, step=16)

        def g_pre(g):
            g.dma_start(out=gw_sb[:, :, :],
                        in_=io["l1f_gwT"].rearrange("(dc p) n -> p dc n", p=128)
                        ).then_inc(ctx.sP.h, 16)
            g.dma_start(out=gbr_sb[:, :], in_=io["l1f_gb_row"]).then_inc(ctx.sP.h, 16)

        b.gpsimd(g_pre)

        gbb_mm = plan(ctx.sT, 1)
        g_mm = plan(ctx.sT, NT)
        sig_a = plan(ctx.sA, NT)
        gbcp_v = plan(ctx.sV, 1)
        gate_v = plan(ctx.sV, NT)

        with nc.psum_tensor(f"g_ps_{uid()}", [128, 16], F32) as g_ps:
            def t_gate(te):
                te.wait_ge(ctx.sP.h, gpre[1])
                te.matmul(g_ps[:, :], r["ones"][0:1, 0:128], gbr_sb[0:1, :],
                          start=True, stop=True).then_inc(ctx.sT.h, 1)
                for t in range(NT):
                    if t == 0:
                        te.wait_ge(ctx.sV.h, gbcp_v[0])
                    else:
                        te.wait_ge(ctx.sA.h, sig_a[t - 1])
                    for dc in range(DC):
                        ins = te.matmul(g_ps[:, :],
                                        lnT[:, dc * T + t * 128:dc * T + (t + 1) * 128],
                                        gw_sb[:, dc, :],
                                        start=(dc == 0), stop=(dc == DC - 1))
                    ins.then_inc(ctx.sT.h, 1)

            b.tensor(t_gate)

            def s_gate(sc):
                for t in range(NT):
                    sc.wait_ge(ctx.sT.h, g_mm[t])
                    sc.activation(scores[:, t * 16:(t + 1) * 16], g_ps[:, :],
                                  AF.Sigmoid).then_inc(ctx.sA.h, 1)

            b.scalar(s_gate)

            def v_gate(v):
                v.wait_ge(ctx.sT.h, gbb_mm[0])
                v.tensor_copy(gb_sb[:, :], g_ps[:, :]).then_inc(ctx.sV.h, 1)
                for t in range(NT):
                    sco = scores[:, t * 16:(t + 1) * 16]
                    v.wait_ge(ctx.sA.h, sig_a[t])
                    v.tensor_add(sfc[:, :], sco, gb_sb[:, :])
                    v.drain()
                    for g4 in range(4):
                        sf_g = sfc[:, g4 * 4:(g4 + 1) * 4]
                        v.reduce_max(r["tc"][:, 0:1], sf_g, axis=AX.X)
                        v.drain()
                        v.tensor_scalar(t4[:, :], sf_g, r["tc"][:, 0:1], NEG,
                                        ALU.is_ge, ALU.mult)
                        v.drain()
                        v.tensor_add(t4[:, :], t4[:, :], sf_g)
                        v.drain()
                        v.reduce_max(r["tc2"][:, 0:1], t4[:, :], axis=AX.X)
                        v.drain()
                        v.tensor_add(gs4[:, g4:g4 + 1], r["tc"][:, 0:1],
                                     r["tc2"][:, 0:1])
                        v.drain()
                    v.reduce_max(r["tc"][:, 0:1], gs4[:, :], axis=AX.X)
                    v.drain()
                    v.tensor_scalar(t4[:, :], gs4[:, :], r["tc"][:, 0:1], NEG,
                                    ALU.is_ge, ALU.mult)
                    v.drain()
                    v.tensor_add(t4[:, :], t4[:, :], gs4[:, :])
                    v.drain()
                    v.reduce_max(r["tc2"][:, 0:1], t4[:, :], axis=AX.X)
                    v.drain()
                    v.tensor_scalar(t4[:, :], gs4[:, :], r["tc2"][:, 0:1], None,
                                    ALU.is_ge, ALU.bypass)
                    v.drain()
                    v.tensor_scalar(t4[:, :], t4[:, :], 1.0, 1.0e30,
                                    ALU.subtract, ALU.mult)
                    v.drain()
                    for g4 in range(4):
                        v.tensor_scalar_add(msk[:, g4 * 4:(g4 + 1) * 4],
                                            sfc[:, g4 * 4:(g4 + 1) * 4],
                                            t4[:, g4:g4 + 1])
                    v.drain()
                    v.tensor_copy(mskc[:, :], msk[:, :])
                    v.drain()
                    for it in range(4):
                        v.reduce_max(r["tc"][:, 0:1], msk[:, :], axis=AX.X)
                        v.drain()
                        if it < 3:
                            v.tensor_scalar(r["m9"][:, 0:16], msk[:, :],
                                            r["tc"][:, 0:1], NEG,
                                            ALU.is_ge, ALU.mult)
                            v.drain()
                            v.tensor_add(msk[:, :], msk[:, :], r["m9"][:, 0:16])
                            v.drain()
                    v.tensor_scalar(r["m9"][:, 0:16], mskc[:, :], r["tc"][:, 0:1],
                                    None, ALU.is_ge, ALU.bypass)
                    v.drain()
                    v.tensor_mul(r["m9"][:, 0:16], r["m9"][:, 0:16], sco)
                    v.drain()
                    v.reduce_sum(r["tc"][:, 0:1], r["m9"][:, 0:16], axis=AX.X)
                    v.drain()
                    v.tensor_scalar_add(r["tc"][:, 0:1], r["tc"][:, 0:1], 1.0e-20)
                    v.drain()
                    v.reciprocal(r["tc2"][:, 0:1], r["tc"][:, 0:1])
                    v.drain()
                    v.tensor_scalar(coef[:, t * 16:(t + 1) * 16], r["m9"][:, 0:16],
                                    r["tc2"][:, 0:1], 1.25, ALU.mult, ALU.mult)
                    v.drain()
                    v.tensor_scalar_mul(r["tc"][:, 0:1], r["rstd"][:, t:t + 1], 0.5)
                    v.drain()
                    v.tensor_scalar(r["dump"][:, 0:D], x[:, t * D:(t + 1) * D],
                                    r["mean"][:, t:t + 1], r["tc"][:, 0:1],
                                    ALU.subtract, ALU.mult)
                    v.drain()
                    v.tensor_add(x[:, t * D:(t + 1) * D], x[:, t * D:(t + 1) * D],
                                 r["dump"][:, 0:D]).then_inc(ctx.sV.h, 1)

            b.vector(v_gate)

        emit_barrier(ctx)

        expert_es = ExitStack()
        wg_buf = expert_es.enter_context(
            nc.sbuf_tensor(f"wg_{uid()}", [128, 2, DC, 512], BF16))
        wu_buf = expert_es.enter_context(
            nc.sbuf_tensor(f"wu_{uid()}", [128, 2, DC, 512], BF16))
        wd_buf = expert_es.enter_context(
            nc.sbuf_tensor(f"wd_{uid()}", [128, 2, 4, 1024], BF16))
        we_sem = [ctx.sW0, ctx.sW1]
        up_mm, act_a, act_v, dn_mm, dn_v, wdma = [], [], [], [], [], []
        for e in range(E1):
            wdma.append(plan(we_sem[e % 2], 3, step=16))
            up_mm.append(plan(ctx.sT, 8))
            dn_mm.append(plan(ctx.sT, NT))
            act_a.append(plan(ctx.sA, 4))
            act_v.append(plan(ctx.sV, 4))
            dn_v.append(plan(ctx.sV, NT))

        def g_we(g):
            for e in range(E1):
                if e >= 2:
                    g.wait_ge(ctx.sV.h, dn_v[e - 2][-1])
                g.dma_start(out=wg_buf[:, e % 2, :, :],
                            in_=io["l1f_wgT"][e].rearrange("(dc p) n -> p dc n", p=128)
                            ).then_inc(we_sem[e % 2].h, 16)
                g.dma_start(out=wu_buf[:, e % 2, :, :],
                            in_=io["l1f_wuT"][e].rearrange("(dc p) n -> p dc n", p=128)
                            ).then_inc(we_sem[e % 2].h, 16)
                g.dma_start(out=wd_buf[:, e % 2, :, :],
                            in_=io["l1f_wdT"][e].rearrange("(ic p) n -> p ic n", p=128)
                            ).then_inc(we_sem[e % 2].h, 16)

        b.gpsimd(g_we)

        for e in range(E1):
            _l1ffn_expert(ctx, r, e, dict(
                lnT=lnT, hgs=hgs, act=act, coef=coef,
                wg=wg_buf[:, e % 2, :, :], wu=wu_buf[:, e % 2, :, :],
                wd=wd_buf[:, e % 2, :, :],
                wdma=wdma[e], up_mm=up_mm[e], dn_mm=dn_mm[e],
                act_a=act_a[e], act_v=act_v[e], dn_v=dn_v[e],
                we_sem=we_sem[e % 2],
                prev_dn_v=(dn_v[e - 1][-1] if e > 0 else None)))

        expert_es.close()
        _l1ffn_shared(ctx, r, io, dict(lnT=lnT, hgs=hgs, act=act,
                                       prev_dn_v=dn_v[-1][-1]))

    emit_barrier(ctx)


def _l1ffn_expert(ctx, r, e, s):
    nc, b = ctx.nc, ctx.block
    x = r["x"]
    lnT, hgs, act = s["lnT"], s["hgs"], s["act"]
    from contextlib import ExitStack
    with ExitStack() as es:
        ec = es.enter_context
        up_ps = [ec(nc.psum_tensor(f"up{i}_{e}_{uid()}", [128, 512], F32))[:, :]
                 for i in range(4)]

        def t_up(te):
            te.wait_ge(s["we_sem"].h, s["wdma"][2])
            if s["prev_dn_v"] is not None:
                te.wait_ge(ctx.sV.h, s["prev_dn_v"])
            for kind in range(2):
                w = s["wg"] if kind == 0 else s["wu"]
                for ic in range(4):
                    i = kind * 4 + ic
                    if kind == 1:
                        if ic >= 1:
                            te.wait_ge(ctx.sV.h, s["act_v"][ic - 1])
                        else:
                            te.wait_ge(ctx.sA.h, s["act_a"][3])
                    for dc in range(DC):
                        for tb in range(4):
                            if dc == 0 and i >= 1 and kind == 0 and tb == 0 and ic >= 1:
                                te.wait_ge(ctx.sA.h, s["act_a"][ic - 1])
                            ins = te.matmul(
                                up_ps[tb], w[:, dc, ic * 128:(ic + 1) * 128],
                                lnT[:, dc * T + tb * 512:dc * T + (tb + 1) * 512],
                                start=(dc == 0), stop=(dc == DC - 1))
                        if dc == DC - 1:
                            ins.then_inc(ctx.sT.h, 1)

        b.tensor(t_up)

        def s_up(sc):
            for ic in range(4):
                sc.wait_ge(ctx.sT.h, s["up_mm"][ic])
                for tb in range(4):
                    ins = sc.activation(hgs[:, ic * 2048 + tb * 512:
                                            ic * 2048 + (tb + 1) * 512],
                                        up_ps[tb], AF.Silu)
                ins.then_inc(ctx.sA.h, 1)

        b.scalar(s_up)

        def v_up(v):
            for ic in range(4):
                v.wait_ge(ctx.sT.h, s["up_mm"][4 + ic])
                for tb in range(4):
                    ins = v.tensor_tensor(
                        act[:, ic * 2048 + tb * 512:ic * 2048 + (tb + 1) * 512],
                        hgs[:, ic * 2048 + tb * 512:ic * 2048 + (tb + 1) * 512],
                        up_ps[tb], ALU.mult)
                ins.then_inc(ctx.sV.h, 1)

        b.vector(v_up)

    with ExitStack() as es:
        ec = es.enter_context
        y_ps = [ec(nc.psum_tensor(f"y{i}_{e}_{uid()}", [128, 1024], F32))[:, :]
                for i in range(2)]

        def t_dn(te):
            for tt in range(NT):
                if tt == 0:
                    te.wait_ge(ctx.sV.h, s["act_v"][3])
                if tt >= 2:
                    te.wait_ge(ctx.sV.h, s["dn_v"][tt - 2])
                for ic in range(4):
                    for nb in range(2):
                        ins = te.matmul(
                            y_ps[tt % 2][:, nb * 512:(nb + 1) * 512],
                            act[:, ic * 2048 + tt * 128:ic * 2048 + (tt + 1) * 128],
                            s["wd"][:, ic, nb * 512:(nb + 1) * 512],
                            start=(ic == 0), stop=(ic == 3))
                ins.then_inc(ctx.sT.h, 1)

        b.tensor(t_dn)

        def v_dn(v):
            for tt in range(NT):
                v.wait_ge(ctx.sT.h, s["dn_mm"][tt])
                v.scalar_tensor_tensor(
                    x[:, tt * D:(tt + 1) * D], y_ps[tt % 2],
                    s["coef"][:, tt * 16 + e:tt * 16 + e + 1],
                    x[:, tt * D:(tt + 1) * D],
                    ALU.mult, ALU.add).then_inc(ctx.sV.h, 1)

        b.vector(v_dn)


def _l1ffn_shared(ctx, r, io, s):
    nc, b = ctx.nc, ctx.block
    x = r["x"]
    lnT, hgs, act = s["lnT"], s["hgs"], s["act"]
    from contextlib import ExitStack
    actS = hgs.tensor if False else None
    with ExitStack() as es:
        ec = es.enter_context
        wsg_sb = ec(nc.sbuf_tensor(f"wsg_{uid()}", [128, 2, DC, 128], BF16))
        wsu_sb = ec(nc.sbuf_tensor(f"wsu_{uid()}", [128, 2, DC, 128], BF16))
        wsd_sb = ec(nc.sbuf_tensor(f"wsd_{uid()}", [128, DC, 1024], BF16))
        hgS = ec(nc.sbuf_tensor(f"hgS_{uid()}", [128, 2048], BF16))

        ws_sem = [ctx.sW2, ctx.sW3]
        wdma = []
        for sc8 in range(SH // 128):
            wdma.append(plan(ws_sem[sc8 % 2], 2, step=16))
        wsd_dma = plan(ctx.sP, 1, step=16)

        up_mm = [plan(ctx.sT, 2) for _ in range(8)]
        silu_a = [plan(ctx.sA, 4) for _ in range(8)]
        mul_v = [plan(ctx.sV, 4) for _ in range(8)]
        dn_mm = plan(ctx.sT, NT)
        dn_v = plan(ctx.sV, NT)

        def g_ws(g):
            g.wait_ge(ctx.sV.h, s["prev_dn_v"])
            g.dma_start(out=wsd_sb[:, :, :],
                        in_=io["l1f_wsdT"].rearrange("(sc p) n -> p sc n", p=128)
                        ).then_inc(ctx.sP.h, 16)
            for sc8 in range(8):
                if sc8 >= 2:
                    g.wait_ge(ctx.sV.h, mul_v[sc8 - 2][-1])
                g.dma_start(out=wsg_sb[:, sc8 % 2, :, :],
                            in_=io["l1f_wsgT"].rearrange("(dc p) n -> p dc n", p=128)
                            [:, :, sc8 * 128:(sc8 + 1) * 128]
                            ).then_inc(ws_sem[sc8 % 2].h, 16)
                g.dma_start(out=wsu_sb[:, sc8 % 2, :, :],
                            in_=io["l1f_wsuT"].rearrange("(dc p) n -> p dc n", p=128)
                            [:, :, sc8 * 128:(sc8 + 1) * 128]
                            ).then_inc(ws_sem[sc8 % 2].h, 16)

        b.gpsimd(g_ws)

        with ExitStack() as es2:
            up_ps = [es2.enter_context(
                nc.psum_tensor(f"ups{i}_{uid()}", [128, 512], F32))[:, :]
                for i in range(8)]

            def t_sup(te):
                if s["prev_dn_v"] is not None:
                    te.wait_ge(ctx.sV.h, s["prev_dn_v"])
                for sc8 in range(8):
                    te.wait_ge(ws_sem[sc8 % 2].h, wdma[sc8][1])
                    for kind in range(2):
                        w = wsg_sb if kind == 0 else wsu_sb
                        ps_base = 0 if kind == 0 else 4
                        if kind == 0 and sc8 >= 1:
                            te.wait_ge(ctx.sA.h, silu_a[sc8 - 1][-1])
                        if kind == 1 and sc8 >= 1:
                            te.wait_ge(ctx.sV.h, mul_v[sc8 - 1][-1])
                        for dc in range(DC):
                            for tb in range(4):
                                ins = te.matmul(
                                    up_ps[ps_base + tb],
                                    w[:, sc8 % 2, dc, :],
                                    lnT[:, dc * T + tb * 512:dc * T + (tb + 1) * 512],
                                    start=(dc == 0), stop=(dc == DC - 1))
                            if dc == DC - 1:
                                ins.then_inc(ctx.sT.h, 1)

            b.tensor(t_sup)

            def s_sup(sc):
                for sc8 in range(8):
                    sc.wait_ge(ctx.sT.h, up_mm[sc8][0])
                    if sc8 >= 1:
                        sc.wait_ge(ctx.sV.h, mul_v[sc8 - 1][-1])
                    for tb in range(4):
                        sc.activation(hgS[:, tb * 512:(tb + 1) * 512],
                                      up_ps[tb], AF.Silu).then_inc(ctx.sA.h, 1)

            b.scalar(s_sup)

            def v_sup(v):
                for sc8 in range(8):
                    v.wait_ge(ctx.sT.h, up_mm[sc8][1])
                    for tb in range(4):
                        v.wait_ge(ctx.sA.h, silu_a[sc8][tb])
                        dst = hgs if sc8 < 4 else act
                        v.tensor_tensor(
                            dst[:, (sc8 % 4) * 2048 + tb * 512:
                                (sc8 % 4) * 2048 + (tb + 1) * 512],
                            hgS[:, tb * 512:(tb + 1) * 512],
                            up_ps[4 + tb], ALU.mult).then_inc(ctx.sV.h, 1)

            b.vector(v_sup)

        with ExitStack() as es3:
            y_ps = [es3.enter_context(
                nc.psum_tensor(f"ys{i}_{uid()}", [128, 1024], F32))[:, :]
                for i in range(2)]

            def t_sdn(te):
                te.wait_ge(ctx.sP.h, wsd_dma[0])
                for tt in range(NT):
                    if tt == 0:
                        te.wait_ge(ctx.sV.h, mul_v[7][-1])
                    if tt >= 2:
                        te.wait_ge(ctx.sV.h, dn_v[tt - 2])
                    for sc8 in range(8):
                        src = hgs if sc8 < 4 else act
                        for nb in range(2):
                            ins = te.matmul(
                                y_ps[tt % 2][:, nb * 512:(nb + 1) * 512],
                                src[:, (sc8 % 4) * 2048 + tt * 128:
                                    (sc8 % 4) * 2048 + (tt + 1) * 128],
                                wsd_sb[:, sc8, nb * 512:(nb + 1) * 512],
                                start=(sc8 == 0), stop=(sc8 == 7))
                    ins.then_inc(ctx.sT.h, 1)

            b.tensor(t_sdn)

            def v_sdn(v):
                for tt in range(NT):
                    v.wait_ge(ctx.sT.h, dn_mm[tt])
                    v.scalar_tensor_tensor(
                        x[:, tt * D:(tt + 1) * D], y_ps[tt % 2], 0.5,
                        x[:, tt * D:(tt + 1) * D],
                        ALU.mult, ALU.add).then_inc(ctx.sV.h, 1)

            b.vector(v_sdn)



def emit_l1attn(ctx, r, io, cc):
    nc, b = ctx.nc, ctx.block
    from contextlib import ExitStack
    with ExitStack() as es:
        ec = es.enter_context
        arena = ec(nc.sbuf_tensor(f"l1a_arena_{uid()}",
                                  [128, 32768 + 16384 + 8448], BF16))
        lnT = arena[:, 0:DC * T]
        ln4T = arena[:, 0:32768].rearrange("p (dc t) -> p dc t", dc=DC)
        qkvgT = arena[:, 32768:32768 + 16384]
        flat_sb = arena[:, 32768:32768 + 8192]
        flatT_sb = arena[:, 32768 + 8192:32768 + 16384]
        qtok = arena[:, 49152:49152 + 2048]
        ktok = arena[:, 51200:51200 + 2048]
        fin = arena[:, 53248:53248 + 4096]
        ftok = arena[:, 57344:57344 + 256]
        wpT_sb = arena[:, 49152:49152 + 8192].rearrange("p (tp n) -> p tp n", tp=DC)
        wbuf = ec(nc.sbuf_tensor(f"l1aw_{uid()}", [128, DC, 4, 128], BF16))
        ksum = ec(nc.sbuf_tensor(f"ksum_{uid()}", [128, 32], BF16))
        b_sb = ec(nc.sbuf_tensor(f"l1ab_{uid()}", [128, 16], F32))
        bp_row = ec(nc.sbuf_tensor(f"l1abp_{uid()}", [1, D], BF16))
        M_sb = ec(nc.sbuf_tensor(f"msb_{uid()}", [128, 64], BF16))
        z_sb = ec(nc.sbuf_tensor(f"zsb_{uid()}", [128, 1], F32))
        zt_sb = ec(nc.sbuf_tensor(f"ztsb_{uid()}", [128, 1], F32))
        t1f = r["dump"][:, 512:1024]

        emit_ln(ctx, r, lnT)

        snd_x = plan(ctx.sX, 2, step=16)

        def q_send(g):
            g.dma_start(out=cc["in"].ap().rearrange("(dc p) t -> p dc t", p=128),
                        in_=lnT.rearrange("p (dc t) -> p dc t", dc=DC)
                        ).then_inc(ctx.sX.h, 16)
            g.dma_start(out=b_sb[:, :], in_=io["l1a_b"]).then_inc(ctx.sX.h, 16)

        b.sync(q_send)

        cc_done = plan(ctx.sG, 1)
        wp_dma = plan(ctx.sP, 1, step=16)
        kv_x = plan(ctx.sW, 2, step=16)

        def g_cc(g):
            g.wait_ge(ctx.sX.h, snd_x[1])
            g.collective_compute(
                "AllGather", ALU.bypass, replica_groups=cc["rg"],
                ins=[cc["in"].ap().opt()], outs=[cc["out"].ap().opt()],
            ).then_inc(ctx.sG.h, 1)
            g.dma_start(out=bp_row[:, :], in_=io["l1a_bp_row"]).then_inc(ctx.sP.h, 16)
            g.wait_ge(ctx.sG.h, cc_done[0])
            for rk in range(2):
                g.dma_start(
                    out=ln4T[:, :, rk * 2048:(rk + 1) * 2048],
                    in_=cc["out"].ap()[rk * 1024:(rk + 1) * 1024, :]
                    .rearrange("(dc p) t -> p dc t", p=128),
                ).then_inc(ctx.sW.h, 16)

        b.gpsimd(g_cc)

        st = dict(ln4T=ln4T, qkvgT=qkvgT, wbuf=wbuf, ksum=ksum, b_sb=b_sb,
                  M_sb=M_sb, z_sb=z_sb, zt_sb=zt_sb, t1f=t1f, qtok=qtok,
                  ktok=ktok, fin=fin, ftok=ftok, bounce=cc["bounce"],
                  kv_x=kv_x)
        prev = {"v": None, "t": None, "x": None}
        for pq in range(4):
            _l1attn_pair(ctx, r, io, pq, st, prev)

        _l1attn_proj(ctx, r, io, dict(flat=flat_sb, flatT=flatT_sb, wpT=wpT_sb,
                                      bp_row=bp_row, bounce=cc["bounce"],
                                      wp_dma=wp_dma, prev=prev))
        del wp_dma

    emit_barrier(ctx)


def _l1attn_pair(ctx, r, io, pq, s, prev):
    nc, b = ctx.nc, ctx.block
    from contextlib import ExitStack
    qkvgT = s["qkvgT"]

    wdma = plan(ctx.sW, 4, step=16)

    def g_w(g):
        if prev["t"] is not None:
            g.wait_ge(ctx.sT.h, prev["t"])
        for k in range(4):
            g.dma_start(out=s["wbuf"][:, :, k, :],
                        in_=io["l1a_wT"].rearrange("(dc p) (k c) -> p dc k c",
                                                   p=128, k=4)
                        [:, :, k, pq * 128:(pq + 1) * 128]
                        ).then_inc(ctx.sW.h, 16)

    b.gpsimd(g_w)

    pr_mm = plan(ctx.sT, 4 * 8)
    pr_ev = plan(ctx.sA, 4 * 8)
    sp_a = plan(ctx.sA, 8)
    sp_v = plan(ctx.sV, 8)

    with ExitStack() as es:
        ec = es.enter_context
        u_ps = [ec(nc.psum_tensor(f"u{i}_{pq}_{uid()}", [128, 512], F32))[:, :]
                for i in range(8)]

        def t_pr(te):
            te.wait_ge(ctx.sW.h, wdma[3])
            if prev["v"] is not None:
                te.wait_ge(ctx.sV.h, prev["v"])
            for kind in range(4):
                for dc in range(DC):
                    for tb in range(8):
                        i = kind * 8 + tb
                        if dc == 0 and kind >= 1:
                            te.wait_ge(ctx.sA.h, pr_ev[i - 8])
                        ins = te.matmul(
                            u_ps[tb], s["wbuf"][:, dc, kind, :],
                            s["ln4T"][:, dc, tb * 512:(tb + 1) * 512],
                            start=(dc == 0), stop=(dc == DC - 1))
                        if dc == DC - 1:
                            ins.then_inc(ctx.sT.h, 1)

        b.tensor(t_pr)

        def s_pr(sc):
            for kind in range(4):
                fn = AF.Identity if kind < 2 else AF.Gelu
                for tb in range(8):
                    i = kind * 8 + tb
                    sc.wait_ge(ctx.sT.h, pr_mm[i])
                    sc.activation(qkvgT[:, kind * 4096 + tb * 512:
                                        kind * 4096 + (tb + 1) * 512],
                                  u_ps[tb], fn,
                                  bias=s["b_sb"][:, pq * 4 + kind:pq * 4 + kind + 1]
                                  ).then_inc(ctx.sA.h, 1)
            for i8 in range(8):
                kq, qtr = i8 // 4, i8 % 4
                sl = qkvgT[:, kq * 4096 + qtr * 1024:kq * 4096 + (qtr + 1) * 1024]
                sc.wait_ge(ctx.sA.h, pr_ev[kq * 8 + qtr * 2 + 1])
                if i8 >= 1:
                    sc.wait_ge(ctx.sV.h, sp_v[i8 - 1])
                sc.activation(r["dump"][:, :], sl, AF.Abs)
                sc.drain()
                sc.activation(r["dump"][:, :], r["dump"][:, :], AF.Exp, scale=-1.0)
                sc.drain()
                sc.activation(r["dump"][:, :], r["dump"][:, :], AF.Ln, bias=1.0)
                sc.drain()
                sc.activation(r["ln_tok"][:, :], sl, AF.Relu
                              ).then_inc(ctx.sA.h, 1)

        b.scalar(s_pr)

        def v_pr(v):
            for i8 in range(8):
                kq, qtr = i8 // 4, i8 % 4
                sl = qkvgT[:, kq * 4096 + qtr * 1024:kq * 4096 + (qtr + 1) * 1024]
                v.wait_ge(ctx.sA.h, sp_a[i8])
                v.tensor_tensor(sl, r["ln_tok"][:, :], r["dump"][:, :],
                                ALU.add).then_inc(ctx.sV.h, 1)

        b.vector(v_pr)

    for hl in range(2):
        _l1attn_head(ctx, r, io, pq, hl, s, sp_v[-1], prev)


def _l1attn_head(ctx, r, io, pq, hl, s, sp_last, prev):
    nc, b = ctx.nc, ctx.block
    from contextlib import ExitStack
    qkvgT = s["qkvgT"]
    h = pq * 2 + hl
    base = hl * 64
    tpos = (0, 64) if hl == 1 else None
    tpos2 = (64, 64) if hl == 1 else None

    tr_mm = plan(ctx.sT, 64)
    mz_mm = plan(ctx.sT, 1)
    o1_mm = plan(ctx.sT, 8)
    ft_mm = plan(ctx.sT, 32)
    tr_ev = []
    for i in range(64):
        if i < 32:
            tr_ev.append(plan(ctx.sV, 1)[-1])
        else:
            tr_ev.append(plan(ctx.sV, 2)[-1])
    mz_v = plan(ctx.sV, 2)
    fin_v = plan(ctx.sV, 8 * 2)
    ft_v = plan(ctx.sV, 32)
    ft_x = plan(ctx.sX, 32, step=16)

    with ExitStack() as es:
        ec = es.enter_context
        t_ps = [ec(nc.psum_tensor(f"lt{i}_{pq}_{hl}_{uid()}", [128, 128], BF16))[:, :]
                for i in range(2)]
        M_ps = ec(nc.psum_tensor(f"lm_{pq}_{hl}_{uid()}", [128, 64], F32))[:, :]
        zd_ps = ec(nc.psum_tensor(f"lz_{pq}_{hl}_{uid()}", [128, 1], F32))[:, :]
        o1_ps = [ec(nc.psum_tensor(f"lo{i}_{pq}_{hl}_{uid()}", [128, 512], F32))[:, :]
                 for i in range(2)]

        def t_h(te):
            if hl == 0:
                te.wait_ge(ctx.sV.h, sp_last)
            else:
                te.wait_ge(ctx.sV.h, prev["v"])
            for kq in range(2):
                for tc in range(32):
                    i = kq * 32 + tc
                    if i >= 2:
                        te.wait_ge(ctx.sV.h, tr_ev[i - 2])
                    te.transpose(t_ps[i % 2],
                                 qkvgT[:, kq * 4096 + tc * 128:
                                       kq * 4096 + (tc + 1) * 128],
                                 r["ident"][:, :]).then_inc(ctx.sT.h, 1)
            te.wait_ge(ctx.sV.h, tr_ev[63])
            for tc in range(32):
                te.matmul(M_ps[base:base + 64, :],
                          s["ktok"][:, tc * 64:(tc + 1) * 64],
                          s["qtok"][:, tc * 64:(tc + 1) * 64],
                          start=(tc == 0), stop=(tc == 31), tile_position=tpos)
                ins = te.matmul(zd_ps[base:base + 64, :],
                                s["qtok"][:, tc * 64:(tc + 1) * 64],
                                s["ksum"][:, tc:tc + 1],
                                start=(tc == 0), stop=(tc == 31),
                                tile_position=tpos)
            ins.then_inc(ctx.sT.h, 1)
            te.wait_ge(ctx.sV.h, mz_v[0])
            for tb in range(8):
                if tb >= 2:
                    te.wait_ge(ctx.sV.h, fin_v[(tb - 2) * 2 + 1])
                te.matmul(o1_ps[tb % 2][base:base + 64, :],
                          s["M_sb"][base:base + 64, :],
                          qkvgT[base:base + 64,
                                2 * 4096 + tb * 512:2 * 4096 + (tb + 1) * 512],
                          start=True, stop=True,
                          tile_position=tpos2).then_inc(ctx.sT.h, 1)
            for tc in range(32):
                te.wait_ge(ctx.sV.h, fin_v[(tc // 4) * 2 + 1])
                if tc >= 2:
                    te.wait_ge(ctx.sV.h, ft_v[tc - 2])
                te.transpose(t_ps[tc % 2][:, 0:64],
                             s["fin"][base:base + 64, tc * 128:(tc + 1) * 128],
                             r["ident"][base:base + 64, base:base + 64]
                             ).then_inc(ctx.sT.h, 1)

        b.tensor(t_h)

        def v_h(v):
            for kq in range(2):
                for tc in range(32):
                    i = kq * 32 + tc
                    v.wait_ge(ctx.sT.h, tr_mm[i])
                    dst = s["qtok"] if kq == 0 else s["ktok"]
                    v.tensor_copy(dst[:, tc * 64:(tc + 1) * 64],
                                  t_ps[i % 2][:, base:base + 64]
                                  ).then_inc(ctx.sV.h, 1)
                    if kq == 1:
                        with nc.allow_low_precision(reason="bf16 ksum operand"):
                            v.reduce_sum(s["ksum"][:, tc:tc + 1],
                                         t_ps[i % 2][:, base:base + 64],
                                         axis=AX.X).then_inc(ctx.sV.h, 1)
            v.wait_ge(ctx.sT.h, mz_mm[0])
            v.tensor_copy(s["M_sb"][base:base + 64, :],
                          M_ps[base:base + 64, :]).then_inc(ctx.sV.h, 1)
            v.tensor_scalar(s["zt_sb"][base:base + 64, :],
                            zd_ps[base:base + 64, :], L1_SCALE, float(D),
                            ALU.mult, ALU.add)
            v.drain()
            with nc.allow_low_precision(reason="z gate reciprocal"):
                v.reciprocal(s["z_sb"][base:base + 64, :],
                             s["zt_sb"][base:base + 64, :]).then_inc(ctx.sV.h, 1)
            for tb in range(8):
                v.wait_ge(ctx.sT.h, o1_mm[tb])
                sl = slice(base, base + 64)
                v.scalar_tensor_tensor(
                    s["t1f"][sl, :], o1_ps[tb % 2][sl, :], L1_SCALE,
                    qkvgT[sl, 2 * 4096 + tb * 512:2 * 4096 + (tb + 1) * 512],
                    ALU.mult, ALU.add).then_inc(ctx.sV.h, 1)
                v.drain()
                v.scalar_tensor_tensor(
                    s["fin"][sl, tb * 512:(tb + 1) * 512], s["t1f"][sl, :],
                    s["z_sb"][sl, 0:1],
                    qkvgT[sl, 3 * 4096 + tb * 512:3 * 4096 + (tb + 1) * 512],
                    ALU.mult, ALU.mult).then_inc(ctx.sV.h, 1)
            for tc in range(32):
                v.wait_ge(ctx.sT.h, ft_mm[tc])
                if tc >= 2:
                    v.wait_ge(ctx.sX.h, ft_x[tc - 2])
                v.tensor_copy(s["ftok"][:, (tc % 2) * 64:(tc % 2) * 64 + 64],
                              t_ps[tc % 2][:, 0:64]
                              ).then_inc(ctx.sV.h, 1)

        b.vector(v_h)

        def q_h(g):
            for tc in range(32):
                g.wait_ge(ctx.sV.h, ft_v[tc])
                g.dma_start(
                    out=s["bounce"].ap()
                    .rearrange("(hh t j) -> hh t j", hh=8, t=TK)
                    [h, tc * 128:(tc + 1) * 128, :],
                    in_=s["ftok"][:, (tc % 2) * 64:(tc % 2) * 64 + 64],
                ).then_inc(ctx.sX.h, 16)

        b.sync(q_h)

    prev["v"] = ft_v[-1]
    prev["t"] = ft_mm[-1]
    prev["x"] = ft_x[-1]


def _l1attn_proj(ctx, r, io, s):
    nc, b = ctx.nc, ctx.block
    from contextlib import ExitStack
    x = r["x"]
    flat, flatT = s["flat"], s["flatT"]
    prev = s["prev"]

    wpl = plan(ctx.sP, 1, step=16)
    for half in range(2):
        fl_x = plan(ctx.sW, 1, step=16)
        fT_mm = plan(ctx.sT, 64)
        fT_v = plan(ctx.sV, 64)
        pj_mm = plan(ctx.sT, 8)
        pj_v = plan(ctx.sV, 8)

        def g_fl(g, half=half, fl_x=fl_x):
            if half == 0:
                g.wait_ge(ctx.sX.h, prev["x"])
                g.wait_ge(ctx.sV.h, prev["v"])
                g.dma_start(out=s["wpT"],
                            in_=io["l1a_wpT"].rearrange("(tp p) n -> p tp n", p=128)
                            ).then_inc(ctx.sP.h, 16)
            else:
                g.wait_ge(ctx.sV.h, prev["pj"])
            g.dma_start(
                out=flat[:, :].rearrange("p (t d) -> p t d", t=8),
                in_=s["bounce"].ap()
                .rearrange("(n p d) -> p n d", p=128, d=D)
                [:, half * 8:(half + 1) * 8, :],
            ).then_inc(ctx.sW.h, 16)

        b.gpsimd(g_fl)

        with ExitStack() as es:
            ec = es.enter_context
            t_ps = [ec(nc.psum_tensor(f"pt{i}_{half}_{uid()}", [128, 128], BF16))[:, :]
                    for i in range(2)]
            p_ps = [ec(nc.psum_tensor(f"pp{i}_{half}_{uid()}", [128, 1024], F32))[:, :]
                    for i in range(2)]

            def t_f(te, half=half, fl_x=fl_x, fT_mm=fT_mm, fT_v=fT_v,
                    pj_mm=pj_mm, pj_v=pj_v):
                te.wait_ge(ctx.sW.h, fl_x[0])
                for i in range(64):
                    if i >= 2:
                        te.wait_ge(ctx.sV.h, fT_v[i - 2])
                    tt, dc = i // 8, i % 8
                    te.transpose(t_ps[i % 2],
                                 flat[:, tt * 1024 + dc * 128:
                                      tt * 1024 + (dc + 1) * 128],
                                 r["ident"][:, :]).then_inc(ctx.sT.h, 1)
                te.wait_ge(ctx.sP.h, wpl[0])
                for tt in range(8):
                    te.wait_ge(ctx.sV.h, fT_v[tt * 8 + 7])
                    if tt >= 2:
                        te.wait_ge(ctx.sV.h, pj_v[tt - 2])
                    for dc in range(DC):
                        for nb in range(2):
                            te.matmul(p_ps[tt % 2][:, nb * 512:(nb + 1) * 512],
                                      flatT[:, dc * 1024 + tt * 128:
                                            dc * 1024 + (tt + 1) * 128],
                                      s["wpT"][:, dc, nb * 512:(nb + 1) * 512],
                                      start=(dc == 0), stop=False)
                    for nb in range(2):
                        ins = te.matmul(p_ps[tt % 2][:, nb * 512:(nb + 1) * 512],
                                        r["ones"][0:1, 0:128],
                                        s["bp_row"][0:1, nb * 512:(nb + 1) * 512],
                                        start=False, stop=True)
                    ins.then_inc(ctx.sT.h, 1)

            b.tensor(t_f)

            def v_f(v, half=half, fT_mm=fT_mm, fT_v=fT_v, pj_mm=pj_mm, pj_v=pj_v):
                for i in range(64):
                    v.wait_ge(ctx.sT.h, fT_mm[i])
                    tt, dc = i // 8, i % 8
                    v.tensor_copy(flatT[:, dc * 1024 + tt * 128:
                                        dc * 1024 + (tt + 1) * 128],
                                  t_ps[i % 2]).then_inc(ctx.sV.h, 1)
                for tt in range(8):
                    g = half * 8 + tt
                    v.wait_ge(ctx.sT.h, pj_mm[tt])
                    v.tensor_add(x[:, g * D:(g + 1) * D], x[:, g * D:(g + 1) * D],
                                 p_ps[tt % 2]).then_inc(ctx.sV.h, 1)

            b.vector(v_f)

        prev["pj"] = pj_v[-1]



PHASESETS = {
    "l0ffn": ["l0ffn"],
    "l0attn": ["l0attn"],
    "l1ffn": ["l1ffn"],
    "l1attn": ["l1attn"],
}


def build_program(phases, replica_groups=RG):
    nc = bass.Bass("TRN2", target_bir_lowering=False, detect_race_conditions=False)
    io = {}
    io["x_in"] = nc.dram_tensor("x_in", [T, D], F32, kind="ExternalInput").ap()
    io["y"] = nc.dram_tensor("y", [T, D], F32, kind="ExternalOutput").ap()
    need = set(phases)
    if "l0ffn" in need:
        io["l0_rw"] = nc.dram_tensor("l0_rw", [D, 9], F32, kind="ExternalInput").ap()
        io["l0_rb_row"] = nc.dram_tensor("l0_rb_row", [1, 9], F32, kind="ExternalInput").ap()
        io["l0_w1"] = nc.dram_tensor("l0_w1", [E0, D, F0], F32, kind="ExternalInput").ap()
        io["l0_w2"] = nc.dram_tensor("l0_w2", [E0, F0, D], F32, kind="ExternalInput").ap()
    if "l0attn" in need:
        for nme in ["l0_wqT", "l0_wkT", "l0_wvT", "l0_woT"]:
            io[nme] = nc.dram_tensor(nme, [D, D], F32, kind="ExternalInput").ap()
        for nme in ["l0_bq", "l0_bk", "l0_bv"]:
            io[nme] = nc.dram_tensor(nme, [128, 8], F32, kind="ExternalInput").ap()
        io["l0_bo_row"] = nc.dram_tensor("l0_bo_row", [1, D], F32, kind="ExternalInput").ap()
    if "l1ffn" in need:
        io["l1f_gwT"] = nc.dram_tensor("l1f_gwT", [D, 16], F32, kind="ExternalInput").ap()
        io["l1f_gb_row"] = nc.dram_tensor("l1f_gb_row", [1, 16], F32, kind="ExternalInput").ap()
        io["l1f_wgT"] = nc.dram_tensor("l1f_wgT", [E1, D, I1], F32, kind="ExternalInput").ap()
        io["l1f_wuT"] = nc.dram_tensor("l1f_wuT", [E1, D, I1], F32, kind="ExternalInput").ap()
        io["l1f_wdT"] = nc.dram_tensor("l1f_wdT", [E1, I1, D], F32, kind="ExternalInput").ap()
        io["l1f_wsgT"] = nc.dram_tensor("l1f_wsgT", [D, SH], F32, kind="ExternalInput").ap()
        io["l1f_wsuT"] = nc.dram_tensor("l1f_wsuT", [D, SH], F32, kind="ExternalInput").ap()
        io["l1f_wsdT"] = nc.dram_tensor("l1f_wsdT", [SH, D], F32, kind="ExternalInput").ap()
    if "l1attn" in need:
        io["l1a_wT"] = nc.dram_tensor("l1a_wT", [D, 2048], F32, kind="ExternalInput").ap()
        io["l1a_b"] = nc.dram_tensor("l1a_b", [128, 16], F32, kind="ExternalInput").ap()
        io["l1a_wpT"] = nc.dram_tensor("l1a_wpT", [D, D], F32, kind="ExternalInput").ap()
        io["l1a_bp_row"] = nc.dram_tensor("l1a_bp_row", [1, D], F32, kind="ExternalInput").ap()
    n_cc = sum(1 for p in phases if p in ("l0attn", "l1attn"))
    cc_bufs = []
    for i in range(n_cc):
        cc_bufs.append({
            "in": nc.dram_tensor(f"cc_in{i}", [D, T], BF16, kind="Internal"),
            "out": nc.dram_tensor(f"cc_out{i}", [2 * D, T], BF16, kind="Internal"),
            "bounce": nc.dram_tensor(f"bounce{i}", [8 * TK * HD], BF16, kind="Internal"),
            "rg": replica_groups,
        })
    cc_idx = [0]

    ident_np = np.eye(128, dtype=ml_dtypes.bfloat16)
    ones_np = np.ones((1, 128), dtype=ml_dtypes.bfloat16)

    from contextlib import ExitStack
    with ExitStack() as es:
        ec = es.enter_context
        x_sb = ec(nc.sbuf_tensor("x_sb", [128, NT * D], F32))
        mean = ec(nc.sbuf_tensor("mean", [128, NT], F32))
        rstd = ec(nc.sbuf_tensor("rstd", [128, NT], F32))
        var = ec(nc.sbuf_tensor("var", [128, NT], F32))
        s1 = ec(nc.sbuf_tensor("s1", [128, NT], F32))
        s2 = ec(nc.sbuf_tensor("s2", [128, NT], F32))
        dump = ec(nc.sbuf_tensor("dump", [128, 1024], F32))
        ln_tok = ec(nc.sbuf_tensor("ln_tok", [128, 1024], BF16))
        tc = ec(nc.sbuf_tensor("tc", [128, 1], F32))
        tc2 = ec(nc.sbuf_tensor("tc2", [128, 1], F32))
        m9 = ec(nc.sbuf_tensor("m9", [128, 16], F32))
        coef = ec(nc.sbuf_tensor("coef", [128, NT * 16], F32))
        ident = ec(nc.sbuf_tensor("ident", [128, 128], BF16))
        ones = ec(nc.sbuf_tensor("ones", [1, 128], BF16))
        hT = ec(nc.semaphore("sT"))
        hV = ec(nc.semaphore("sV"))
        hA = ec(nc.semaphore("sA"))
        hG = ec(nc.semaphore("sG"))
        hW = ec(nc.semaphore("sW"))
        hX = ec(nc.semaphore("sX"))
        hW0 = ec(nc.semaphore("sW0"))
        hW1 = ec(nc.semaphore("sW1"))
        hW2 = ec(nc.semaphore("sW2"))
        hW3 = ec(nc.semaphore("sW3"))
        hP = ec(nc.semaphore("sP"))
        block = ec(nc.Block())
        sems = dict(sT=Sem(hT), sV=Sem(hV), sA=Sem(hA), sG=Sem(hG),
                    sW=Sem(hW), sX=Sem(hX), sW0=Sem(hW0), sW1=Sem(hW1),
                    sW2=Sem(hW2), sW3=Sem(hW3), sP=Sem(hP))
        ctx = Ctx(nc, block, sems)
        ident_dram = nc.inline_tensor(ident_np, "ident_const").ap()
        ones_dram = nc.inline_tensor(ones_np, "ones_const").ap()

        r = dict(x=x_sb[:, :], mean=mean[:, :], rstd=rstd[:, :], var=var[:, :],
                 s1=s1[:, :], s2=s2[:, :], dump=dump[:, :],
                 ln_tok=ln_tok[:, :], tc=tc[:, :], tc2=tc2[:, :], m9=m9[:, :],
                 coef=coef[:, :], ident=ident[:, :], ones=ones[:, :])

        ctx.sX.inc(48)

        def g_load(g):
            g.dma_start(out=x_sb[:, :].rearrange("p (t d) -> p t d", t=NT),
                        in_=io["x_in"].rearrange("(t p) d -> p t d", p=128)
                        ).then_inc(ctx.sX.h, 16)
            g.dma_start(out=ident[:, :], in_=ident_dram).then_inc(ctx.sX.h, 16)
            g.dma_start(out=ones[:, :], in_=ones_dram).then_inc(ctx.sX.h, 16)

        block.sync(g_load)
        emit_barrier(ctx)

        for ph in phases:
            if ph == "l0ffn":
                emit_l0ffn(ctx, r, io)
            elif ph == "l0attn":
                emit_l0attn(ctx, r, io, cc_bufs[cc_idx[0]])
                cc_idx[0] += 1
            elif ph == "l1ffn":
                emit_l1ffn(ctx, r, io)
            elif ph == "l1attn":
                emit_l1attn(ctx, r, io, cc_bufs[cc_idx[0]])
                cc_idx[0] += 1
            elif ph == "nop":
                pass
            elif ph == "lnonly":
                with nc.sbuf_tensor("lnT_dbg", [128, DC * T], BF16) as lnT_dbg:
                    emit_ln(ctx, r, lnT_dbg[:, :])
                    dbg = nc.dram_tensor("dbg", [128, DC * T], F32,
                                         kind="ExternalOutput").ap()
                    with nc.sbuf_tensor("dbg_f", [128, 2048], F32) as dbg_f:
                        dbg_v = plan(ctx.sV, 8)
                        dbg_x = plan(ctx.sX, 8, step=16)

                        def v_dbg(v):
                            for i in range(8):
                                if i >= 1:
                                    v.wait_ge(ctx.sX.h, dbg_x[i - 1])
                                v.tensor_copy(dbg_f[:, :],
                                              lnT_dbg[:, i * 2048:(i + 1) * 2048]
                                              ).then_inc(ctx.sV.h, 1)

                        def q_dbg(g):
                            for i in range(8):
                                g.wait_ge(ctx.sV.h, dbg_v[i])
                                g.dma_start(out=dbg[:, i * 2048:(i + 1) * 2048],
                                            in_=dbg_f[:, :]).then_inc(ctx.sX.h, 16)

                        ctx.block.vector(v_dbg)
                        ctx.block.sync(q_dbg)
                        emit_barrier(ctx)
                dbg2 = nc.dram_tensor("dbg2", [128, 5 * NT], F32,
                                      kind="ExternalOutput").ap()
                st_x = plan(ctx.sX, 1, step=16)

                def q_dbg2(g):
                    g.dma_start(out=dbg2[:, 0 * NT:1 * NT], in_=s1_t).then_inc(ctx.sX.h, 16)
                    g.dma_start(out=dbg2[:, 1 * NT:2 * NT], in_=s2_t).then_inc(ctx.sX.h, 16)
                    g.dma_start(out=dbg2[:, 2 * NT:3 * NT], in_=mean_t).then_inc(ctx.sX.h, 16)
                    g.dma_start(out=dbg2[:, 3 * NT:4 * NT], in_=var_t).then_inc(ctx.sX.h, 16)
                    g.dma_start(out=dbg2[:, 4 * NT:5 * NT], in_=rstd_t).then_inc(ctx.sX.h, 16)

                ctx.sX.inc(4 * 16)
                s1_t, s2_t = r["s1"], r["s2"]
                mean_t, var_t, rstd_t = r["mean"], r["var"], r["rstd"]
                ctx.block.sync(q_dbg2)
                emit_barrier(ctx)
            else:
                raise ValueError(ph)

        ctx.sX.inc(16)

        def g_store(g):
            g.dma_start(out=io["y"].rearrange("(t p) d -> p t d", p=128),
                        in_=x_sb[:, :].rearrange("p (t d) -> p t d", t=NT)
                        ).then_inc(ctx.sX.h, 16)
            g.wait_ge(ctx.sX.h, ctx.sX.v)

        block.sync(g_store)

    return nc



def prep_inputs(inputs):
    f32 = np.float32
    w = {}
    w["l0_rw"] = np.ascontiguousarray(inputs["L0_ffn_router_w"], dtype=f32)
    w["l0_rb_row"] = np.ascontiguousarray(inputs["L0_ffn_router_b"], dtype=f32).reshape(1, 9)
    w["l0_w1"] = np.ascontiguousarray(inputs["L0_ffn_w1"], dtype=f32)
    w["l0_w2"] = np.ascontiguousarray(inputs["L0_ffn_w2"], dtype=f32)
    for nme, key in [("l0_wqT", "L0_attn_Wq"), ("l0_wkT", "L0_attn_Wk"),
                     ("l0_wvT", "L0_attn_Wv"), ("l0_woT", "L0_attn_Wo")]:
        w[nme] = np.ascontiguousarray(np.asarray(inputs[key], dtype=f32).T)
    for nme, key in [("l0_bq", "L0_attn_bq"), ("l0_bk", "L0_attn_bk"),
                     ("l0_bv", "L0_attn_bv")]:
        w[nme] = np.ascontiguousarray(
            np.asarray(inputs[key], dtype=f32).reshape(8, 128).T)
    w["l0_bo_row"] = np.ascontiguousarray(inputs["L0_attn_bo"], dtype=f32).reshape(1, D)
    w["l1f_gwT"] = np.ascontiguousarray(np.asarray(inputs["L1_ffn_gate_w"], dtype=f32).T)
    w["l1f_gb_row"] = np.ascontiguousarray(inputs["L1_ffn_gate_bias"], dtype=f32).reshape(1, 16)
    w["l1f_wgT"] = np.ascontiguousarray(np.asarray(inputs["L1_ffn_Wg"], dtype=f32).transpose(0, 2, 1))
    w["l1f_wuT"] = np.ascontiguousarray(np.asarray(inputs["L1_ffn_Wu"], dtype=f32).transpose(0, 2, 1))
    w["l1f_wdT"] = np.ascontiguousarray(np.asarray(inputs["L1_ffn_Wd"], dtype=f32).transpose(0, 2, 1))
    w["l1f_wsgT"] = np.ascontiguousarray(np.asarray(inputs["L1_ffn_Ws_g"], dtype=f32).T)
    w["l1f_wsuT"] = np.ascontiguousarray(np.asarray(inputs["L1_ffn_Ws_u"], dtype=f32).T)
    w["l1f_wsdT"] = np.ascontiguousarray(np.asarray(inputs["L1_ffn_Ws_d"], dtype=f32).T)

    w["l1a_wpT"] = np.ascontiguousarray(np.asarray(inputs["L1_attn_w_proj"], dtype=f32).T)
    w["l1a_bp_row"] = np.ascontiguousarray(inputs["L1_attn_b_proj"], dtype=f32).reshape(1, D)
    wq4 = np.asarray(inputs["L1_attn_w_qkvg"], dtype=f32)
    bq4 = np.asarray(inputs["L1_attn_b_qkvg"], dtype=f32)

    x = np.asarray(inputs["L0_attn_x"], dtype=f32)
    per_core = []
    for c in range(NCORES):
        bidx, h = c // 2, c % 2
        m = dict(w)
        m["x_in"] = np.ascontiguousarray(x[bidx, h * T:(h + 1) * T, :])
        rows = np.empty((4, 4, 2, 64), dtype=np.int64)
        for k in range(4):
            for pq in range(4):
                for hl in range(2):
                    hh = h * 8 + pq * 2 + hl
                    rows[k, pq, hl] = k * 1024 + hh * 64 + np.arange(64)
        ridx = rows.reshape(-1)
        m["l1a_wT"] = np.ascontiguousarray(wq4[ridx].T)
        m["l1a_b"] = np.ascontiguousarray(
            bq4[ridx].reshape(4, 4, 2 * 64).transpose(2, 1, 0).reshape(128, 16))
        per_core.append(m)
    return per_core


def launch_keys(phases):
    keys = ["x_in"]
    if "l0ffn" in phases:
        keys += ["l0_rw", "l0_rb_row", "l0_w1", "l0_w2"]
    if "l0attn" in phases:
        keys += ["l0_wqT", "l0_wkT", "l0_wvT", "l0_woT",
                 "l0_bq", "l0_bk", "l0_bv", "l0_bo_row"]
    if "l1ffn" in phases:
        keys += ["l1f_gwT", "l1f_gb_row", "l1f_wgT", "l1f_wuT", "l1f_wdT",
                 "l1f_wsgT", "l1f_wsuT", "l1f_wsdT"]
    if "l1attn" in phases:
        keys += ["l1a_wT", "l1a_b", "l1a_wpT", "l1a_bp_row"]
    return keys


def run_launch(nc, per_core, keys):
    in_maps = [{k: m[k] for k in keys} for m in per_core]
    res = run_bass_kernel_spmd(nc, in_maps, core_ids=list(range(NCORES)))
    return [r["y"] for r in res.results]


FULL_PHASES = ["l0ffn", "l0attn", "l0ffn", "l1ffn", "l1attn", "l1ffn"]


def kernel(**inputs):
    per_core = prep_inputs(inputs)
    nc = build_program(FULL_PHASES)
    ys = run_launch(nc, per_core, launch_keys(FULL_PHASES))
    out = np.empty((4, TK, D), dtype=np.float32)
    for c in range(NCORES):
        bidx, h = c // 2, c % 2
        out[bidx, h * T:(h + 1) * T, :] = ys[c]
    return out


# version 04
# speedup vs baseline: 1.0943x; 1.0943x over previous
import numpy as np
import ml_dtypes
import concourse.bass as bass
import concourse.mybir as mybir
from concourse.bass_utils import run_bass_kernel_spmd

F32 = mybir.dt.float32
BF16 = mybir.dt.bfloat16
AF = mybir.ActivationFunctionType
ALU = mybir.AluOpType
AX = mybir.AxisListType

NCORES = 8
D = 1024
DC = 8
T = 2048
NT = 16
TK = 4096
NTK = 32
NH = 16
HD = 64
E0 = 8
F0 = 4096
E1 = 16
I1 = 512
SH = 1024
EPS = 1e-5
ISQRT_D = 1.0 / 32.0
L1_SCALE = HD ** -0.5
RG = [[0, 1], [2, 3], [4, 5], [6, 7]]

NEG = -1.0e30
DEBUG_ATTN = False

_UID = [0]


def uid():
    _UID[0] += 1
    return _UID[0]


class Sem:
    def __init__(self, h):
        self.h = h
        self.v = 0

    def inc(self, n=1):
        self.v += n
        return self.v


def plan(sem, n, step=1):
    base = sem.v
    sem.v += n * step
    return [base + (i + 1) * step for i in range(n)]


class Ctx:

    def __init__(self, nc, block, sems):
        self.nc = nc
        self.block = block
        for k, v in sems.items():
            setattr(self, k, v)

    def barrier_vals(self):
        return [(s, s.v) for s in (self.sT, self.sV, self.sA, self.sG, self.sW,
                                   self.sX, self.sW0, self.sW1, self.sW2,
                                   self.sW3, self.sP)]


def emit_barrier(ctx):
    vals = ctx.barrier_vals()
    b = ctx.block

    def body(eng):
        for s, v in vals:
            if v > 0:
                eng.wait_ge(s.h, v)

    b.tensor(body)
    b.vector(body)
    b.scalar(body)
    b.gpsimd(body)
    b.sync(body)



def emit_ln(ctx, r, lnT):
    nc, b = ctx.nc, ctx.block
    with (nc.psum_tensor(f"tr_ps0_{uid()}", [128, 128], BF16) as tr_ps0,
          nc.psum_tensor(f"tr_ps1_{uid()}", [128, 128], BF16) as tr_ps1):
        _emit_ln_inner(ctx, r, lnT, [tr_ps0[:, :], tr_ps1[:, :]])
    emit_barrier(ctx)


def _emit_ln_inner(ctx, r, lnT, tr_ps):
    nc, b = ctx.nc, ctx.block
    r = dict(r)
    r["tr_ps"] = tr_ps
    x = r["x"]
    mean, rstd = r["mean"], r["rstd"]
    sq_a = plan(ctx.sA, NT)
    rs_a = plan(ctx.sA, NT)
    ev_a = plan(ctx.sA, NT * DC)
    st_v = plan(ctx.sV, NT)
    ln_v = plan(ctx.sV, NT)
    tr_t = plan(ctx.sT, NT * DC)

    def scalar1(sc):
        for t in range(NT):
            sc.activation(r["dump"][:, :], x[:, t * D:(t + 1) * D], AF.Square,
                          accum_out=r["s2"][:, t:t + 1]).then_inc(ctx.sA.h, 1)
        for t in range(NT):
            sc.wait_ge(ctx.sV.h, st_v[t])
            sc.activation(r["tc2"][:, 0:1], r["var"][:, t:t + 1], AF.Ln)
            sc.drain()
            sc.activation(rstd[:, t:t + 1], r["tc2"][:, 0:1], AF.Exp,
                          scale=-0.5).then_inc(ctx.sA.h, 1)
        for t in range(NT):
            for dc in range(DC):
                idx = t * DC + dc
                sc.wait_ge(ctx.sT.h, tr_t[idx])
                sc.copy(lnT[:, dc * T + t * 128: dc * T + (t + 1) * 128],
                        r["tr_ps"][idx % 2][:, :]).then_inc(ctx.sA.h, 1)

    b.scalar(scalar1)

    def vector1(v):
        for t in range(NT):
            v.wait_ge(ctx.sA.h, sq_a[t])
            v.reduce_sum(r["s1"][:, t:t + 1], x[:, t * D:(t + 1) * D], axis=AX.X)
            v.drain()
            v.tensor_scalar_mul(mean[:, t:t + 1], r["s1"][:, t:t + 1], 1.0 / D)
            v.drain()
            v.tensor_mul(r["tc"][:, 0:1], mean[:, t:t + 1], mean[:, t:t + 1])
            v.tensor_scalar(r["var"][:, t:t + 1], r["s2"][:, t:t + 1], 1.0 / D,
                            EPS, ALU.mult, ALU.add)
            v.drain()
            v.tensor_sub(r["var"][:, t:t + 1], r["var"][:, t:t + 1],
                         r["tc"][:, 0:1]).then_inc(ctx.sV.h, 1)
        for t in range(NT):
            v.wait_ge(ctx.sA.h, rs_a[t])
            if t >= 1:
                v.wait_ge(ctx.sT.h, tr_t[8 * t - 1])
            v.tensor_scalar(r["ln_tok"][:, 0:D], x[:, t * D:(t + 1) * D],
                            mean[:, t:t + 1], rstd[:, t:t + 1],
                            ALU.subtract, ALU.mult).then_inc(ctx.sV.h, 1)

    b.vector(vector1)

    def tensor1(te):
        for t in range(NT):
            te.wait_ge(ctx.sV.h, ln_v[t])
            for dc in range(DC):
                idx = t * DC + dc
                if idx >= 2:
                    te.wait_ge(ctx.sA.h, ev_a[idx - 2])
                te.transpose(r["tr_ps"][idx % 2][:, :],
                             r["ln_tok"][:, dc * 128:(dc + 1) * 128],
                             r["ident"][:, :]).then_inc(ctx.sT.h, 1)

    b.tensor(tensor1)



def emit_l0ffn(ctx, r, io):
    nc, b = ctx.nc, ctx.block
    x = r["x"]

    from contextlib import ExitStack
    with ExitStack() as es:
        ec = es.enter_context
        arena = ec(nc.sbuf_tensor(f"ffn0_arena_{uid()}", [128, (DC * T) + 32 * 1024], BF16))
        w1buf = ec(nc.sbuf_tensor(f"w1buf_{uid()}", [128, 2, DC, 128], BF16))
        w2buf = ec(nc.sbuf_tensor(f"w2buf_{uid()}", [128, 2, 8, 1024], BF16))
        rw_sb = ec(nc.sbuf_tensor(f"rw_sb_{uid()}", [128, DC, 9], BF16))
        rb_sb = ec(nc.sbuf_tensor(f"rb_sb_{uid()}", [1, 9], BF16))
        probs = ec(nc.sbuf_tensor(f"probs_{uid()}", [128, 16], F32))
        lnT = arena[:, 0:DC * T]
        h_sb = arena[:, DC * T:DC * T + 32 * 1024]

        emit_ln(ctx, r, lnT)
        router_es = ExitStack()
        r_ps = router_es.enter_context(nc.psum_tensor(f"r_ps_{uid()}", [128, 9], F32))

        coef = r["coef"]

        ctx.sP.inc(32)

        def g_pre(g):
            g.dma_start(out=rw_sb[:, :, :],
                        in_=io["l0_rw"].rearrange("(dc p) n -> p dc n", p=128)
                        ).then_inc(ctx.sP.h, 16)
            g.dma_start(out=rb_sb[:, :],
                        in_=io["l0_rb_row"]).then_inc(ctx.sP.h, 16)

        b.gpsimd(g_pre)
        w_pre = ctx.sP.v

        rt_t = plan(ctx.sT, NT)

        def t_router(te):
            te.wait_ge(ctx.sP.h, w_pre)
            for t in range(NT):
                if t >= 1:
                    te.wait_ge(ctx.sA.h, rt_ev[t - 1])
                for dc in range(DC):
                    te.matmul(r_ps[:, :],
                              lnT[:, dc * T + t * 128: dc * T + (t + 1) * 128],
                              rw_sb[:, dc, :],
                              start=(dc == 0), stop=False)
                te.matmul(r_ps[:, :], r["ones"][0:1, 0:128], rb_sb[0:1, :],
                          start=False, stop=True).then_inc(ctx.sT.h, 1)

        rt_ev = plan(ctx.sA, NT)
        gate_sched = [plan(ctx.sV, 3) for _ in range(NT)]

        b.tensor(t_router)

        def s_gate(sc):
            for t in range(NT):
                sc.wait_ge(ctx.sV.h, gate_sched[t][0])
                if t >= 1:
                    sc.wait_ge(ctx.sV.h, gate_sched[t - 1][1])
                sc.activation(probs[:, 0:9], r_ps[:, :], AF.Exp,
                              bias=r["tc2"][:, 0:1]).then_inc(ctx.sA.h, 1)

        b.scalar(s_gate)

        def v_gate(v):
            for t in range(NT):
                v.wait_ge(ctx.sT.h, rt_t[t])
                v.reduce_max(r["tc"][:, 0:1], r_ps[:, :], axis=AX.X)
                v.drain()
                v.tensor_scalar_mul(r["tc2"][:, 0:1], r["tc"][:, 0:1],
                                    -1.0).then_inc(ctx.sV.h, 1)
                v.wait_ge(ctx.sA.h, rt_ev[t])
                v.reduce_max(r["tc"][:, 0:1], probs[:, 0:9], axis=AX.X)
                v.drain()
                v.tensor_scalar(r["m9"][:, 0:9], probs[:, 0:9], r["tc"][:, 0:1],
                                NEG, ALU.is_ge, ALU.mult)
                v.drain()
                v.tensor_add(r["m9"][:, 0:9], r["m9"][:, 0:9], probs[:, 0:9])
                v.drain()
                v.reduce_max(r["tc"][:, 0:1], r["m9"][:, 0:9], axis=AX.X)
                v.drain()
                v.tensor_scalar(r["m9"][:, 0:9], probs[:, 0:9], r["tc"][:, 0:1],
                                None, ALU.is_ge, ALU.bypass)
                v.drain()
                v.tensor_mul(r["m9"][:, 0:9], r["m9"][:, 0:9], probs[:, 0:9])
                v.reduce_sum(r["tc"][:, 0:1], probs[:, 0:9], axis=AX.X)
                v.drain()
                v.reciprocal(r["tc2"][:, 0:1], r["tc"][:, 0:1])
                v.drain()
                v.tensor_scalar(coef[:, t * 16:t * 16 + 9], r["m9"][:, 0:9],
                                r["tc2"][:, 0:1], 0.5, ALU.mult,
                                ALU.mult).then_inc(ctx.sV.h, 1)
                v.drain()
                v.tensor_scalar(r["tc"][:, 0:1], coef[:, t * 16 + 8:t * 16 + 9],
                                r["rstd"][:, t:t + 1], None, ALU.mult, ALU.bypass)
                v.drain()
                v.tensor_scalar(r["dump"][:, 0:D], x[:, t * D:(t + 1) * D],
                                r["mean"][:, t:t + 1], r["tc"][:, 0:1],
                                ALU.subtract, ALU.mult)
                v.drain()
                v.tensor_add(x[:, t * D:(t + 1) * D], x[:, t * D:(t + 1) * D],
                             r["dump"][:, 0:D]).then_inc(ctx.sV.h, 1)

        b.vector(v_gate)

        emit_barrier(ctx)
        router_es.close()
        h_ps0 = ec(nc.psum_tensor(f"h_ps0_{uid()}", [128, 1024], F32))
        h_ps1 = ec(nc.psum_tensor(f"h_ps1_{uid()}", [128, 1024], F32))
        y_ps0 = ec(nc.psum_tensor(f"y_ps0_{uid()}", [128, 1024], F32))
        y_ps1 = ec(nc.psum_tensor(f"y_ps1_{uid()}", [128, 1024], F32))
        h_ps = [h_ps0, h_ps1]
        y_ps = [y_ps0, y_ps1]

        n_eh = E0 * 2
        w1_dma, w2_dma = [], []
        w1_sem = [ctx.sW0, ctx.sW1]
        w2_sem = [ctx.sW2, ctx.sW3]
        for i in range(n_eh * 32):
            w1_dma += plan(w1_sem[i % 2], 1, step=16)
        for j in range(n_eh * 4):
            w2_dma += plan(w2_sem[j % 2], 1, step=16)
        h_mm, y_mm = [], []
        for _eh in range(n_eh):
            h_mm += plan(ctx.sT, 32)
            y_mm += plan(ctx.sT, 32)
        h_ev = plan(ctx.sA, n_eh * 32)
        y_add = plan(ctx.sV, n_eh * 32)

        def g_weights(g):
            for e in range(E0):
                for half in range(2):
                    i_eh = e * 2 + half
                    for ft in range(32):
                        i = i_eh * 32 + ft
                        if i >= 2:
                            g.wait_ge(ctx.sT.h, h_mm[i - 2])
                        g.dma_start(
                            out=w1buf[:, i % 2, :, :],
                            in_=io["l0_w1"][e].rearrange("(dc p) f -> p dc f", p=128)
                            [:, :, ft * 128:(ft + 1) * 128],
                        ).then_inc(w1_sem[i % 2].h, 16)
                    for fcg in range(4):
                        j = i_eh * 4 + fcg
                        if j >= 2:
                            g.wait_ge(ctx.sT.h, y_mm[(j - 2) * 8 + 7])
                        g.dma_start(
                            out=w2buf[:, j % 2, :, :],
                            in_=io["l0_w2"][e][fcg * 1024:(fcg + 1) * 1024, :]
                            .rearrange("(c p) n -> p c n", p=128),
                        ).then_inc(w2_sem[j % 2].h, 16)

        b.gpsimd(g_weights)

        def t_exp(te):
            for e in range(E0):
                for half in range(2):
                    i_eh = e * 2 + half
                    for ft in range(32):
                        i = i_eh * 32 + ft
                        te.wait_ge(w1_sem[i % 2].h, w1_dma[i])
                        if i >= 2:
                            te.wait_ge(ctx.sA.h, h_ev[i - 2])
                        for dc in range(DC):
                            for nb in range(2):
                                ins = te.matmul(
                                    h_ps[i % 2][:, nb * 512:(nb + 1) * 512],
                                    w1buf[:, i % 2, dc, :],
                                    lnT[:, dc * T + half * 1024 + nb * 512:
                                        dc * T + half * 1024 + (nb + 1) * 512],
                                    start=(dc == 0), stop=(dc == DC - 1))
                                if dc == DC - 1 and nb == 1:
                                    ins.then_inc(ctx.sT.h, 1)
                    for fcg in range(4):
                        j = i_eh * 4 + fcg
                        te.wait_ge(w2_sem[j % 2].h, w2_dma[j])
                        te.wait_ge(ctx.sA.h, h_ev[i_eh * 32 + 31])
                        for tt in range(8):
                            jj = j * 8 + tt
                            if jj >= 2:
                                te.wait_ge(ctx.sV.h, y_add[jj - 2])
                            for fc in range(8):
                                fcol = (fcg * 8 + fc) * 1024 + tt * 128
                                for nb in range(2):
                                    ins = te.matmul(
                                        y_ps[jj % 2][:, nb * 512:(nb + 1) * 512],
                                        h_sb[:, fcol:fcol + 128],
                                        w2buf[:, j % 2, fc, nb * 512:(nb + 1) * 512],
                                        start=(fc == 0), stop=(fc == 7))
                                    if fc == 7 and nb == 1:
                                        ins.then_inc(ctx.sT.h, 1)

        b.tensor(t_exp)

        def s_gelu(sc):
            for e in range(E0):
                for half in range(2):
                    i_eh = e * 2 + half
                    for ft in range(32):
                        i = i_eh * 32 + ft
                        sc.wait_ge(ctx.sT.h, h_mm[i])
                        sc.activation(h_sb[:, ft * 1024:(ft + 1) * 1024],
                                      h_ps[i % 2][:, :], AF.Gelu
                                      ).then_inc(ctx.sA.h, 1)

        b.scalar(s_gelu)

        def v_acc(v):
            for e in range(E0):
                for half in range(2):
                    i_eh = e * 2 + half
                    for fcg in range(4):
                        j = i_eh * 4 + fcg
                        for tt in range(8):
                            jj = j * 8 + tt
                            v.wait_ge(ctx.sT.h, y_mm[jj])
                            g = half * 8 + tt
                            v.scalar_tensor_tensor(
                                x[:, g * D:(g + 1) * D], y_ps[jj % 2][:, :],
                                coef[:, g * 16 + e:g * 16 + e + 1],
                                x[:, g * D:(g + 1) * D],
                                ALU.mult, ALU.add).then_inc(ctx.sV.h, 1)

        b.vector(v_acc)

        emit_barrier(ctx)



def emit_l0attn(ctx, r, io, cc):
    nc, b = ctx.nc, ctx.block
    from contextlib import ExitStack
    with ExitStack() as es:
        ec = es.enter_context
        C0 = 32768 + 16384
        VO = 32 * 130
        arena = ec(nc.sbuf_tensor(f"attn_arena_{uid()}", [128, C0 + VO + 4096 + 4096], BF16))
        lnT = arena[:, 0:DC * T]
        kvT = arena[:, 0:32768].rearrange("p (dc t) -> p dc t", dc=DC)
        qT = arena[:, 32768:32768 + 16384]
        Vo = arena[:, C0:C0 + VO]
        vT = arena[:, C0 + VO:C0 + VO + 4096]
        kT = vT
        E2 = [arena[:, C0 + VO + 4096 + i * 1024:C0 + VO + 4096 + (i + 1) * 1024]
              for i in range(3)]
        o_sb = ec(nc.sbuf_tensor(f"o_sb_{uid()}", [128, 2048], BF16))
        wq_sb = ec(nc.sbuf_tensor(f"wq_sb_{uid()}", [128, 2, DC, 128], BF16))
        wv_sb = ec(nc.sbuf_tensor(f"wv_sb_{uid()}", [128, DC, 128], BF16))
        wk_sb = ec(nc.sbuf_tensor(f"wk_sb_{uid()}", [128, DC, 128], BF16))
        wo_sb = ec(nc.sbuf_tensor(f"wo_sb_{uid()}", [128, 1024], BF16))
        bq_sb = ec(nc.sbuf_tensor(f"bq_sb_{uid()}", [128, 8], F32))
        bk_sb = ec(nc.sbuf_tensor(f"bk_sb_{uid()}", [128, 8], F32))
        bv_sb = ec(nc.sbuf_tensor(f"bv_sb_{uid()}", [128, 8], F32))
        o_f = r["dump"][0:64, 0:512]
        dsc = r["dump"][0:64, 512:768].bitcast(BF16)[0:1, :]
        bo_sb = r["ln_tok"][0:1, 0:1024]

        emit_ln(ctx, r, lnT)

        snd_x = plan(ctx.sX, 4, step=16)

        def q_send(g):
            g.dma_start(out=cc["in"].ap().rearrange("(dc p) t -> p dc t", p=128),
                        in_=lnT.rearrange("p (dc t) -> p dc t", dc=DC)
                        ).then_inc(ctx.sX.h, 16)
            g.dma_start(out=bq_sb[:, :], in_=io["l0_bq"]).then_inc(ctx.sX.h, 16)
            g.dma_start(out=bk_sb[:, :], in_=io["l0_bk"]).then_inc(ctx.sX.h, 16)
            g.dma_start(out=bv_sb[:, :], in_=io["l0_bv"]).then_inc(ctx.sX.h, 16)

        b.sync(q_send)

        cc_done = plan(ctx.sG, 1)
        bo_dma = plan(ctx.sP, 1, step=16)
        wq_sem = [ctx.sW0, ctx.sW1]
        wq_dma = []
        for i in range(8):
            wq_dma += plan(wq_sem[i % 2], 1, step=16)
        q_mm = plan(ctx.sT, 8 * 4)
        q_ev = plan(ctx.sA, 8 * 4)

        def g_cc(g):
            g.wait_ge(ctx.sX.h, snd_x[3])
            g.collective_compute(
                "AllGather", ALU.bypass, replica_groups=cc["rg"],
                ins=[cc["in"].ap().opt()], outs=[cc["out"].ap().opt()],
            ).then_inc(ctx.sG.h, 1)
            g.dma_start(out=bo_sb[:, :], in_=io["l0_bo_row"]).then_inc(ctx.sP.h, 16)
            for pp in range(8):
                if pp >= 2:
                    g.wait_ge(ctx.sT.h, q_mm[(pp - 2) * 4 + 3])
                g.dma_start(out=wq_sb[:, pp % 2, :, :],
                            in_=io["l0_wqT"].rearrange("(dc p) f -> p dc f", p=128)
                            [:, :, pp * 128:(pp + 1) * 128]
                            ).then_inc(wq_sem[pp % 2].h, 16)

        b.gpsimd(g_cc)

        with (nc.psum_tensor(f"q_ps0_{uid()}", [128, 512], F32) as q_ps0,
              nc.psum_tensor(f"q_ps1_{uid()}", [128, 512], F32) as q_ps1):
            q_ps = [q_ps0[:, :], q_ps1[:, :]]

            def t_q(te):
                for pp in range(8):
                    te.wait_ge(wq_sem[pp % 2].h, wq_dma[pp])
                    for tb in range(4):
                        i = pp * 4 + tb
                        if i >= 2:
                            te.wait_ge(ctx.sA.h, q_ev[i - 2])
                        for dc in range(DC):
                            ins = te.matmul(q_ps[i % 2], wq_sb[:, pp % 2, dc, :],
                                            lnT[:, dc * T + tb * 512:dc * T + (tb + 1) * 512],
                                            start=(dc == 0), stop=(dc == DC - 1))
                        ins.then_inc(ctx.sT.h, 1)

            b.tensor(t_q)

            def s_q(sc):
                sc.wait_ge(ctx.sX.h, snd_x[3])
                for pp in range(8):
                    for tb in range(4):
                        i = pp * 4 + tb
                        sc.wait_ge(ctx.sT.h, q_mm[i])
                        sc.activation(qT[:, pp * 2048 + tb * 512:pp * 2048 + (tb + 1) * 512],
                                      q_ps[i % 2], AF.Identity,
                                      bias=bq_sb[:, pp:pp + 1]).then_inc(ctx.sA.h, 1)

            b.scalar(s_q)

        kv_x = plan(ctx.sW, 2, step=16)

        def g_kv(g):
            g.wait_ge(ctx.sG.h, cc_done[0])
            g.wait_ge(ctx.sA.h, q_ev[-1])
            for rk in range(2):
                g.dma_start(
                    out=kvT[:, :, rk * 2048:(rk + 1) * 2048],
                    in_=cc["out"].ap()[rk * 1024:(rk + 1) * 1024, :]
                    .rearrange("(dc p) t -> p dc t", p=128),
                ).then_inc(ctx.sW.h, 16)

        b.gpsimd(g_kv)

        vo_init = plan(ctx.sV, 1)

        def v_voinit(v):
            v.memset(Vo.rearrange("p (kc h j) -> p kc h j", h=2, j=65)[:, :, :, 64:65],
                     1.0).then_inc(ctx.sV.h, 1)

        b.vector(v_voinit)

        st = dict(kvT=kvT, qT=qT, Vo=Vo, vT=vT, kT=kT, E2=E2, o_sb=o_sb,
                  wv_sb=wv_sb, wk_sb=wk_sb, wo_sb=wo_sb, bk_sb=bk_sb, bv_sb=bv_sb,
                  bo_sb=bo_sb, dsc=dsc, o_f=o_f, kv_x=kv_x, vo_init=vo_init,
                  bo_dma=bo_dma)
        prev_last_v, prev_last_t = None, None
        for pp in range(8):
            prev_last_v, prev_last_t = _l0attn_pair(ctx, r, io, pp, st,
                                                    prev_last_v, prev_last_t)
            if DEBUG_ATTN and pp == 0:
                dbg3 = nc.dram_tensor("dbg3", [128, 2048], BF16,
                                      kind="ExternalOutput").ap()
                dbg4 = nc.dram_tensor("dbg4", [128, 4096], BF16,
                                      kind="ExternalOutput").ap()
                dbg5 = nc.dram_tensor("dbg5", [128, 4160], BF16,
                                      kind="ExternalOutput").ap()
                dx = plan(ctx.sX, 3, step=16)

                def q_dbg(g):
                    g.wait_ge(ctx.sV.h, prev_last_v)
                    g.dma_start(out=dbg3, in_=o_sb[:, :]).then_inc(ctx.sX.h, 16)
                    g.dma_start(out=dbg4, in_=kT[:, :]).then_inc(ctx.sX.h, 16)
                    g.dma_start(out=dbg5, in_=Vo[:, :]).then_inc(ctx.sX.h, 16)

                b.sync(q_dbg)
                emit_barrier(ctx)

    emit_barrier(ctx)


def _l0attn_pair(ctx, r, io, pp, s, prev_last_v, prev_last_t):
    nc, b = ctx.nc, ctx.block
    from contextlib import ExitStack
    x = r["x"]
    kvT, qT, Vo, vT, kT, E2, o_sb = (s["kvT"], s["qT"], s["Vo"], s["vT"], s["kT"],
                                     s["E2"], s["o_sb"])

    wdma = plan(ctx.sW, 3, step=16)

    def g_w(g):
        if prev_last_t is not None:
            g.wait_ge(ctx.sT.h, prev_last_t)
        g.dma_start(out=s["wv_sb"][:, :, :],
                    in_=io["l0_wvT"].rearrange("(dc p) f -> p dc f", p=128)
                    [:, :, pp * 128:(pp + 1) * 128]).then_inc(ctx.sW.h, 16)
        g.dma_start(out=s["wk_sb"][:, :, :],
                    in_=io["l0_wkT"].rearrange("(dc p) f -> p dc f", p=128)
                    [:, :, pp * 128:(pp + 1) * 128]).then_inc(ctx.sW.h, 16)
        g.dma_start(out=s["wo_sb"][:, :],
                    in_=io["l0_woT"][pp * 128:(pp + 1) * 128, :]).then_inc(ctx.sW.h, 16)

    b.gpsimd(g_w)

    v_mm = plan(ctx.sT, 8)
    tr_mm = plan(ctx.sT, 32)
    k_mm = plan(ctx.sT, 8)
    v_ev = plan(ctx.sA, 8)
    k_ev = plan(ctx.sA, 8)
    tr_ev = plan(ctx.sV, 32)

    with ExitStack() as es:
        ec = es.enter_context
        vk_ps = [ec(nc.psum_tensor(f"vk_ps{i}_{pp}_{uid()}", [128, 512], F32))[:, :]
                 for i in range(4)]
        t_ps = [ec(nc.psum_tensor(f"t_ps{i}_{pp}_{uid()}", [128, 128], BF16))[:, :]
                for i in range(2)]

        def t_a(te):
            te.wait_ge(ctx.sW.h, wdma[2])
            if prev_last_v is not None:
                te.wait_ge(ctx.sV.h, prev_last_v)
            for tbg in range(2):
                for dc in range(DC):
                    for tb4 in range(4):
                        i = tbg * 4 + tb4
                        if dc == 0 and tbg == 1:
                            te.wait_ge(ctx.sA.h, v_ev[i - 4])
                        ins = te.matmul(vk_ps[tb4], s["wv_sb"][:, dc, :],
                                        kvT[:, dc, tbg * 2048 + tb4 * 512:
                                            tbg * 2048 + (tb4 + 1) * 512],
                                        start=(dc == 0), stop=(dc == DC - 1))
                        if dc == DC - 1:
                            ins.then_inc(ctx.sT.h, 1)
            for kc in range(32):
                te.wait_ge(ctx.sA.h, v_ev[kc // 4])
                if kc >= 2:
                    te.wait_ge(ctx.sV.h, tr_ev[kc - 2])
                if kc == 0:
                    te.wait_ge(ctx.sV.h, s["vo_init"][0])
                te.transpose(t_ps[kc % 2], vT[:, kc * 128:(kc + 1) * 128],
                             r["ident"][:, :]).then_inc(ctx.sT.h, 1)
            for tbg in range(2):
                for dc in range(DC):
                    for tb4 in range(4):
                        i = tbg * 4 + tb4
                        if dc == 0:
                            te.wait_ge(ctx.sA.h, v_ev[7] if tbg == 0 else k_ev[i - 4])
                        ins = te.matmul(vk_ps[tb4], s["wk_sb"][:, dc, :],
                                        kvT[:, dc, tbg * 2048 + tb4 * 512:
                                            tbg * 2048 + (tb4 + 1) * 512],
                                        start=(dc == 0), stop=(dc == DC - 1))
                        if dc == DC - 1:
                            ins.then_inc(ctx.sT.h, 1)

        b.tensor(t_a)

        def s_a(sc):
            for i in range(8):
                sc.wait_ge(ctx.sT.h, v_mm[i])
                sc.activation(vT[:, i * 512:(i + 1) * 512], vk_ps[i % 4],
                              AF.Identity, bias=s["bv_sb"][:, pp:pp + 1]
                              ).then_inc(ctx.sA.h, 1)
            for i in range(8):
                sc.wait_ge(ctx.sT.h, k_mm[i])
                if i == 0:
                    sc.wait_ge(ctx.sT.h, tr_mm[31])
                sc.activation(kT[:, i * 512:(i + 1) * 512], vk_ps[i % 4],
                              AF.Identity, bias=s["bk_sb"][:, pp:pp + 1]
                              ).then_inc(ctx.sA.h, 1)

        b.scalar(s_a)

        def v_a(v):
            for kc in range(32):
                v.wait_ge(ctx.sT.h, tr_mm[kc])
                dst = Vo[:, kc * 130:(kc + 1) * 130].rearrange(
                    "p (h j) -> p h j", h=2)[:, :, 0:64]
                src = t_ps[kc % 2].rearrange("p (h j) -> p h j", h=2)
                v.tensor_copy(dst, src).then_inc(ctx.sV.h, 1)

        b.vector(v_a)

    sc_mm, pv_mm, den_mm, ex_ev, den_v = [], [], [], [], []
    for qb in range(4):
        sq, pq = [None] * 32, [None] * 32
        sq[0] = plan(ctx.sT, 1)[0]
        sq[1] = plan(ctx.sT, 1)[0]
        sq[2] = plan(ctx.sT, 1)[0]
        for kc in range(32):
            pq[kc] = plan(ctx.sT, 1)[0]
            if kc + 3 < 32:
                sq[kc + 3] = plan(ctx.sT, 1)[0]
        sc_mm.append(sq)
        pv_mm.append(pq)
        den_mm.append(plan(ctx.sT, 2))
        ex_ev.append(plan(ctx.sA, 32))
        den_v.append(plan(ctx.sV, 6))
    p_mm = plan(ctx.sT, NT)
    p_add = plan(ctx.sV, NT)

    with ExitStack() as es:
        ec = es.enter_context
        s_ps = [ec(nc.psum_tensor(f"s_ps{i}_{pp}_{uid()}", [128, 1024], F32))[:, :]
                for i in range(3)]
        o_ps = [ec(nc.psum_tensor(f"o_ps{i}_{pp}_{uid()}", [65, 512], F32))[:, :]
                for i in range(2)]

        def t_b(te):
            def scores(qb, kc):
                if kc >= 3:
                    te.wait_ge(ctx.sA.h, ex_ev[qb][kc - 3])
                elif qb >= 1:
                    te.wait_ge(ctx.sA.h, ex_ev[qb - 1][29 + kc])
                    if kc == 0:
                        te.wait_ge(ctx.sV.h, den_v[qb - 1][5])
                else:
                    if kc == 0:
                        te.wait_ge(ctx.sA.h, k_ev[7])
                        te.wait_ge(ctx.sV.h, tr_ev[31])
                for hp in range(2):
                    ins = te.matmul(
                        s_ps[kc % 3][:, hp * 512:(hp + 1) * 512],
                        kT[hp * 64:(hp + 1) * 64, kc * 128:(kc + 1) * 128],
                        qT[hp * 64:(hp + 1) * 64,
                           pp * 2048 + qb * 512:pp * 2048 + (qb + 1) * 512],
                        start=True, stop=True)
                ins.then_inc(ctx.sT.h, 1)

            def pv(qb, kc):
                te.wait_ge(ctx.sA.h, ex_ev[qb][kc])
                for hp in range(2):
                    ins = te.matmul(
                        o_ps[hp],
                        Vo[:, kc * 130 + hp * 65:kc * 130 + hp * 65 + 65],
                        E2[kc % 3][:, hp * 512:(hp + 1) * 512],
                        start=(kc == 0), stop=(kc == 31))
                ins.then_inc(ctx.sT.h, 1)

            for qb in range(4):
                scores(qb, 0)
                scores(qb, 1)
                scores(qb, 2)
                for kc in range(32):
                    pv(qb, kc)
                    if kc + 3 < 32:
                        scores(qb, kc + 3)
                for hp in range(2):
                    te.wait_ge(ctx.sV.h, den_v[qb][hp * 3 + 0])
                    te.matmul(s_ps[0][0:64, 0:512], r["ones"][0:1, 0:64],
                              s["dsc"][0:1, :], start=True, stop=True
                              ).then_inc(ctx.sT.h, 1)
            te.wait_ge(ctx.sW.h, wdma[2])
            if pp == 0:
                te.wait_ge(ctx.sP.h, s["bo_dma"][0])
            for tt in range(NT):
                if tt == 0:
                    te.wait_ge(ctx.sV.h, den_v[3][5])
                if tt >= 2:
                    te.wait_ge(ctx.sV.h, p_add[tt - 2])
                for nb in range(2):
                    ins = te.matmul(s_ps[tt % 2][:, nb * 512:(nb + 1) * 512],
                                    o_sb[:, tt * 128:(tt + 1) * 128],
                                    s["wo_sb"][:, nb * 512:(nb + 1) * 512],
                                    start=True, stop=(pp != 0))
                    if pp == 0:
                        ins = te.matmul(s_ps[tt % 2][:, nb * 512:(nb + 1) * 512],
                                        r["ones"][0:1, 0:128],
                                        s["bo_sb"][0:1, nb * 512:(nb + 1) * 512],
                                        start=False, stop=True)
                ins.then_inc(ctx.sT.h, 1)

        b.tensor(t_b)

        def s_b(sc):
            for qb in range(4):
                for kc in range(32):
                    sc.wait_ge(ctx.sT.h, sc_mm[qb][kc])
                    if not (qb == 0 and kc < 3):
                        pq2, pkc = (qb, kc - 3) if kc >= 3 else (qb - 1, 29 + kc)
                        sc.wait_ge(ctx.sT.h, pv_mm[pq2][pkc])
                    sc.activation(E2[kc % 3][:, 0:1024], s_ps[kc % 3], AF.Exp,
                                  scale=ISQRT_D).then_inc(ctx.sA.h, 1)

        b.scalar(s_b)

        def v_b(v):
            for qb in range(4):
                for hp in range(2):
                    v.wait_ge(ctx.sT.h, pv_mm[qb][31])
                    with nc.allow_low_precision(reason="bf16 softmax denom"):
                        v.reciprocal(s["dsc"][0:1, :], o_ps[hp][64:65, :]
                                     ).then_inc(ctx.sV.h, 1)
                    v.tensor_copy(s["o_f"][:, :], o_ps[hp][0:64, :]
                                  ).then_inc(ctx.sV.h, 1)
                    v.wait_ge(ctx.sT.h, den_mm[qb][hp])
                    v.drain()
                    v.tensor_mul(
                        o_sb[hp * 64:(hp + 1) * 64, qb * 512:(qb + 1) * 512],
                        s["o_f"][:, :], s_ps[0][0:64, 0:512]).then_inc(ctx.sV.h, 1)
            for tt in range(NT):
                v.wait_ge(ctx.sT.h, p_mm[tt])
                v.tensor_add(x[:, tt * D:(tt + 1) * D], x[:, tt * D:(tt + 1) * D],
                             s_ps[tt % 2]).then_inc(ctx.sV.h, 1)

        b.vector(v_b)

    return p_add[-1], p_mm[-1]



def emit_l1ffn(ctx, r, io):
    nc, b = ctx.nc, ctx.block
    x = r["x"]
    from contextlib import ExitStack
    with ExitStack() as es:
        ec = es.enter_context
        arena = ec(nc.sbuf_tensor(f"ffn1_arena_{uid()}", [128, 32768], BF16))
        lnT = arena[:, 0:16384]
        hgs = arena[:, 16384:24576]
        act = arena[:, 24576:32768]
        gw_sb = ec(nc.sbuf_tensor(f"gw_{uid()}", [128, DC, 16], BF16))
        gbr_sb = ec(nc.sbuf_tensor(f"gbr_{uid()}", [1, 16], BF16))
        gb_sb = ec(nc.sbuf_tensor(f"gb_{uid()}", [128, 16], F32))
        scores = ec(nc.sbuf_tensor(f"scor_{uid()}", [128, NT * 16], F32))
        sfc = ec(nc.sbuf_tensor(f"sfc_{uid()}", [128, 16], F32))
        msk = ec(nc.sbuf_tensor(f"msk_{uid()}", [128, 16], F32))
        mskc = ec(nc.sbuf_tensor(f"mskc_{uid()}", [128, 16], F32))
        gs4 = ec(nc.sbuf_tensor(f"gs4_{uid()}", [128, 4], F32))
        t4 = ec(nc.sbuf_tensor(f"t4_{uid()}", [128, 4], F32))
        coef = r["coef"]

        emit_ln(ctx, r, lnT)

        gpre = plan(ctx.sP, 2, step=16)

        def g_pre(g):
            g.dma_start(out=gw_sb[:, :, :],
                        in_=io["l1f_gwT"].rearrange("(dc p) n -> p dc n", p=128)
                        ).then_inc(ctx.sP.h, 16)
            g.dma_start(out=gbr_sb[:, :], in_=io["l1f_gb_row"]).then_inc(ctx.sP.h, 16)

        b.gpsimd(g_pre)

        gbb_mm = plan(ctx.sT, 1)
        g_mm = plan(ctx.sT, NT)
        sig_a = plan(ctx.sA, NT)
        gbcp_v = plan(ctx.sV, 1)
        gate_v = plan(ctx.sV, NT)

        with nc.psum_tensor(f"g_ps_{uid()}", [128, 16], F32) as g_ps:
            def t_gate(te):
                te.wait_ge(ctx.sP.h, gpre[1])
                te.matmul(g_ps[:, :], r["ones"][0:1, 0:128], gbr_sb[0:1, :],
                          start=True, stop=True).then_inc(ctx.sT.h, 1)
                for t in range(NT):
                    if t == 0:
                        te.wait_ge(ctx.sV.h, gbcp_v[0])
                    else:
                        te.wait_ge(ctx.sA.h, sig_a[t - 1])
                    for dc in range(DC):
                        ins = te.matmul(g_ps[:, :],
                                        lnT[:, dc * T + t * 128:dc * T + (t + 1) * 128],
                                        gw_sb[:, dc, :],
                                        start=(dc == 0), stop=(dc == DC - 1))
                    ins.then_inc(ctx.sT.h, 1)

            b.tensor(t_gate)

            def s_gate(sc):
                for t in range(NT):
                    sc.wait_ge(ctx.sT.h, g_mm[t])
                    sc.activation(scores[:, t * 16:(t + 1) * 16], g_ps[:, :],
                                  AF.Sigmoid).then_inc(ctx.sA.h, 1)

            b.scalar(s_gate)

            def v_gate(v):
                v.wait_ge(ctx.sT.h, gbb_mm[0])
                v.tensor_copy(gb_sb[:, :], g_ps[:, :]).then_inc(ctx.sV.h, 1)
                for t in range(NT):
                    sco = scores[:, t * 16:(t + 1) * 16]
                    v.wait_ge(ctx.sA.h, sig_a[t])
                    v.tensor_add(sfc[:, :], sco, gb_sb[:, :])
                    v.drain()
                    for g4 in range(4):
                        sf_g = sfc[:, g4 * 4:(g4 + 1) * 4]
                        v.reduce_max(r["tc"][:, 0:1], sf_g, axis=AX.X)
                        v.drain()
                        v.tensor_scalar(t4[:, :], sf_g, r["tc"][:, 0:1], NEG,
                                        ALU.is_ge, ALU.mult)
                        v.drain()
                        v.tensor_add(t4[:, :], t4[:, :], sf_g)
                        v.drain()
                        v.reduce_max(r["tc2"][:, 0:1], t4[:, :], axis=AX.X)
                        v.drain()
                        v.tensor_add(gs4[:, g4:g4 + 1], r["tc"][:, 0:1],
                                     r["tc2"][:, 0:1])
                        v.drain()
                    v.reduce_max(r["tc"][:, 0:1], gs4[:, :], axis=AX.X)
                    v.drain()
                    v.tensor_scalar(t4[:, :], gs4[:, :], r["tc"][:, 0:1], NEG,
                                    ALU.is_ge, ALU.mult)
                    v.drain()
                    v.tensor_add(t4[:, :], t4[:, :], gs4[:, :])
                    v.drain()
                    v.reduce_max(r["tc2"][:, 0:1], t4[:, :], axis=AX.X)
                    v.drain()
                    v.tensor_scalar(t4[:, :], gs4[:, :], r["tc2"][:, 0:1], None,
                                    ALU.is_ge, ALU.bypass)
                    v.drain()
                    v.tensor_scalar(t4[:, :], t4[:, :], 1.0, 1.0e30,
                                    ALU.subtract, ALU.mult)
                    v.drain()
                    for g4 in range(4):
                        v.tensor_scalar_add(msk[:, g4 * 4:(g4 + 1) * 4],
                                            sfc[:, g4 * 4:(g4 + 1) * 4],
                                            t4[:, g4:g4 + 1])
                    v.drain()
                    v.tensor_copy(mskc[:, :], msk[:, :])
                    v.drain()
                    for it in range(4):
                        v.reduce_max(r["tc"][:, 0:1], msk[:, :], axis=AX.X)
                        v.drain()
                        if it < 3:
                            v.tensor_scalar(r["m9"][:, 0:16], msk[:, :],
                                            r["tc"][:, 0:1], NEG,
                                            ALU.is_ge, ALU.mult)
                            v.drain()
                            v.tensor_add(msk[:, :], msk[:, :], r["m9"][:, 0:16])
                            v.drain()
                    v.tensor_scalar(r["m9"][:, 0:16], mskc[:, :], r["tc"][:, 0:1],
                                    None, ALU.is_ge, ALU.bypass)
                    v.drain()
                    v.tensor_mul(r["m9"][:, 0:16], r["m9"][:, 0:16], sco)
                    v.drain()
                    v.reduce_sum(r["tc"][:, 0:1], r["m9"][:, 0:16], axis=AX.X)
                    v.drain()
                    v.tensor_scalar_add(r["tc"][:, 0:1], r["tc"][:, 0:1], 1.0e-20)
                    v.drain()
                    v.reciprocal(r["tc2"][:, 0:1], r["tc"][:, 0:1])
                    v.drain()
                    v.tensor_scalar(coef[:, t * 16:(t + 1) * 16], r["m9"][:, 0:16],
                                    r["tc2"][:, 0:1], 1.25, ALU.mult, ALU.mult)
                    v.drain()
                    v.tensor_scalar_mul(r["tc"][:, 0:1], r["rstd"][:, t:t + 1], 0.5)
                    v.drain()
                    v.tensor_scalar(r["dump"][:, 0:D], x[:, t * D:(t + 1) * D],
                                    r["mean"][:, t:t + 1], r["tc"][:, 0:1],
                                    ALU.subtract, ALU.mult)
                    v.drain()
                    v.tensor_add(x[:, t * D:(t + 1) * D], x[:, t * D:(t + 1) * D],
                                 r["dump"][:, 0:D]).then_inc(ctx.sV.h, 1)

            b.vector(v_gate)

        emit_barrier(ctx)

        expert_es = ExitStack()
        wg_buf = expert_es.enter_context(
            nc.sbuf_tensor(f"wg_{uid()}", [128, 2, DC, 512], BF16))
        wu_buf = expert_es.enter_context(
            nc.sbuf_tensor(f"wu_{uid()}", [128, 2, DC, 512], BF16))
        wd_buf = expert_es.enter_context(
            nc.sbuf_tensor(f"wd_{uid()}", [128, 2, 4, 1024], BF16))
        we_sem = [ctx.sW0, ctx.sW1]
        up_mm, act_a, act_v, dn_mm, dn_v, wdma = [], [], [], [], [], []
        for e in range(E1):
            wdma.append(plan(we_sem[e % 2], 3, step=16))
            up_mm.append(plan(ctx.sT, 8))
            dn_mm.append(plan(ctx.sT, NT))
            act_a.append(plan(ctx.sA, 4))
            act_v.append(plan(ctx.sV, 4))
            dn_v.append(plan(ctx.sV, NT))

        def g_we(g):
            for e in range(E1):
                if e >= 2:
                    g.wait_ge(ctx.sV.h, dn_v[e - 2][-1])
                g.dma_start(out=wg_buf[:, e % 2, :, :],
                            in_=io["l1f_wgT"][e].rearrange("(dc p) n -> p dc n", p=128)
                            ).then_inc(we_sem[e % 2].h, 16)
                g.dma_start(out=wu_buf[:, e % 2, :, :],
                            in_=io["l1f_wuT"][e].rearrange("(dc p) n -> p dc n", p=128)
                            ).then_inc(we_sem[e % 2].h, 16)
                g.dma_start(out=wd_buf[:, e % 2, :, :],
                            in_=io["l1f_wdT"][e].rearrange("(ic p) n -> p ic n", p=128)
                            ).then_inc(we_sem[e % 2].h, 16)

        b.gpsimd(g_we)

        for e in range(E1):
            _l1ffn_expert(ctx, r, e, dict(
                lnT=lnT, hgs=hgs, act=act, coef=coef,
                wg=wg_buf[:, e % 2, :, :], wu=wu_buf[:, e % 2, :, :],
                wd=wd_buf[:, e % 2, :, :],
                wdma=wdma[e], up_mm=up_mm[e], dn_mm=dn_mm[e],
                act_a=act_a[e], act_v=act_v[e], dn_v=dn_v[e],
                we_sem=we_sem[e % 2],
                prev_dn_v=(dn_v[e - 1][-1] if e > 0 else None)))

        expert_es.close()
        _l1ffn_shared(ctx, r, io, dict(lnT=lnT, hgs=hgs, act=act,
                                       prev_dn_v=dn_v[-1][-1]))

    emit_barrier(ctx)


def _l1ffn_expert(ctx, r, e, s):
    nc, b = ctx.nc, ctx.block
    x = r["x"]
    lnT, hgs, act = s["lnT"], s["hgs"], s["act"]
    from contextlib import ExitStack
    with ExitStack() as es:
        ec = es.enter_context
        up_ps = [ec(nc.psum_tensor(f"up{i}_{e}_{uid()}", [128, 512], F32))[:, :]
                 for i in range(4)]

        def t_up(te):
            te.wait_ge(s["we_sem"].h, s["wdma"][2])
            if s["prev_dn_v"] is not None:
                te.wait_ge(ctx.sV.h, s["prev_dn_v"])
            for kind in range(2):
                w = s["wg"] if kind == 0 else s["wu"]
                for ic in range(4):
                    i = kind * 4 + ic
                    if kind == 1:
                        if ic >= 1:
                            te.wait_ge(ctx.sV.h, s["act_v"][ic - 1])
                        else:
                            te.wait_ge(ctx.sA.h, s["act_a"][3])
                    for dc in range(DC):
                        for tb in range(4):
                            if dc == 0 and i >= 1 and kind == 0 and tb == 0 and ic >= 1:
                                te.wait_ge(ctx.sA.h, s["act_a"][ic - 1])
                            ins = te.matmul(
                                up_ps[tb], w[:, dc, ic * 128:(ic + 1) * 128],
                                lnT[:, dc * T + tb * 512:dc * T + (tb + 1) * 512],
                                start=(dc == 0), stop=(dc == DC - 1))
                        if dc == DC - 1:
                            ins.then_inc(ctx.sT.h, 1)

        b.tensor(t_up)

        def s_up(sc):
            for ic in range(4):
                sc.wait_ge(ctx.sT.h, s["up_mm"][ic])
                for tb in range(4):
                    ins = sc.activation(hgs[:, ic * 2048 + tb * 512:
                                            ic * 2048 + (tb + 1) * 512],
                                        up_ps[tb], AF.Silu)
                ins.then_inc(ctx.sA.h, 1)

        b.scalar(s_up)

        def v_up(v):
            for ic in range(4):
                v.wait_ge(ctx.sT.h, s["up_mm"][4 + ic])
                for tb in range(4):
                    ins = v.tensor_tensor(
                        act[:, ic * 2048 + tb * 512:ic * 2048 + (tb + 1) * 512],
                        hgs[:, ic * 2048 + tb * 512:ic * 2048 + (tb + 1) * 512],
                        up_ps[tb], ALU.mult)
                ins.then_inc(ctx.sV.h, 1)

        b.vector(v_up)

    with ExitStack() as es:
        ec = es.enter_context
        y_ps = [ec(nc.psum_tensor(f"y{i}_{e}_{uid()}", [128, 1024], F32))[:, :]
                for i in range(2)]

        def t_dn(te):
            for tt in range(NT):
                if tt == 0:
                    te.wait_ge(ctx.sV.h, s["act_v"][3])
                if tt >= 2:
                    te.wait_ge(ctx.sV.h, s["dn_v"][tt - 2])
                for ic in range(4):
                    for nb in range(2):
                        ins = te.matmul(
                            y_ps[tt % 2][:, nb * 512:(nb + 1) * 512],
                            act[:, ic * 2048 + tt * 128:ic * 2048 + (tt + 1) * 128],
                            s["wd"][:, ic, nb * 512:(nb + 1) * 512],
                            start=(ic == 0), stop=(ic == 3))
                ins.then_inc(ctx.sT.h, 1)

        b.tensor(t_dn)

        def v_dn(v):
            for tt in range(NT):
                v.wait_ge(ctx.sT.h, s["dn_mm"][tt])
                v.scalar_tensor_tensor(
                    x[:, tt * D:(tt + 1) * D], y_ps[tt % 2],
                    s["coef"][:, tt * 16 + e:tt * 16 + e + 1],
                    x[:, tt * D:(tt + 1) * D],
                    ALU.mult, ALU.add).then_inc(ctx.sV.h, 1)

        b.vector(v_dn)


def _l1ffn_shared(ctx, r, io, s):
    nc, b = ctx.nc, ctx.block
    x = r["x"]
    lnT, hgs, act = s["lnT"], s["hgs"], s["act"]
    from contextlib import ExitStack
    actS = hgs.tensor if False else None
    with ExitStack() as es:
        ec = es.enter_context
        wsg_sb = ec(nc.sbuf_tensor(f"wsg_{uid()}", [128, 2, DC, 128], BF16))
        wsu_sb = ec(nc.sbuf_tensor(f"wsu_{uid()}", [128, 2, DC, 128], BF16))
        wsd_sb = ec(nc.sbuf_tensor(f"wsd_{uid()}", [128, DC, 1024], BF16))
        hgS = ec(nc.sbuf_tensor(f"hgS_{uid()}", [128, 2048], BF16))

        ws_sem = [ctx.sW2, ctx.sW3]
        wdma = []
        for sc8 in range(SH // 128):
            wdma.append(plan(ws_sem[sc8 % 2], 2, step=16))
        wsd_dma = plan(ctx.sP, 1, step=16)

        up_mm = [plan(ctx.sT, 2) for _ in range(8)]
        silu_a = [plan(ctx.sA, 4) for _ in range(8)]
        mul_v = [plan(ctx.sV, 4) for _ in range(8)]
        dn_mm = plan(ctx.sT, NT)
        dn_v = plan(ctx.sV, NT)

        def g_ws(g):
            g.wait_ge(ctx.sV.h, s["prev_dn_v"])
            g.dma_start(out=wsd_sb[:, :, :],
                        in_=io["l1f_wsdT"].rearrange("(sc p) n -> p sc n", p=128)
                        ).then_inc(ctx.sP.h, 16)
            for sc8 in range(8):
                if sc8 >= 2:
                    g.wait_ge(ctx.sV.h, mul_v[sc8 - 2][-1])
                g.dma_start(out=wsg_sb[:, sc8 % 2, :, :],
                            in_=io["l1f_wsgT"].rearrange("(dc p) n -> p dc n", p=128)
                            [:, :, sc8 * 128:(sc8 + 1) * 128]
                            ).then_inc(ws_sem[sc8 % 2].h, 16)
                g.dma_start(out=wsu_sb[:, sc8 % 2, :, :],
                            in_=io["l1f_wsuT"].rearrange("(dc p) n -> p dc n", p=128)
                            [:, :, sc8 * 128:(sc8 + 1) * 128]
                            ).then_inc(ws_sem[sc8 % 2].h, 16)

        b.gpsimd(g_ws)

        with ExitStack() as es2:
            up_ps = [es2.enter_context(
                nc.psum_tensor(f"ups{i}_{uid()}", [128, 512], F32))[:, :]
                for i in range(8)]

            def t_sup(te):
                if s["prev_dn_v"] is not None:
                    te.wait_ge(ctx.sV.h, s["prev_dn_v"])
                for sc8 in range(8):
                    te.wait_ge(ws_sem[sc8 % 2].h, wdma[sc8][1])
                    for kind in range(2):
                        w = wsg_sb if kind == 0 else wsu_sb
                        ps_base = 0 if kind == 0 else 4
                        if kind == 0 and sc8 >= 1:
                            te.wait_ge(ctx.sA.h, silu_a[sc8 - 1][-1])
                        if kind == 1 and sc8 >= 1:
                            te.wait_ge(ctx.sV.h, mul_v[sc8 - 1][-1])
                        for dc in range(DC):
                            for tb in range(4):
                                ins = te.matmul(
                                    up_ps[ps_base + tb],
                                    w[:, sc8 % 2, dc, :],
                                    lnT[:, dc * T + tb * 512:dc * T + (tb + 1) * 512],
                                    start=(dc == 0), stop=(dc == DC - 1))
                            if dc == DC - 1:
                                ins.then_inc(ctx.sT.h, 1)

            b.tensor(t_sup)

            def s_sup(sc):
                for sc8 in range(8):
                    sc.wait_ge(ctx.sT.h, up_mm[sc8][0])
                    if sc8 >= 1:
                        sc.wait_ge(ctx.sV.h, mul_v[sc8 - 1][-1])
                    for tb in range(4):
                        sc.activation(hgS[:, tb * 512:(tb + 1) * 512],
                                      up_ps[tb], AF.Silu).then_inc(ctx.sA.h, 1)

            b.scalar(s_sup)

            def v_sup(v):
                for sc8 in range(8):
                    v.wait_ge(ctx.sT.h, up_mm[sc8][1])
                    for tb in range(4):
                        v.wait_ge(ctx.sA.h, silu_a[sc8][tb])
                        dst = hgs if sc8 < 4 else act
                        v.tensor_tensor(
                            dst[:, (sc8 % 4) * 2048 + tb * 512:
                                (sc8 % 4) * 2048 + (tb + 1) * 512],
                            hgS[:, tb * 512:(tb + 1) * 512],
                            up_ps[4 + tb], ALU.mult).then_inc(ctx.sV.h, 1)

            b.vector(v_sup)

        with ExitStack() as es3:
            y_ps = [es3.enter_context(
                nc.psum_tensor(f"ys{i}_{uid()}", [128, 1024], F32))[:, :]
                for i in range(2)]

            def t_sdn(te):
                te.wait_ge(ctx.sP.h, wsd_dma[0])
                for tt in range(NT):
                    if tt == 0:
                        te.wait_ge(ctx.sV.h, mul_v[7][-1])
                    if tt >= 2:
                        te.wait_ge(ctx.sV.h, dn_v[tt - 2])
                    for sc8 in range(8):
                        src = hgs if sc8 < 4 else act
                        for nb in range(2):
                            ins = te.matmul(
                                y_ps[tt % 2][:, nb * 512:(nb + 1) * 512],
                                src[:, (sc8 % 4) * 2048 + tt * 128:
                                    (sc8 % 4) * 2048 + (tt + 1) * 128],
                                wsd_sb[:, sc8, nb * 512:(nb + 1) * 512],
                                start=(sc8 == 0), stop=(sc8 == 7))
                    ins.then_inc(ctx.sT.h, 1)

            b.tensor(t_sdn)

            def v_sdn(v):
                for tt in range(NT):
                    v.wait_ge(ctx.sT.h, dn_mm[tt])
                    v.scalar_tensor_tensor(
                        x[:, tt * D:(tt + 1) * D], y_ps[tt % 2], 0.5,
                        x[:, tt * D:(tt + 1) * D],
                        ALU.mult, ALU.add).then_inc(ctx.sV.h, 1)

            b.vector(v_sdn)



def emit_l1attn(ctx, r, io, cc):
    nc, b = ctx.nc, ctx.block
    from contextlib import ExitStack
    with ExitStack() as es:
        ec = es.enter_context
        arena = ec(nc.sbuf_tensor(f"l1a_arena_{uid()}",
                                  [128, 32768 + 16384 + 12544], BF16))
        lnT = arena[:, 0:DC * T]
        ln4T = arena[:, 0:32768].rearrange("p (dc t) -> p dc t", dc=DC)
        qkvgT = arena[:, 32768:32768 + 16384]
        flat_sb = arena[:, 32768:32768 + 8192]
        flatT_sb = arena[:, 32768 + 8192:32768 + 16384]
        qtok = arena[:, 49152:49152 + 4096]
        ktok = arena[:, 53248:53248 + 4096]
        fin = arena[:, 57344:57344 + 4096]
        ftok = arena[:, 61440:61440 + 256]
        wpT_sb = arena[:, 49152:49152 + 8192].rearrange("p (tp n) -> p tp n", tp=DC)
        wbuf = ec(nc.sbuf_tensor(f"l1aw_{uid()}", [128, DC, 4, 128], BF16))
        ksum = ec(nc.sbuf_tensor(f"ksum_{uid()}", [128, 64], BF16))
        b_sb = ec(nc.sbuf_tensor(f"l1ab_{uid()}", [128, 16], F32))
        bp_row = ec(nc.sbuf_tensor(f"l1abp_{uid()}", [1, D], BF16))
        M_sb = ec(nc.sbuf_tensor(f"msb_{uid()}", [128, 64], BF16))
        z_sb = ec(nc.sbuf_tensor(f"zsb_{uid()}", [128, 1], F32))
        zt_sb = ec(nc.sbuf_tensor(f"ztsb_{uid()}", [128, 1], F32))
        t1f = r["dump"][:, 512:1024]

        emit_ln(ctx, r, lnT)

        snd_x = plan(ctx.sX, 2, step=16)

        def q_send(g):
            g.dma_start(out=cc["in"].ap().rearrange("(dc p) t -> p dc t", p=128),
                        in_=lnT.rearrange("p (dc t) -> p dc t", dc=DC)
                        ).then_inc(ctx.sX.h, 16)
            g.dma_start(out=b_sb[:, :], in_=io["l1a_b"]).then_inc(ctx.sX.h, 16)

        b.sync(q_send)

        cc_done = plan(ctx.sG, 1)
        wp_dma = plan(ctx.sP, 1, step=16)
        kv_x = plan(ctx.sW, 2, step=16)

        def g_cc(g):
            g.wait_ge(ctx.sX.h, snd_x[1])
            g.collective_compute(
                "AllGather", ALU.bypass, replica_groups=cc["rg"],
                ins=[cc["in"].ap().opt()], outs=[cc["out"].ap().opt()],
            ).then_inc(ctx.sG.h, 1)
            g.dma_start(out=bp_row[:, :], in_=io["l1a_bp_row"]).then_inc(ctx.sP.h, 16)
            g.wait_ge(ctx.sG.h, cc_done[0])
            for rk in range(2):
                g.dma_start(
                    out=ln4T[:, :, rk * 2048:(rk + 1) * 2048],
                    in_=cc["out"].ap()[rk * 1024:(rk + 1) * 1024, :]
                    .rearrange("(dc p) t -> p dc t", p=128),
                ).then_inc(ctx.sW.h, 16)

        b.gpsimd(g_cc)

        st = dict(ln4T=ln4T, qkvgT=qkvgT, wbuf=wbuf, ksum=ksum, b_sb=b_sb,
                  M_sb=M_sb, z_sb=z_sb, zt_sb=zt_sb, t1f=t1f, qtok=qtok,
                  ktok=ktok, fin=fin, ftok=ftok, bounce=cc["bounce"],
                  kv_x=kv_x)
        prev = {"v": None, "t": None, "x": None}
        for pq in range(4):
            _l1attn_pair(ctx, r, io, pq, st, prev)

        _l1attn_proj(ctx, r, io, dict(flat=flat_sb, flatT=flatT_sb, wpT=wpT_sb,
                                      bp_row=bp_row, bounce=cc["bounce"],
                                      wp_dma=wp_dma, prev=prev))
        del wp_dma

    emit_barrier(ctx)


def _l1attn_pair(ctx, r, io, pq, s, prev):
    nc, b = ctx.nc, ctx.block
    from contextlib import ExitStack
    qkvgT = s["qkvgT"]

    wdma = plan(ctx.sW, 4, step=16)

    def g_w(g):
        if prev["t"] is not None:
            g.wait_ge(ctx.sT.h, prev["t"])
        for k in range(4):
            g.dma_start(out=s["wbuf"][:, :, k, :],
                        in_=io["l1a_wT"].rearrange("(dc p) (k c) -> p dc k c",
                                                   p=128, k=4)
                        [:, :, k, pq * 128:(pq + 1) * 128]
                        ).then_inc(ctx.sW.h, 16)

    b.gpsimd(g_w)

    pr_mm = plan(ctx.sT, 4 * 8)
    pr_ev = plan(ctx.sA, 4 * 8)
    sp_a = plan(ctx.sA, 8)
    sp_v = plan(ctx.sV, 8)

    with ExitStack() as es:
        ec = es.enter_context
        u_ps = [ec(nc.psum_tensor(f"u{i}_{pq}_{uid()}", [128, 512], F32))[:, :]
                for i in range(8)]

        def t_pr(te):
            te.wait_ge(ctx.sW.h, wdma[3])
            if prev["v"] is not None:
                te.wait_ge(ctx.sV.h, prev["v"])
            for kind in range(4):
                for dc in range(DC):
                    for tb in range(8):
                        i = kind * 8 + tb
                        if dc == 0 and kind >= 1:
                            te.wait_ge(ctx.sA.h, pr_ev[i - 8])
                        ins = te.matmul(
                            u_ps[tb], s["wbuf"][:, dc, kind, :],
                            s["ln4T"][:, dc, tb * 512:(tb + 1) * 512],
                            start=(dc == 0), stop=(dc == DC - 1))
                        if dc == DC - 1:
                            ins.then_inc(ctx.sT.h, 1)

        b.tensor(t_pr)

        def s_pr(sc):
            for kind in range(4):
                fn = AF.Identity if kind < 2 else AF.Gelu
                for tb in range(8):
                    i = kind * 8 + tb
                    sc.wait_ge(ctx.sT.h, pr_mm[i])
                    sc.activation(qkvgT[:, kind * 4096 + tb * 512:
                                        kind * 4096 + (tb + 1) * 512],
                                  u_ps[tb], fn,
                                  bias=s["b_sb"][:, pq * 4 + kind:pq * 4 + kind + 1]
                                  ).then_inc(ctx.sA.h, 1)
            for i8 in range(8):
                kq, qtr = i8 // 4, i8 % 4
                sl = qkvgT[:, kq * 4096 + qtr * 1024:kq * 4096 + (qtr + 1) * 1024]
                sc.wait_ge(ctx.sA.h, pr_ev[kq * 8 + qtr * 2 + 1])
                if i8 >= 1:
                    sc.wait_ge(ctx.sV.h, sp_v[i8 - 1])
                sc.activation(r["dump"][:, :], sl, AF.Abs)
                sc.drain()
                sc.activation(r["dump"][:, :], r["dump"][:, :], AF.Exp, scale=-1.0)
                sc.drain()
                sc.activation(r["dump"][:, :], r["dump"][:, :], AF.Ln, bias=1.0)
                sc.drain()
                sc.activation(r["ln_tok"][:, :], sl, AF.Relu
                              ).then_inc(ctx.sA.h, 1)

        b.scalar(s_pr)

        def v_pr(v):
            for i8 in range(8):
                kq, qtr = i8 // 4, i8 % 4
                sl = qkvgT[:, kq * 4096 + qtr * 1024:kq * 4096 + (qtr + 1) * 1024]
                v.wait_ge(ctx.sA.h, sp_a[i8])
                v.tensor_tensor(sl, r["ln_tok"][:, :], r["dump"][:, :],
                                ALU.add).then_inc(ctx.sV.h, 1)

        b.vector(v_pr)

    _l1attn_pair_core(ctx, r, io, pq, s, sp_v[-1], prev)


def _l1attn_pair_core(ctx, r, io, pq, s, sp_last, prev):
    nc, b = ctx.nc, ctx.block
    from contextlib import ExitStack
    qkvgT = s["qkvgT"]

    tr_mm = plan(ctx.sT, 64)
    tr_ev = []
    for i in range(64):
        if i < 32:
            tr_ev.append(plan(ctx.sV, 1)[-1])
        else:
            tr_ev.append(plan(ctx.sV, 3)[-1])
    mz_mm = plan(ctx.sT, 1)
    o1_mm = plan(ctx.sT, 8)
    ft_mm = plan(ctx.sT, 32)
    mz_v = plan(ctx.sV, 2)
    fin_v = plan(ctx.sV, 8 * 2)
    ft_v = plan(ctx.sV, 32)
    ft_x = plan(ctx.sX, 32, step=16)

    from contextlib import ExitStack as _ES
    esA = _ES()
    t_ps = [esA.enter_context(nc.psum_tensor(f"lt{i}_{pq}_{uid()}",
                                             [128, 128], BF16))[:, :]
            for i in range(2)]

    def t_tr(te):
        te.wait_ge(ctx.sV.h, sp_last)
        for kq in range(2):
            for tc in range(32):
                i = kq * 32 + tc
                if i >= 2:
                    te.wait_ge(ctx.sV.h, tr_ev[i - 2])
                te.transpose(t_ps[i % 2],
                             qkvgT[:, kq * 4096 + tc * 128:
                                   kq * 4096 + (tc + 1) * 128],
                             r["ident"][:, :]).then_inc(ctx.sT.h, 1)

    b.tensor(t_tr)

    def v_tr(v):
        for kq in range(2):
            for tc in range(32):
                i = kq * 32 + tc
                v.wait_ge(ctx.sT.h, tr_mm[i])
                dst = s["qtok"] if kq == 0 else s["ktok"]
                v.tensor_copy(dst[:, tc * 128:(tc + 1) * 128],
                              t_ps[i % 2]).then_inc(ctx.sV.h, 1)
                if kq == 1:
                    with nc.allow_low_precision(reason="bf16 ksum operand"):
                        for hl in range(2):
                            v.reduce_sum(s["ksum"][:, tc * 2 + hl:tc * 2 + hl + 1],
                                         t_ps[i % 2][:, hl * 64:(hl + 1) * 64],
                                         axis=AX.X).then_inc(ctx.sV.h, 1)

    b.vector(v_tr)
    esA.close()

    esB = _ES()
    M_ps = [esB.enter_context(nc.psum_tensor(f"lm{hl}_{pq}_{uid()}",
                                             [128, 64], F32))[:, :]
            for hl in range(2)]
    zd_ps = [esB.enter_context(nc.psum_tensor(f"lzd{hl}_{pq}_{uid()}",
                                              [128, 1], F32))[:, :]
             for hl in range(2)]

    def t_m(te):
        te.wait_ge(ctx.sV.h, tr_ev[63])
        for tc in range(32):
            for hl in range(2):
                tpos = (0, 64) if hl == 1 else None
                te.matmul(M_ps[hl][hl * 64:(hl + 1) * 64, :],
                          s["ktok"][:, tc * 128 + hl * 64:tc * 128 + hl * 64 + 64],
                          s["qtok"][:, tc * 128 + hl * 64:tc * 128 + hl * 64 + 64],
                          start=(tc == 0), stop=(tc == 31), tile_position=tpos)
                ins = te.matmul(zd_ps[hl][hl * 64:(hl + 1) * 64, :],
                                s["qtok"][:, tc * 128 + hl * 64:
                                          tc * 128 + hl * 64 + 64],
                                s["ksum"][:, tc * 2 + hl:tc * 2 + hl + 1],
                                start=(tc == 0), stop=(tc == 31),
                                tile_position=tpos)
        ins.then_inc(ctx.sT.h, 1)

    b.tensor(t_m)

    def v_m(v):
        v.wait_ge(ctx.sT.h, mz_mm[0])
        for hl in range(2):
            sl = slice(hl * 64, (hl + 1) * 64)
            v.tensor_copy(s["M_sb"][sl, :], M_ps[hl][sl, :])
            v.tensor_scalar(s["zt_sb"][sl, :], zd_ps[hl][sl, :], L1_SCALE,
                            float(D), ALU.mult, ALU.add)
        v.then_inc if False else None
        v.nop().then_inc(ctx.sV.h, 1)
        v.drain()
        with nc.allow_low_precision(reason="z gate reciprocal"):
            v.reciprocal(s["z_sb"][:, :], s["zt_sb"][:, :]).then_inc(ctx.sV.h, 1)

    b.vector(v_m)
    esB.close()

    esC = _ES()
    o1_ps = [[esC.enter_context(nc.psum_tensor(f"lo{i}{hl}_{pq}_{uid()}",
                                               [128, 512], F32))[:, :]
              for hl in range(2)] for i in range(2)]
    t_ps = [esC.enter_context(nc.psum_tensor(f"lt2{i}_{pq}_{uid()}",
                                             [128, 128], BF16))[:, :]
            for i in range(2)]

    def t_o(te):
        te.wait_ge(ctx.sV.h, mz_v[1])
        for tb in range(8):
            if tb >= 2:
                te.wait_ge(ctx.sV.h, fin_v[(tb - 2) * 2 + 1])
            for hl in range(2):
                tpos = (64, 64) if hl == 1 else None
                ins = te.matmul(
                    o1_ps[tb % 2][hl][hl * 64:(hl + 1) * 64, :],
                    s["M_sb"][hl * 64:(hl + 1) * 64, :],
                    qkvgT[hl * 64:(hl + 1) * 64,
                          2 * 4096 + tb * 512:2 * 4096 + (tb + 1) * 512],
                    start=True, stop=True, tile_position=tpos)
            ins.then_inc(ctx.sT.h, 1)
        for tc in range(32):
            te.wait_ge(ctx.sV.h, fin_v[(tc // 4) * 2 + 1])
            if tc >= 2:
                te.wait_ge(ctx.sV.h, ft_v[tc - 2])
            te.transpose(t_ps[tc % 2],
                         s["fin"][:, tc * 128:(tc + 1) * 128],
                         r["ident"][:, :]).then_inc(ctx.sT.h, 1)

    b.tensor(t_o)

    def v_o(v):
        for tb in range(8):
            v.wait_ge(ctx.sT.h, o1_mm[tb])
            for hl in range(2):
                sl = slice(hl * 64, (hl + 1) * 64)
                v.scalar_tensor_tensor(
                    s["t1f"][sl, :], o1_ps[tb % 2][hl][sl, :], L1_SCALE,
                    qkvgT[sl, 2 * 4096 + tb * 512:2 * 4096 + (tb + 1) * 512],
                    ALU.mult, ALU.add)
            v.nop().then_inc(ctx.sV.h, 1)
            v.drain()
            v.scalar_tensor_tensor(
                s["fin"][:, tb * 512:(tb + 1) * 512], s["t1f"][:, :],
                s["z_sb"][:, 0:1],
                qkvgT[:, 3 * 4096 + tb * 512:3 * 4096 + (tb + 1) * 512],
                ALU.mult, ALU.mult).then_inc(ctx.sV.h, 1)
        for tc in range(32):
            v.wait_ge(ctx.sT.h, ft_mm[tc])
            if tc >= 2:
                v.wait_ge(ctx.sX.h, ft_x[tc - 2])
            v.tensor_copy(s["ftok"][:, (tc % 2) * 128:(tc % 2) * 128 + 128],
                          t_ps[tc % 2]).then_inc(ctx.sV.h, 1)

    b.vector(v_o)

    def q_h(g):
        for tc in range(32):
            g.wait_ge(ctx.sV.h, ft_v[tc])
            g.dma_start(
                out=s["bounce"].ap()
                .rearrange("(hh t j) -> hh t j", hh=8, t=TK)
                [pq * 2:pq * 2 + 2, tc * 128:(tc + 1) * 128, :]
                .transpose([1, 0, 2]),
                in_=s["ftok"][:, (tc % 2) * 128:(tc % 2) * 128 + 128]
                .rearrange("p (hh j) -> p hh j", hh=2),
            ).then_inc(ctx.sX.h, 16)

    b.sync(q_h)
    esC.close()


    prev["v"] = ft_v[-1]
    prev["t"] = ft_mm[-1]
    prev["x"] = ft_x[-1]


def _l1attn_proj(ctx, r, io, s):
    nc, b = ctx.nc, ctx.block
    from contextlib import ExitStack
    x = r["x"]
    flat, flatT = s["flat"], s["flatT"]
    prev = s["prev"]

    wpl = plan(ctx.sP, 1, step=16)
    for half in range(2):
        fl_x = plan(ctx.sW, 1, step=16)
        fT_mm = plan(ctx.sT, 64)
        fT_v = plan(ctx.sV, 64)
        pj_mm = plan(ctx.sT, 8)
        pj_v = plan(ctx.sV, 8)

        def g_fl(g, half=half, fl_x=fl_x):
            if half == 0:
                g.wait_ge(ctx.sX.h, prev["x"])
                g.wait_ge(ctx.sV.h, prev["v"])
                g.dma_start(out=s["wpT"],
                            in_=io["l1a_wpT"].rearrange("(tp p) n -> p tp n", p=128)
                            ).then_inc(ctx.sP.h, 16)
            else:
                g.wait_ge(ctx.sV.h, prev["pj"])
            g.dma_start(
                out=flat[:, :].rearrange("p (t d) -> p t d", t=8),
                in_=s["bounce"].ap()
                .rearrange("(n p d) -> p n d", p=128, d=D)
                [:, half * 8:(half + 1) * 8, :],
            ).then_inc(ctx.sW.h, 16)

        b.gpsimd(g_fl)

        with ExitStack() as es:
            ec = es.enter_context
            t_ps = [ec(nc.psum_tensor(f"pt{i}_{half}_{uid()}", [128, 128], BF16))[:, :]
                    for i in range(2)]
            p_ps = [ec(nc.psum_tensor(f"pp{i}_{half}_{uid()}", [128, 1024], F32))[:, :]
                    for i in range(2)]

            def t_f(te, half=half, fl_x=fl_x, fT_mm=fT_mm, fT_v=fT_v,
                    pj_mm=pj_mm, pj_v=pj_v):
                te.wait_ge(ctx.sW.h, fl_x[0])
                for i in range(64):
                    if i >= 2:
                        te.wait_ge(ctx.sV.h, fT_v[i - 2])
                    tt, dc = i // 8, i % 8
                    te.transpose(t_ps[i % 2],
                                 flat[:, tt * 1024 + dc * 128:
                                      tt * 1024 + (dc + 1) * 128],
                                 r["ident"][:, :]).then_inc(ctx.sT.h, 1)
                te.wait_ge(ctx.sP.h, wpl[0])
                for tt in range(8):
                    te.wait_ge(ctx.sV.h, fT_v[tt * 8 + 7])
                    if tt >= 2:
                        te.wait_ge(ctx.sV.h, pj_v[tt - 2])
                    for dc in range(DC):
                        for nb in range(2):
                            te.matmul(p_ps[tt % 2][:, nb * 512:(nb + 1) * 512],
                                      flatT[:, dc * 1024 + tt * 128:
                                            dc * 1024 + (tt + 1) * 128],
                                      s["wpT"][:, dc, nb * 512:(nb + 1) * 512],
                                      start=(dc == 0), stop=False)
                    for nb in range(2):
                        ins = te.matmul(p_ps[tt % 2][:, nb * 512:(nb + 1) * 512],
                                        r["ones"][0:1, 0:128],
                                        s["bp_row"][0:1, nb * 512:(nb + 1) * 512],
                                        start=False, stop=True)
                    ins.then_inc(ctx.sT.h, 1)

            b.tensor(t_f)

            def v_f(v, half=half, fT_mm=fT_mm, fT_v=fT_v, pj_mm=pj_mm, pj_v=pj_v):
                for i in range(64):
                    v.wait_ge(ctx.sT.h, fT_mm[i])
                    tt, dc = i // 8, i % 8
                    v.tensor_copy(flatT[:, dc * 1024 + tt * 128:
                                        dc * 1024 + (tt + 1) * 128],
                                  t_ps[i % 2]).then_inc(ctx.sV.h, 1)
                for tt in range(8):
                    g = half * 8 + tt
                    v.wait_ge(ctx.sT.h, pj_mm[tt])
                    v.tensor_add(x[:, g * D:(g + 1) * D], x[:, g * D:(g + 1) * D],
                                 p_ps[tt % 2]).then_inc(ctx.sV.h, 1)

            b.vector(v_f)

        prev["pj"] = pj_v[-1]



PHASESETS = {
    "l0ffn": ["l0ffn"],
    "l0attn": ["l0attn"],
    "l1ffn": ["l1ffn"],
    "l1attn": ["l1attn"],
}


def build_program(phases, replica_groups=RG):
    nc = bass.Bass("TRN2", target_bir_lowering=False, detect_race_conditions=False)
    io = {}
    io["x_in"] = nc.dram_tensor("x_in", [T, D], F32, kind="ExternalInput").ap()
    io["y"] = nc.dram_tensor("y", [T, D], F32, kind="ExternalOutput").ap()
    need = set(phases)
    if "l0ffn" in need:
        io["l0_rw"] = nc.dram_tensor("l0_rw", [D, 9], F32, kind="ExternalInput").ap()
        io["l0_rb_row"] = nc.dram_tensor("l0_rb_row", [1, 9], F32, kind="ExternalInput").ap()
        io["l0_w1"] = nc.dram_tensor("l0_w1", [E0, D, F0], F32, kind="ExternalInput").ap()
        io["l0_w2"] = nc.dram_tensor("l0_w2", [E0, F0, D], F32, kind="ExternalInput").ap()
    if "l0attn" in need:
        for nme in ["l0_wqT", "l0_wkT", "l0_wvT", "l0_woT"]:
            io[nme] = nc.dram_tensor(nme, [D, D], F32, kind="ExternalInput").ap()
        for nme in ["l0_bq", "l0_bk", "l0_bv"]:
            io[nme] = nc.dram_tensor(nme, [128, 8], F32, kind="ExternalInput").ap()
        io["l0_bo_row"] = nc.dram_tensor("l0_bo_row", [1, D], F32, kind="ExternalInput").ap()
    if "l1ffn" in need:
        io["l1f_gwT"] = nc.dram_tensor("l1f_gwT", [D, 16], F32, kind="ExternalInput").ap()
        io["l1f_gb_row"] = nc.dram_tensor("l1f_gb_row", [1, 16], F32, kind="ExternalInput").ap()
        io["l1f_wgT"] = nc.dram_tensor("l1f_wgT", [E1, D, I1], F32, kind="ExternalInput").ap()
        io["l1f_wuT"] = nc.dram_tensor("l1f_wuT", [E1, D, I1], F32, kind="ExternalInput").ap()
        io["l1f_wdT"] = nc.dram_tensor("l1f_wdT", [E1, I1, D], F32, kind="ExternalInput").ap()
        io["l1f_wsgT"] = nc.dram_tensor("l1f_wsgT", [D, SH], F32, kind="ExternalInput").ap()
        io["l1f_wsuT"] = nc.dram_tensor("l1f_wsuT", [D, SH], F32, kind="ExternalInput").ap()
        io["l1f_wsdT"] = nc.dram_tensor("l1f_wsdT", [SH, D], F32, kind="ExternalInput").ap()
    if "l1attn" in need:
        io["l1a_wT"] = nc.dram_tensor("l1a_wT", [D, 2048], F32, kind="ExternalInput").ap()
        io["l1a_b"] = nc.dram_tensor("l1a_b", [128, 16], F32, kind="ExternalInput").ap()
        io["l1a_wpT"] = nc.dram_tensor("l1a_wpT", [D, D], F32, kind="ExternalInput").ap()
        io["l1a_bp_row"] = nc.dram_tensor("l1a_bp_row", [1, D], F32, kind="ExternalInput").ap()
    n_cc = sum(1 for p in phases if p in ("l0attn", "l1attn"))
    cc_bufs = []
    for i in range(n_cc):
        cc_bufs.append({
            "in": nc.dram_tensor(f"cc_in{i}", [D, T], BF16, kind="Internal"),
            "out": nc.dram_tensor(f"cc_out{i}", [2 * D, T], BF16, kind="Internal"),
            "bounce": nc.dram_tensor(f"bounce{i}", [8 * TK * HD], BF16, kind="Internal"),
            "rg": replica_groups,
        })
    cc_idx = [0]

    ident_np = np.eye(128, dtype=ml_dtypes.bfloat16)
    ones_np = np.ones((1, 128), dtype=ml_dtypes.bfloat16)

    from contextlib import ExitStack
    with ExitStack() as es:
        ec = es.enter_context
        x_sb = ec(nc.sbuf_tensor("x_sb", [128, NT * D], F32))
        mean = ec(nc.sbuf_tensor("mean", [128, NT], F32))
        rstd = ec(nc.sbuf_tensor("rstd", [128, NT], F32))
        var = ec(nc.sbuf_tensor("var", [128, NT], F32))
        s1 = ec(nc.sbuf_tensor("s1", [128, NT], F32))
        s2 = ec(nc.sbuf_tensor("s2", [128, NT], F32))
        dump = ec(nc.sbuf_tensor("dump", [128, 1024], F32))
        ln_tok = ec(nc.sbuf_tensor("ln_tok", [128, 1024], BF16))
        tc = ec(nc.sbuf_tensor("tc", [128, 1], F32))
        tc2 = ec(nc.sbuf_tensor("tc2", [128, 1], F32))
        m9 = ec(nc.sbuf_tensor("m9", [128, 16], F32))
        coef = ec(nc.sbuf_tensor("coef", [128, NT * 16], F32))
        ident = ec(nc.sbuf_tensor("ident", [128, 128], BF16))
        ones = ec(nc.sbuf_tensor("ones", [1, 128], BF16))
        hT = ec(nc.semaphore("sT"))
        hV = ec(nc.semaphore("sV"))
        hA = ec(nc.semaphore("sA"))
        hG = ec(nc.semaphore("sG"))
        hW = ec(nc.semaphore("sW"))
        hX = ec(nc.semaphore("sX"))
        hW0 = ec(nc.semaphore("sW0"))
        hW1 = ec(nc.semaphore("sW1"))
        hW2 = ec(nc.semaphore("sW2"))
        hW3 = ec(nc.semaphore("sW3"))
        hP = ec(nc.semaphore("sP"))
        block = ec(nc.Block())
        sems = dict(sT=Sem(hT), sV=Sem(hV), sA=Sem(hA), sG=Sem(hG),
                    sW=Sem(hW), sX=Sem(hX), sW0=Sem(hW0), sW1=Sem(hW1),
                    sW2=Sem(hW2), sW3=Sem(hW3), sP=Sem(hP))
        ctx = Ctx(nc, block, sems)
        ident_dram = nc.inline_tensor(ident_np, "ident_const").ap()
        ones_dram = nc.inline_tensor(ones_np, "ones_const").ap()

        r = dict(x=x_sb[:, :], mean=mean[:, :], rstd=rstd[:, :], var=var[:, :],
                 s1=s1[:, :], s2=s2[:, :], dump=dump[:, :],
                 ln_tok=ln_tok[:, :], tc=tc[:, :], tc2=tc2[:, :], m9=m9[:, :],
                 coef=coef[:, :], ident=ident[:, :], ones=ones[:, :])

        ctx.sX.inc(48)

        def g_load(g):
            g.dma_start(out=x_sb[:, :].rearrange("p (t d) -> p t d", t=NT),
                        in_=io["x_in"].rearrange("(t p) d -> p t d", p=128)
                        ).then_inc(ctx.sX.h, 16)
            g.dma_start(out=ident[:, :], in_=ident_dram).then_inc(ctx.sX.h, 16)
            g.dma_start(out=ones[:, :], in_=ones_dram).then_inc(ctx.sX.h, 16)

        block.sync(g_load)
        emit_barrier(ctx)

        for ph in phases:
            if ph == "l0ffn":
                emit_l0ffn(ctx, r, io)
            elif ph == "l0attn":
                emit_l0attn(ctx, r, io, cc_bufs[cc_idx[0]])
                cc_idx[0] += 1
            elif ph == "l1ffn":
                emit_l1ffn(ctx, r, io)
            elif ph == "l1attn":
                emit_l1attn(ctx, r, io, cc_bufs[cc_idx[0]])
                cc_idx[0] += 1
            elif ph == "nop":
                pass
            elif ph == "lnonly":
                with nc.sbuf_tensor("lnT_dbg", [128, DC * T], BF16) as lnT_dbg:
                    emit_ln(ctx, r, lnT_dbg[:, :])
                    dbg = nc.dram_tensor("dbg", [128, DC * T], F32,
                                         kind="ExternalOutput").ap()
                    with nc.sbuf_tensor("dbg_f", [128, 2048], F32) as dbg_f:
                        dbg_v = plan(ctx.sV, 8)
                        dbg_x = plan(ctx.sX, 8, step=16)

                        def v_dbg(v):
                            for i in range(8):
                                if i >= 1:
                                    v.wait_ge(ctx.sX.h, dbg_x[i - 1])
                                v.tensor_copy(dbg_f[:, :],
                                              lnT_dbg[:, i * 2048:(i + 1) * 2048]
                                              ).then_inc(ctx.sV.h, 1)

                        def q_dbg(g):
                            for i in range(8):
                                g.wait_ge(ctx.sV.h, dbg_v[i])
                                g.dma_start(out=dbg[:, i * 2048:(i + 1) * 2048],
                                            in_=dbg_f[:, :]).then_inc(ctx.sX.h, 16)

                        ctx.block.vector(v_dbg)
                        ctx.block.sync(q_dbg)
                        emit_barrier(ctx)
                dbg2 = nc.dram_tensor("dbg2", [128, 5 * NT], F32,
                                      kind="ExternalOutput").ap()
                st_x = plan(ctx.sX, 1, step=16)

                def q_dbg2(g):
                    g.dma_start(out=dbg2[:, 0 * NT:1 * NT], in_=s1_t).then_inc(ctx.sX.h, 16)
                    g.dma_start(out=dbg2[:, 1 * NT:2 * NT], in_=s2_t).then_inc(ctx.sX.h, 16)
                    g.dma_start(out=dbg2[:, 2 * NT:3 * NT], in_=mean_t).then_inc(ctx.sX.h, 16)
                    g.dma_start(out=dbg2[:, 3 * NT:4 * NT], in_=var_t).then_inc(ctx.sX.h, 16)
                    g.dma_start(out=dbg2[:, 4 * NT:5 * NT], in_=rstd_t).then_inc(ctx.sX.h, 16)

                ctx.sX.inc(4 * 16)
                s1_t, s2_t = r["s1"], r["s2"]
                mean_t, var_t, rstd_t = r["mean"], r["var"], r["rstd"]
                ctx.block.sync(q_dbg2)
                emit_barrier(ctx)
            else:
                raise ValueError(ph)

        ctx.sX.inc(16)

        def g_store(g):
            g.dma_start(out=io["y"].rearrange("(t p) d -> p t d", p=128),
                        in_=x_sb[:, :].rearrange("p (t d) -> p t d", t=NT)
                        ).then_inc(ctx.sX.h, 16)
            g.wait_ge(ctx.sX.h, ctx.sX.v)

        block.sync(g_store)

    return nc



def prep_inputs(inputs):
    f32 = np.float32
    w = {}
    w["l0_rw"] = np.ascontiguousarray(inputs["L0_ffn_router_w"], dtype=f32)
    w["l0_rb_row"] = np.ascontiguousarray(inputs["L0_ffn_router_b"], dtype=f32).reshape(1, 9)
    w["l0_w1"] = np.ascontiguousarray(inputs["L0_ffn_w1"], dtype=f32)
    w["l0_w2"] = np.ascontiguousarray(inputs["L0_ffn_w2"], dtype=f32)
    for nme, key in [("l0_wqT", "L0_attn_Wq"), ("l0_wkT", "L0_attn_Wk"),
                     ("l0_wvT", "L0_attn_Wv"), ("l0_woT", "L0_attn_Wo")]:
        w[nme] = np.ascontiguousarray(np.asarray(inputs[key], dtype=f32).T)
    for nme, key in [("l0_bq", "L0_attn_bq"), ("l0_bk", "L0_attn_bk"),
                     ("l0_bv", "L0_attn_bv")]:
        w[nme] = np.ascontiguousarray(
            np.asarray(inputs[key], dtype=f32).reshape(8, 128).T)
    w["l0_bo_row"] = np.ascontiguousarray(inputs["L0_attn_bo"], dtype=f32).reshape(1, D)
    w["l1f_gwT"] = np.ascontiguousarray(np.asarray(inputs["L1_ffn_gate_w"], dtype=f32).T)
    w["l1f_gb_row"] = np.ascontiguousarray(inputs["L1_ffn_gate_bias"], dtype=f32).reshape(1, 16)
    w["l1f_wgT"] = np.ascontiguousarray(np.asarray(inputs["L1_ffn_Wg"], dtype=f32).transpose(0, 2, 1))
    w["l1f_wuT"] = np.ascontiguousarray(np.asarray(inputs["L1_ffn_Wu"], dtype=f32).transpose(0, 2, 1))
    w["l1f_wdT"] = np.ascontiguousarray(np.asarray(inputs["L1_ffn_Wd"], dtype=f32).transpose(0, 2, 1))
    w["l1f_wsgT"] = np.ascontiguousarray(np.asarray(inputs["L1_ffn_Ws_g"], dtype=f32).T)
    w["l1f_wsuT"] = np.ascontiguousarray(np.asarray(inputs["L1_ffn_Ws_u"], dtype=f32).T)
    w["l1f_wsdT"] = np.ascontiguousarray(np.asarray(inputs["L1_ffn_Ws_d"], dtype=f32).T)

    w["l1a_wpT"] = np.ascontiguousarray(np.asarray(inputs["L1_attn_w_proj"], dtype=f32).T)
    w["l1a_bp_row"] = np.ascontiguousarray(inputs["L1_attn_b_proj"], dtype=f32).reshape(1, D)
    wq4 = np.asarray(inputs["L1_attn_w_qkvg"], dtype=f32)
    bq4 = np.asarray(inputs["L1_attn_b_qkvg"], dtype=f32)

    x = np.asarray(inputs["L0_attn_x"], dtype=f32)
    per_core = []
    for c in range(NCORES):
        bidx, h = c // 2, c % 2
        m = dict(w)
        m["x_in"] = np.ascontiguousarray(x[bidx, h * T:(h + 1) * T, :])
        rows = np.empty((4, 4, 2, 64), dtype=np.int64)
        for k in range(4):
            for pq in range(4):
                for hl in range(2):
                    hh = h * 8 + pq * 2 + hl
                    rows[k, pq, hl] = k * 1024 + hh * 64 + np.arange(64)
        ridx = rows.reshape(-1)
        m["l1a_wT"] = np.ascontiguousarray(wq4[ridx].T)
        m["l1a_b"] = np.ascontiguousarray(
            bq4[ridx].reshape(4, 4, 2 * 64).transpose(2, 1, 0).reshape(128, 16))
        per_core.append(m)
    return per_core


def launch_keys(phases):
    keys = ["x_in"]
    if "l0ffn" in phases:
        keys += ["l0_rw", "l0_rb_row", "l0_w1", "l0_w2"]
    if "l0attn" in phases:
        keys += ["l0_wqT", "l0_wkT", "l0_wvT", "l0_woT",
                 "l0_bq", "l0_bk", "l0_bv", "l0_bo_row"]
    if "l1ffn" in phases:
        keys += ["l1f_gwT", "l1f_gb_row", "l1f_wgT", "l1f_wuT", "l1f_wdT",
                 "l1f_wsgT", "l1f_wsuT", "l1f_wsdT"]
    if "l1attn" in phases:
        keys += ["l1a_wT", "l1a_b", "l1a_wpT", "l1a_bp_row"]
    return keys


def run_launch(nc, per_core, keys):
    in_maps = [{k: m[k] for k in keys} for m in per_core]
    res = run_bass_kernel_spmd(nc, in_maps, core_ids=list(range(NCORES)))
    return [r["y"] for r in res.results]


FULL_PHASES = ["l0ffn", "l0attn", "l0ffn", "l1ffn", "l1attn", "l1ffn"]


def kernel(**inputs):
    per_core = prep_inputs(inputs)
    nc = build_program(FULL_PHASES)
    ys = run_launch(nc, per_core, launch_keys(FULL_PHASES))
    out = np.empty((4, TK, D), dtype=np.float32)
    for c in range(NCORES):
        bidx, h = c // 2, c % 2
        out[bidx, h * T:(h + 1) * T, :] = ys[c]
    return out


# version 05
# speedup vs baseline: 1.1253x; 1.0283x over previous
import numpy as np
import ml_dtypes
import concourse.bass as bass
import concourse.mybir as mybir
from concourse.bass_utils import run_bass_kernel_spmd

F32 = mybir.dt.float32
BF16 = mybir.dt.bfloat16
AF = mybir.ActivationFunctionType
ALU = mybir.AluOpType
AX = mybir.AxisListType

NCORES = 8
D = 1024
DC = 8
T = 2048
NT = 16
TK = 4096
NTK = 32
NH = 16
HD = 64
E0 = 8
F0 = 4096
E1 = 16
I1 = 512
SH = 1024
EPS = 1e-5
ISQRT_D = 1.0 / 32.0
L1_SCALE = HD ** -0.5
RG = [[0, 1], [2, 3], [4, 5], [6, 7]]

NEG = -1.0e30
DEBUG_ATTN = False

_UID = [0]


def uid():
    _UID[0] += 1
    return _UID[0]


class Sem:
    def __init__(self, h):
        self.h = h
        self.v = 0

    def inc(self, n=1):
        self.v += n
        return self.v


def plan(sem, n, step=1):
    base = sem.v
    sem.v += n * step
    return [base + (i + 1) * step for i in range(n)]


class Ctx:

    def __init__(self, nc, block, sems):
        self.nc = nc
        self.block = block
        for k, v in sems.items():
            setattr(self, k, v)

    def barrier_vals(self):
        return [(s, s.v) for s in (self.sT, self.sV, self.sA, self.sG, self.sW,
                                   self.sX, self.sW0, self.sW1, self.sW2,
                                   self.sW3, self.sP)]


def emit_barrier(ctx):
    vals = ctx.barrier_vals()
    b = ctx.block

    def body(eng):
        for s, v in vals:
            if v > 0:
                eng.wait_ge(s.h, v)

    b.tensor(body)
    b.vector(body)
    b.scalar(body)
    b.gpsimd(body)
    b.sync(body)



def emit_ln(ctx, r, lnT):
    nc, b = ctx.nc, ctx.block
    with (nc.psum_tensor(f"tr_ps0_{uid()}", [128, 128], BF16) as tr_ps0,
          nc.psum_tensor(f"tr_ps1_{uid()}", [128, 128], BF16) as tr_ps1):
        _emit_ln_inner(ctx, r, lnT, [tr_ps0[:, :], tr_ps1[:, :]])
    emit_barrier(ctx)


def _emit_ln_inner(ctx, r, lnT, tr_ps):
    nc, b = ctx.nc, ctx.block
    r = dict(r)
    r["tr_ps"] = tr_ps
    x = r["x"]
    mean, rstd = r["mean"], r["rstd"]
    sq_a = plan(ctx.sA, NT)
    rs_a = plan(ctx.sA, NT)
    ev_a = plan(ctx.sA, NT * DC)
    st_v = plan(ctx.sV, NT)
    ln_v = plan(ctx.sV, NT)
    tr_t = plan(ctx.sT, NT * DC)

    def scalar1(sc):
        for t in range(NT):
            sc.activation(r["dump"][:, :], x[:, t * D:(t + 1) * D], AF.Square,
                          accum_out=r["s2"][:, t:t + 1]).then_inc(ctx.sA.h, 1)
        for t in range(NT):
            sc.wait_ge(ctx.sV.h, st_v[t])
            sc.activation(r["tc2"][:, 0:1], r["var"][:, t:t + 1], AF.Ln)
            sc.drain()
            sc.activation(rstd[:, t:t + 1], r["tc2"][:, 0:1], AF.Exp,
                          scale=-0.5).then_inc(ctx.sA.h, 1)
        for t in range(NT):
            for dc in range(DC):
                idx = t * DC + dc
                sc.wait_ge(ctx.sT.h, tr_t[idx])
                sc.copy(lnT[:, dc * T + t * 128: dc * T + (t + 1) * 128],
                        r["tr_ps"][idx % 2][:, :]).then_inc(ctx.sA.h, 1)

    b.scalar(scalar1)

    def vector1(v):
        for t in range(NT):
            v.wait_ge(ctx.sA.h, sq_a[t])
            v.reduce_sum(r["s1"][:, t:t + 1], x[:, t * D:(t + 1) * D], axis=AX.X)
            v.drain()
            v.tensor_scalar_mul(mean[:, t:t + 1], r["s1"][:, t:t + 1], 1.0 / D)
            v.drain()
            v.tensor_mul(r["tc"][:, 0:1], mean[:, t:t + 1], mean[:, t:t + 1])
            v.tensor_scalar(r["var"][:, t:t + 1], r["s2"][:, t:t + 1], 1.0 / D,
                            EPS, ALU.mult, ALU.add)
            v.drain()
            v.tensor_sub(r["var"][:, t:t + 1], r["var"][:, t:t + 1],
                         r["tc"][:, 0:1]).then_inc(ctx.sV.h, 1)
        for t in range(NT):
            v.wait_ge(ctx.sA.h, rs_a[t])
            if t >= 1:
                v.wait_ge(ctx.sT.h, tr_t[8 * t - 1])
            v.tensor_scalar(r["ln_tok"][:, 0:D], x[:, t * D:(t + 1) * D],
                            mean[:, t:t + 1], rstd[:, t:t + 1],
                            ALU.subtract, ALU.mult).then_inc(ctx.sV.h, 1)

    b.vector(vector1)

    def tensor1(te):
        for t in range(NT):
            te.wait_ge(ctx.sV.h, ln_v[t])
            for dc in range(DC):
                idx = t * DC + dc
                if idx >= 2:
                    te.wait_ge(ctx.sA.h, ev_a[idx - 2])
                te.transpose(r["tr_ps"][idx % 2][:, :],
                             r["ln_tok"][:, dc * 128:(dc + 1) * 128],
                             r["ident"][:, :]).then_inc(ctx.sT.h, 1)

    b.tensor(tensor1)



def emit_l0ffn(ctx, r, io):
    nc, b = ctx.nc, ctx.block
    x = r["x"]

    from contextlib import ExitStack
    with ExitStack() as es:
        ec = es.enter_context
        arena = ec(nc.sbuf_tensor(f"ffn0_arena_{uid()}", [128, (DC * T) + 32 * 1024], BF16))
        w1buf = ec(nc.sbuf_tensor(f"w1buf_{uid()}", [128, 2, DC, 128], BF16))
        w2buf = ec(nc.sbuf_tensor(f"w2buf_{uid()}", [128, 2, 8, 1024], BF16))
        rw_sb = ec(nc.sbuf_tensor(f"rw_sb_{uid()}", [128, DC, 9], BF16))
        rb_sb = ec(nc.sbuf_tensor(f"rb_sb_{uid()}", [1, 9], BF16))
        probs = ec(nc.sbuf_tensor(f"probs_{uid()}", [128, 16], F32))
        lnT = arena[:, 0:DC * T]
        h_sb = arena[:, DC * T:DC * T + 32 * 1024]

        emit_ln(ctx, r, lnT)
        router_es = ExitStack()
        r_ps = router_es.enter_context(nc.psum_tensor(f"r_ps_{uid()}", [128, 9], F32))

        coef = r["coef"]
        logit_sb = router_es.enter_context(
            nc.sbuf_tensor(f"logit_{uid()}", [128, NT * 9], F32))
        prob_sb = router_es.enter_context(
            nc.sbuf_tensor(f"prob_{uid()}", [128, NT * 9], F32))
        m16a = router_es.enter_context(nc.sbuf_tensor(f"m16a_{uid()}", [128, 16], F32))
        m16b = router_es.enter_context(nc.sbuf_tensor(f"m16b_{uid()}", [128, 16], F32))

        ctx.sP.inc(32)

        def g_pre(g):
            g.dma_start(out=rw_sb[:, :, :],
                        in_=io["l0_rw"].rearrange("(dc p) n -> p dc n", p=128)
                        ).then_inc(ctx.sP.h, 16)
            g.dma_start(out=rb_sb[:, :],
                        in_=io["l0_rb_row"]).then_inc(ctx.sP.h, 16)

        b.gpsimd(g_pre)
        w_pre = ctx.sP.v

        rt_t = plan(ctx.sT, NT)
        rt_ev = plan(ctx.sA, NT)
        exp_a = plan(ctx.sA, 1)
        sub_v = plan(ctx.sV, 1)
        gate_done = plan(ctx.sV, 1)
        skip_v = plan(ctx.sV, NT)

        def t_router(te):
            te.wait_ge(ctx.sP.h, w_pre)
            for t in range(NT):
                if t >= 1:
                    te.wait_ge(ctx.sA.h, rt_ev[t - 1])
                for dc in range(DC):
                    te.matmul(r_ps[:, :],
                              lnT[:, dc * T + t * 128: dc * T + (t + 1) * 128],
                              rw_sb[:, dc, :],
                              start=(dc == 0), stop=False)
                te.matmul(r_ps[:, :], r["ones"][0:1, 0:128], rb_sb[0:1, :],
                          start=False, stop=True).then_inc(ctx.sT.h, 1)

        b.tensor(t_router)

        def s_router(sc):
            for t in range(NT):
                sc.wait_ge(ctx.sT.h, rt_t[t])
                sc.copy(logit_sb[:, t * 9:(t + 1) * 9], r_ps[:, :]
                        ).then_inc(ctx.sA.h, 1)

        b.scalar(s_router)

        def s_exp(sc):
            sc.wait_ge(ctx.sV.h, sub_v[0])
            sc.activation(prob_sb[:, :], prob_sb[:, :], AF.Exp
                          ).then_inc(ctx.sA.h, 1)

        def v_gate(v):
            L3 = logit_sb[:, :].rearrange("p (t e) -> p t e", e=9)
            P = prob_sb[:, :]
            P3 = P.rearrange("p (t e) -> p t e", e=9)
            v.wait_ge(ctx.sA.h, rt_ev[NT - 1])
            v.reduce_max(m16a[:, :], L3, axis=AX.X)
            v.drain()
            v.tensor_tensor(P3, L3, m16a[:, :].rearrange("p t -> p t ()")
                            .broadcast_to([128, NT, 9]), ALU.subtract
                            ).then_inc(ctx.sV.h, 1)
            v.wait_ge(ctx.sA.h, exp_a[0])
            v.reduce_max(m16a[:, :], P3, axis=AX.X)
            v.drain()
            v.tensor_tensor(L3, P3, m16a[:, :].rearrange("p t -> p t ()")
                            .broadcast_to([128, NT, 9]), ALU.is_ge)
            v.drain()
            v.tensor_scalar(logit_sb[:, :], logit_sb[:, :], NEG, None,
                            ALU.mult, ALU.bypass)
            v.drain()
            v.tensor_add(logit_sb[:, :], logit_sb[:, :], P)
            v.drain()
            v.reduce_max(m16b[:, :], L3, axis=AX.X)
            v.drain()
            v.tensor_tensor(L3, P3, m16b[:, :].rearrange("p t -> p t ()")
                            .broadcast_to([128, NT, 9]), ALU.is_ge)
            v.drain()
            v.tensor_mul(logit_sb[:, :], logit_sb[:, :], P)
            v.reduce_sum(m16a[:, :], P3, axis=AX.X)
            v.drain()
            v.reciprocal(m16b[:, :], m16a[:, :])
            v.drain()
            v.tensor_tensor(L3, L3, m16b[:, :].rearrange("p t -> p t ()")
                            .broadcast_to([128, NT, 9]), ALU.mult)
            v.drain()
            cview = coef.rearrange("p (t e) -> p t e", e=16)[:, :, 0:9]
            v.tensor_scalar(cview, L3, 0.5, None, ALU.mult,
                            ALU.bypass).then_inc(ctx.sV.h, 1)
            v.drain()
            for t in range(NT):
                v.tensor_scalar(r["tc"][:, 0:1], coef[:, t * 16 + 8:t * 16 + 9],
                                r["rstd"][:, t:t + 1], None, ALU.mult, ALU.bypass)
                v.drain()
                v.tensor_scalar(r["dump"][:, 0:D], x[:, t * D:(t + 1) * D],
                                r["mean"][:, t:t + 1], r["tc"][:, 0:1],
                                ALU.subtract, ALU.mult)
                v.drain()
                v.tensor_add(x[:, t * D:(t + 1) * D], x[:, t * D:(t + 1) * D],
                             r["dump"][:, 0:D]).then_inc(ctx.sV.h, 1)

        b.vector(v_gate)
        b.scalar(s_exp)

        router_es.close()
        h_ps0 = ec(nc.psum_tensor(f"h_ps0_{uid()}", [128, 1024], F32))
        h_ps1 = ec(nc.psum_tensor(f"h_ps1_{uid()}", [128, 1024], F32))
        y_ps0 = ec(nc.psum_tensor(f"y_ps0_{uid()}", [128, 1024], F32))
        y_ps1 = ec(nc.psum_tensor(f"y_ps1_{uid()}", [128, 1024], F32))
        h_ps = [h_ps0, h_ps1]
        y_ps = [y_ps0, y_ps1]

        n_eh = E0 * 2
        w1_dma, w2_dma = [], []
        w1_sem = [ctx.sW0, ctx.sW1]
        w2_sem = [ctx.sW2, ctx.sW3]
        for i in range(n_eh * 32):
            w1_dma += plan(w1_sem[i % 2], 1, step=16)
        for j in range(n_eh * 4):
            w2_dma += plan(w2_sem[j % 2], 1, step=16)
        h_mm, y_mm = [], []
        for _eh in range(n_eh):
            h_mm += plan(ctx.sT, 32)
            y_mm += plan(ctx.sT, 32)
        h_ev = plan(ctx.sA, n_eh * 32)
        y_add = plan(ctx.sV, n_eh * 32)

        def g_weights(g):
            for e in range(E0):
                for half in range(2):
                    i_eh = e * 2 + half
                    for ft in range(32):
                        i = i_eh * 32 + ft
                        if i >= 2:
                            g.wait_ge(ctx.sT.h, h_mm[i - 2])
                        g.dma_start(
                            out=w1buf[:, i % 2, :, :],
                            in_=io["l0_w1"][e].rearrange("(dc p) f -> p dc f", p=128)
                            [:, :, ft * 128:(ft + 1) * 128],
                        ).then_inc(w1_sem[i % 2].h, 16)
                    for fcg in range(4):
                        j = i_eh * 4 + fcg
                        if j >= 2:
                            g.wait_ge(ctx.sT.h, y_mm[(j - 2) * 8 + 7])
                        g.dma_start(
                            out=w2buf[:, j % 2, :, :],
                            in_=io["l0_w2"][e][fcg * 1024:(fcg + 1) * 1024, :]
                            .rearrange("(c p) n -> p c n", p=128),
                        ).then_inc(w2_sem[j % 2].h, 16)

        b.gpsimd(g_weights)

        def t_exp(te):
            te.wait_ge(ctx.sA.h, rt_ev[NT - 1])
            for e in range(E0):
                for half in range(2):
                    i_eh = e * 2 + half
                    for ft in range(32):
                        i = i_eh * 32 + ft
                        te.wait_ge(w1_sem[i % 2].h, w1_dma[i])
                        if i >= 2:
                            te.wait_ge(ctx.sA.h, h_ev[i - 2])
                        for dc in range(DC):
                            for nb in range(2):
                                ins = te.matmul(
                                    h_ps[i % 2][:, nb * 512:(nb + 1) * 512],
                                    w1buf[:, i % 2, dc, :],
                                    lnT[:, dc * T + half * 1024 + nb * 512:
                                        dc * T + half * 1024 + (nb + 1) * 512],
                                    start=(dc == 0), stop=(dc == DC - 1))
                                if dc == DC - 1 and nb == 1:
                                    ins.then_inc(ctx.sT.h, 1)
                    for fcg in range(4):
                        j = i_eh * 4 + fcg
                        te.wait_ge(w2_sem[j % 2].h, w2_dma[j])
                        te.wait_ge(ctx.sA.h, h_ev[i_eh * 32 + 31])
                        for tt in range(8):
                            jj = j * 8 + tt
                            if jj >= 2:
                                te.wait_ge(ctx.sV.h, y_add[jj - 2])
                            for fc in range(8):
                                fcol = (fcg * 8 + fc) * 1024 + tt * 128
                                for nb in range(2):
                                    ins = te.matmul(
                                        y_ps[jj % 2][:, nb * 512:(nb + 1) * 512],
                                        h_sb[:, fcol:fcol + 128],
                                        w2buf[:, j % 2, fc, nb * 512:(nb + 1) * 512],
                                        start=(fc == 0), stop=(fc == 7))
                                    if fc == 7 and nb == 1:
                                        ins.then_inc(ctx.sT.h, 1)

        b.tensor(t_exp)

        def s_gelu(sc):
            for e in range(E0):
                for half in range(2):
                    i_eh = e * 2 + half
                    for ft in range(32):
                        i = i_eh * 32 + ft
                        sc.wait_ge(ctx.sT.h, h_mm[i])
                        sc.activation(h_sb[:, ft * 1024:(ft + 1) * 1024],
                                      h_ps[i % 2][:, :], AF.Gelu
                                      ).then_inc(ctx.sA.h, 1)

        b.scalar(s_gelu)

        def v_acc(v):
            for e in range(E0):
                for half in range(2):
                    i_eh = e * 2 + half
                    for fcg in range(4):
                        j = i_eh * 4 + fcg
                        for tt in range(8):
                            jj = j * 8 + tt
                            v.wait_ge(ctx.sT.h, y_mm[jj])
                            g = half * 8 + tt
                            v.scalar_tensor_tensor(
                                x[:, g * D:(g + 1) * D], y_ps[jj % 2][:, :],
                                coef[:, g * 16 + e:g * 16 + e + 1],
                                x[:, g * D:(g + 1) * D],
                                ALU.mult, ALU.add).then_inc(ctx.sV.h, 1)

        b.vector(v_acc)

        emit_barrier(ctx)



def emit_l0attn(ctx, r, io, cc):
    nc, b = ctx.nc, ctx.block
    from contextlib import ExitStack
    with ExitStack() as es:
        ec = es.enter_context
        C0 = 32768 + 16384
        VO = 32 * 130
        arena = ec(nc.sbuf_tensor(f"attn_arena_{uid()}", [128, C0 + VO + 4096 + 4096], BF16))
        lnT = arena[:, 0:DC * T]
        kvT = arena[:, 0:32768].rearrange("p (dc t) -> p dc t", dc=DC)
        qT = arena[:, 32768:32768 + 16384]
        Vo = arena[:, C0:C0 + VO]
        vT = arena[:, C0 + VO:C0 + VO + 4096]
        kT = vT
        E2 = [arena[:, C0 + VO + 4096 + i * 1024:C0 + VO + 4096 + (i + 1) * 1024]
              for i in range(3)]
        o_sb = ec(nc.sbuf_tensor(f"o_sb_{uid()}", [128, 2048], BF16))
        wq_sb = ec(nc.sbuf_tensor(f"wq_sb_{uid()}", [128, 2, DC, 128], BF16))
        wv_sb = ec(nc.sbuf_tensor(f"wv_sb_{uid()}", [128, DC, 128], BF16))
        wk_sb = ec(nc.sbuf_tensor(f"wk_sb_{uid()}", [128, DC, 128], BF16))
        wo_sb = ec(nc.sbuf_tensor(f"wo_sb_{uid()}", [128, 1024], BF16))
        bq_sb = ec(nc.sbuf_tensor(f"bq_sb_{uid()}", [128, 8], F32))
        bk_sb = ec(nc.sbuf_tensor(f"bk_sb_{uid()}", [128, 8], F32))
        bv_sb = ec(nc.sbuf_tensor(f"bv_sb_{uid()}", [128, 8], F32))
        o_f = r["dump"][0:64, 0:512]
        dsc = r["dump"][0:64, 512:768].bitcast(BF16)[0:1, :]
        bo_sb = r["ln_tok"][0:1, 0:1024]

        emit_ln(ctx, r, lnT)

        snd_x = plan(ctx.sX, 4, step=16)

        def q_send(g):
            g.dma_start(out=cc["in"].ap().rearrange("(dc p) t -> p dc t", p=128),
                        in_=lnT.rearrange("p (dc t) -> p dc t", dc=DC)
                        ).then_inc(ctx.sX.h, 16)
            g.dma_start(out=bq_sb[:, :], in_=io["l0_bq"]).then_inc(ctx.sX.h, 16)
            g.dma_start(out=bk_sb[:, :], in_=io["l0_bk"]).then_inc(ctx.sX.h, 16)
            g.dma_start(out=bv_sb[:, :], in_=io["l0_bv"]).then_inc(ctx.sX.h, 16)

        b.sync(q_send)

        cc_done = plan(ctx.sG, 1)
        bo_dma = plan(ctx.sP, 1, step=16)
        wq_sem = [ctx.sW0, ctx.sW1]
        wq_dma = []
        for i in range(8):
            wq_dma += plan(wq_sem[i % 2], 1, step=16)
        q_mm = plan(ctx.sT, 8 * 4)
        q_ev = plan(ctx.sA, 8 * 4)

        def g_cc(g):
            g.wait_ge(ctx.sX.h, snd_x[3])
            g.collective_compute(
                "AllGather", ALU.bypass, replica_groups=cc["rg"],
                ins=[cc["in"].ap().opt()], outs=[cc["out"].ap().opt()],
            ).then_inc(ctx.sG.h, 1)
            g.dma_start(out=bo_sb[:, :], in_=io["l0_bo_row"]).then_inc(ctx.sP.h, 16)
            for pp in range(8):
                if pp >= 2:
                    g.wait_ge(ctx.sT.h, q_mm[(pp - 2) * 4 + 3])
                g.dma_start(out=wq_sb[:, pp % 2, :, :],
                            in_=io["l0_wqT"].rearrange("(dc p) f -> p dc f", p=128)
                            [:, :, pp * 128:(pp + 1) * 128]
                            ).then_inc(wq_sem[pp % 2].h, 16)

        b.gpsimd(g_cc)

        with (nc.psum_tensor(f"q_ps0_{uid()}", [128, 512], F32) as q_ps0,
              nc.psum_tensor(f"q_ps1_{uid()}", [128, 512], F32) as q_ps1):
            q_ps = [q_ps0[:, :], q_ps1[:, :]]

            def t_q(te):
                for pp in range(8):
                    te.wait_ge(wq_sem[pp % 2].h, wq_dma[pp])
                    for tb in range(4):
                        i = pp * 4 + tb
                        if i >= 2:
                            te.wait_ge(ctx.sA.h, q_ev[i - 2])
                        for dc in range(DC):
                            ins = te.matmul(q_ps[i % 2], wq_sb[:, pp % 2, dc, :],
                                            lnT[:, dc * T + tb * 512:dc * T + (tb + 1) * 512],
                                            start=(dc == 0), stop=(dc == DC - 1))
                        ins.then_inc(ctx.sT.h, 1)

            b.tensor(t_q)

            def s_q(sc):
                sc.wait_ge(ctx.sX.h, snd_x[3])
                for pp in range(8):
                    for tb in range(4):
                        i = pp * 4 + tb
                        sc.wait_ge(ctx.sT.h, q_mm[i])
                        sc.activation(qT[:, pp * 2048 + tb * 512:pp * 2048 + (tb + 1) * 512],
                                      q_ps[i % 2], AF.Identity,
                                      bias=bq_sb[:, pp:pp + 1]).then_inc(ctx.sA.h, 1)

            b.scalar(s_q)

        kv_x = plan(ctx.sW, 2, step=16)

        def g_kv(g):
            g.wait_ge(ctx.sG.h, cc_done[0])
            g.wait_ge(ctx.sA.h, q_ev[-1])
            for rk in range(2):
                g.dma_start(
                    out=kvT[:, :, rk * 2048:(rk + 1) * 2048],
                    in_=cc["out"].ap()[rk * 1024:(rk + 1) * 1024, :]
                    .rearrange("(dc p) t -> p dc t", p=128),
                ).then_inc(ctx.sW.h, 16)

        b.gpsimd(g_kv)

        vo_init = plan(ctx.sV, 1)

        def v_voinit(v):
            v.memset(Vo.rearrange("p (kc h j) -> p kc h j", h=2, j=65)[:, :, :, 64:65],
                     1.0).then_inc(ctx.sV.h, 1)

        b.vector(v_voinit)

        st = dict(kvT=kvT, qT=qT, Vo=Vo, vT=vT, kT=kT, E2=E2, o_sb=o_sb,
                  wv_sb=wv_sb, wk_sb=wk_sb, wo_sb=wo_sb, bk_sb=bk_sb, bv_sb=bv_sb,
                  bo_sb=bo_sb, dsc=dsc, o_f=o_f, kv_x=kv_x, vo_init=vo_init,
                  bo_dma=bo_dma)
        prev_last_v, prev_last_t = None, None
        for pp in range(8):
            prev_last_v, prev_last_t = _l0attn_pair(ctx, r, io, pp, st,
                                                    prev_last_v, prev_last_t)
            if DEBUG_ATTN and pp == 0:
                dbg3 = nc.dram_tensor("dbg3", [128, 2048], BF16,
                                      kind="ExternalOutput").ap()
                dbg4 = nc.dram_tensor("dbg4", [128, 4096], BF16,
                                      kind="ExternalOutput").ap()
                dbg5 = nc.dram_tensor("dbg5", [128, 4160], BF16,
                                      kind="ExternalOutput").ap()
                dx = plan(ctx.sX, 3, step=16)

                def q_dbg(g):
                    g.wait_ge(ctx.sV.h, prev_last_v)
                    g.dma_start(out=dbg3, in_=o_sb[:, :]).then_inc(ctx.sX.h, 16)
                    g.dma_start(out=dbg4, in_=kT[:, :]).then_inc(ctx.sX.h, 16)
                    g.dma_start(out=dbg5, in_=Vo[:, :]).then_inc(ctx.sX.h, 16)

                b.sync(q_dbg)
                emit_barrier(ctx)

    emit_barrier(ctx)


def _l0attn_pair(ctx, r, io, pp, s, prev_last_v, prev_last_t):
    nc, b = ctx.nc, ctx.block
    from contextlib import ExitStack
    x = r["x"]
    kvT, qT, Vo, vT, kT, E2, o_sb = (s["kvT"], s["qT"], s["Vo"], s["vT"], s["kT"],
                                     s["E2"], s["o_sb"])

    wdma = plan(ctx.sW, 3, step=16)

    def g_w(g):
        if prev_last_t is not None:
            g.wait_ge(ctx.sT.h, prev_last_t)
        g.dma_start(out=s["wv_sb"][:, :, :],
                    in_=io["l0_wvT"].rearrange("(dc p) f -> p dc f", p=128)
                    [:, :, pp * 128:(pp + 1) * 128]).then_inc(ctx.sW.h, 16)
        g.dma_start(out=s["wk_sb"][:, :, :],
                    in_=io["l0_wkT"].rearrange("(dc p) f -> p dc f", p=128)
                    [:, :, pp * 128:(pp + 1) * 128]).then_inc(ctx.sW.h, 16)
        g.dma_start(out=s["wo_sb"][:, :],
                    in_=io["l0_woT"][pp * 128:(pp + 1) * 128, :]).then_inc(ctx.sW.h, 16)

    b.gpsimd(g_w)

    v_mm = plan(ctx.sT, 8)
    tr_mm = plan(ctx.sT, 32)
    k_mm = plan(ctx.sT, 8)
    v_ev = plan(ctx.sA, 8)
    k_ev = plan(ctx.sA, 8)
    tr_ev = plan(ctx.sV, 32)

    with ExitStack() as es:
        ec = es.enter_context
        vk_ps = [ec(nc.psum_tensor(f"vk_ps{i}_{pp}_{uid()}", [128, 512], F32))[:, :]
                 for i in range(4)]
        t_ps = [ec(nc.psum_tensor(f"t_ps{i}_{pp}_{uid()}", [128, 128], BF16))[:, :]
                for i in range(2)]

        def t_a(te):
            te.wait_ge(ctx.sW.h, wdma[2])
            if prev_last_v is not None:
                te.wait_ge(ctx.sV.h, prev_last_v)
            for tbg in range(2):
                for dc in range(DC):
                    for tb4 in range(4):
                        i = tbg * 4 + tb4
                        if dc == 0 and tbg == 1:
                            te.wait_ge(ctx.sA.h, v_ev[i - 4])
                        ins = te.matmul(vk_ps[tb4], s["wv_sb"][:, dc, :],
                                        kvT[:, dc, tbg * 2048 + tb4 * 512:
                                            tbg * 2048 + (tb4 + 1) * 512],
                                        start=(dc == 0), stop=(dc == DC - 1))
                        if dc == DC - 1:
                            ins.then_inc(ctx.sT.h, 1)
            for kc in range(32):
                te.wait_ge(ctx.sA.h, v_ev[kc // 4])
                if kc >= 2:
                    te.wait_ge(ctx.sV.h, tr_ev[kc - 2])
                if kc == 0:
                    te.wait_ge(ctx.sV.h, s["vo_init"][0])
                te.transpose(t_ps[kc % 2], vT[:, kc * 128:(kc + 1) * 128],
                             r["ident"][:, :]).then_inc(ctx.sT.h, 1)
            for tbg in range(2):
                for dc in range(DC):
                    for tb4 in range(4):
                        i = tbg * 4 + tb4
                        if dc == 0:
                            te.wait_ge(ctx.sA.h, v_ev[7] if tbg == 0 else k_ev[i - 4])
                        ins = te.matmul(vk_ps[tb4], s["wk_sb"][:, dc, :],
                                        kvT[:, dc, tbg * 2048 + tb4 * 512:
                                            tbg * 2048 + (tb4 + 1) * 512],
                                        start=(dc == 0), stop=(dc == DC - 1))
                        if dc == DC - 1:
                            ins.then_inc(ctx.sT.h, 1)

        b.tensor(t_a)

        def s_a(sc):
            for i in range(8):
                sc.wait_ge(ctx.sT.h, v_mm[i])
                sc.activation(vT[:, i * 512:(i + 1) * 512], vk_ps[i % 4],
                              AF.Identity, bias=s["bv_sb"][:, pp:pp + 1]
                              ).then_inc(ctx.sA.h, 1)
            for i in range(8):
                sc.wait_ge(ctx.sT.h, k_mm[i])
                if i == 0:
                    sc.wait_ge(ctx.sT.h, tr_mm[31])
                sc.activation(kT[:, i * 512:(i + 1) * 512], vk_ps[i % 4],
                              AF.Identity, bias=s["bk_sb"][:, pp:pp + 1]
                              ).then_inc(ctx.sA.h, 1)

        b.scalar(s_a)

        def v_a(v):
            for kc in range(32):
                v.wait_ge(ctx.sT.h, tr_mm[kc])
                dst = Vo[:, kc * 130:(kc + 1) * 130].rearrange(
                    "p (h j) -> p h j", h=2)[:, :, 0:64]
                src = t_ps[kc % 2].rearrange("p (h j) -> p h j", h=2)
                v.tensor_copy(dst, src).then_inc(ctx.sV.h, 1)

        b.vector(v_a)

    sc_mm, pv_mm, den_mm, ex_ev, den_v = [], [], [], [], []
    for qb in range(4):
        sq, pq = [None] * 32, [None] * 32
        sq[0] = plan(ctx.sT, 1)[0]
        sq[1] = plan(ctx.sT, 1)[0]
        sq[2] = plan(ctx.sT, 1)[0]
        for kc in range(32):
            pq[kc] = plan(ctx.sT, 1)[0]
            if kc + 3 < 32:
                sq[kc + 3] = plan(ctx.sT, 1)[0]
        sc_mm.append(sq)
        pv_mm.append(pq)
        den_mm.append(plan(ctx.sT, 2))
        ex_ev.append(plan(ctx.sA, 32))
        den_v.append(plan(ctx.sV, 6))
    p_mm = plan(ctx.sT, NT)
    p_add = plan(ctx.sV, NT)

    with ExitStack() as es:
        ec = es.enter_context
        s_ps = [ec(nc.psum_tensor(f"s_ps{i}_{pp}_{uid()}", [128, 1024], F32))[:, :]
                for i in range(3)]
        o_ps = [ec(nc.psum_tensor(f"o_ps{i}_{pp}_{uid()}", [65, 512], F32))[:, :]
                for i in range(2)]

        def t_b(te):
            def scores(qb, kc):
                if kc >= 3:
                    te.wait_ge(ctx.sA.h, ex_ev[qb][kc - 3])
                elif qb >= 1:
                    te.wait_ge(ctx.sA.h, ex_ev[qb - 1][29 + kc])
                    if kc == 0:
                        te.wait_ge(ctx.sV.h, den_v[qb - 1][5])
                else:
                    if kc == 0:
                        te.wait_ge(ctx.sA.h, k_ev[7])
                        te.wait_ge(ctx.sV.h, tr_ev[31])
                for hp in range(2):
                    ins = te.matmul(
                        s_ps[kc % 3][:, hp * 512:(hp + 1) * 512],
                        kT[hp * 64:(hp + 1) * 64, kc * 128:(kc + 1) * 128],
                        qT[hp * 64:(hp + 1) * 64,
                           pp * 2048 + qb * 512:pp * 2048 + (qb + 1) * 512],
                        start=True, stop=True)
                ins.then_inc(ctx.sT.h, 1)

            def pv(qb, kc):
                te.wait_ge(ctx.sA.h, ex_ev[qb][kc])
                for hp in range(2):
                    ins = te.matmul(
                        o_ps[hp],
                        Vo[:, kc * 130 + hp * 65:kc * 130 + hp * 65 + 65],
                        E2[kc % 3][:, hp * 512:(hp + 1) * 512],
                        start=(kc == 0), stop=(kc == 31))
                ins.then_inc(ctx.sT.h, 1)

            for qb in range(4):
                scores(qb, 0)
                scores(qb, 1)
                scores(qb, 2)
                for kc in range(32):
                    pv(qb, kc)
                    if kc + 3 < 32:
                        scores(qb, kc + 3)
                for hp in range(2):
                    te.wait_ge(ctx.sV.h, den_v[qb][hp * 3 + 0])
                    te.matmul(s_ps[0][0:64, 0:512], r["ones"][0:1, 0:64],
                              s["dsc"][0:1, :], start=True, stop=True
                              ).then_inc(ctx.sT.h, 1)
            te.wait_ge(ctx.sW.h, wdma[2])
            if pp == 0:
                te.wait_ge(ctx.sP.h, s["bo_dma"][0])
            for tt in range(NT):
                if tt == 0:
                    te.wait_ge(ctx.sV.h, den_v[3][5])
                if tt >= 2:
                    te.wait_ge(ctx.sV.h, p_add[tt - 2])
                for nb in range(2):
                    ins = te.matmul(s_ps[tt % 2][:, nb * 512:(nb + 1) * 512],
                                    o_sb[:, tt * 128:(tt + 1) * 128],
                                    s["wo_sb"][:, nb * 512:(nb + 1) * 512],
                                    start=True, stop=(pp != 0))
                    if pp == 0:
                        ins = te.matmul(s_ps[tt % 2][:, nb * 512:(nb + 1) * 512],
                                        r["ones"][0:1, 0:128],
                                        s["bo_sb"][0:1, nb * 512:(nb + 1) * 512],
                                        start=False, stop=True)
                ins.then_inc(ctx.sT.h, 1)

        b.tensor(t_b)

        def s_b(sc):
            for qb in range(4):
                for kc in range(32):
                    sc.wait_ge(ctx.sT.h, sc_mm[qb][kc])
                    if not (qb == 0 and kc < 3):
                        pq2, pkc = (qb, kc - 3) if kc >= 3 else (qb - 1, 29 + kc)
                        sc.wait_ge(ctx.sT.h, pv_mm[pq2][pkc])
                    sc.activation(E2[kc % 3][:, 0:1024], s_ps[kc % 3], AF.Exp,
                                  scale=ISQRT_D).then_inc(ctx.sA.h, 1)

        b.scalar(s_b)

        def v_b(v):
            for qb in range(4):
                for hp in range(2):
                    v.wait_ge(ctx.sT.h, pv_mm[qb][31])
                    with nc.allow_low_precision(reason="bf16 softmax denom"):
                        v.reciprocal(s["dsc"][0:1, :], o_ps[hp][64:65, :]
                                     ).then_inc(ctx.sV.h, 1)
                    v.tensor_copy(s["o_f"][:, :], o_ps[hp][0:64, :]
                                  ).then_inc(ctx.sV.h, 1)
                    v.wait_ge(ctx.sT.h, den_mm[qb][hp])
                    v.drain()
                    v.tensor_mul(
                        o_sb[hp * 64:(hp + 1) * 64, qb * 512:(qb + 1) * 512],
                        s["o_f"][:, :], s_ps[0][0:64, 0:512]).then_inc(ctx.sV.h, 1)
            for tt in range(NT):
                v.wait_ge(ctx.sT.h, p_mm[tt])
                v.tensor_add(x[:, tt * D:(tt + 1) * D], x[:, tt * D:(tt + 1) * D],
                             s_ps[tt % 2]).then_inc(ctx.sV.h, 1)

        b.vector(v_b)

    return p_add[-1], p_mm[-1]



def emit_l1ffn(ctx, r, io):
    nc, b = ctx.nc, ctx.block
    x = r["x"]
    from contextlib import ExitStack
    with ExitStack() as es:
        ec = es.enter_context
        arena = ec(nc.sbuf_tensor(f"ffn1_arena_{uid()}", [128, 32768], BF16))
        lnT = arena[:, 0:16384]
        hgs = arena[:, 16384:24576]
        act = arena[:, 24576:32768]
        gw_sb = ec(nc.sbuf_tensor(f"gw_{uid()}", [128, DC, 16], BF16))
        gbr_sb = ec(nc.sbuf_tensor(f"gbr_{uid()}", [1, 16], BF16))
        gb_sb = ec(nc.sbuf_tensor(f"gb_{uid()}", [128, 16], F32))
        scores = ec(nc.sbuf_tensor(f"scor_{uid()}", [128, NT * 16], F32))
        sfc = ec(nc.sbuf_tensor(f"sfc_{uid()}", [128, NT * 16], F32))
        msk = ec(nc.sbuf_tensor(f"msk_{uid()}", [128, NT * 16], F32))
        mskc = ec(nc.sbuf_tensor(f"mskc_{uid()}", [128, NT * 16], F32))
        gs4 = ec(nc.sbuf_tensor(f"gs4_{uid()}", [128, 64], F32))
        t4 = ec(nc.sbuf_tensor(f"t4_{uid()}", [128, 64], F32))
        gm16 = ec(nc.sbuf_tensor(f"gm16_{uid()}", [128, 16], F32))
        gn16 = ec(nc.sbuf_tensor(f"gn16_{uid()}", [128, 16], F32))
        gsc = ec(nc.sbuf_tensor(f"gsc_{uid()}", [128, NT * 16], F32))
        coef = r["coef"]

        emit_ln(ctx, r, lnT)

        gpre = plan(ctx.sP, 2, step=16)

        def g_pre(g):
            g.dma_start(out=gw_sb[:, :, :],
                        in_=io["l1f_gwT"].rearrange("(dc p) n -> p dc n", p=128)
                        ).then_inc(ctx.sP.h, 16)
            g.dma_start(out=gbr_sb[:, :], in_=io["l1f_gb_row"]).then_inc(ctx.sP.h, 16)

        b.gpsimd(g_pre)

        gbb_mm = plan(ctx.sT, 1)
        g_mm = plan(ctx.sT, NT)
        sig_a = plan(ctx.sA, NT)
        gbcp_v = plan(ctx.sV, 1)
        gate_v = plan(ctx.sV, NT)

        with nc.psum_tensor(f"g_ps_{uid()}", [128, 16], F32) as g_ps:
            def t_gate(te):
                te.wait_ge(ctx.sP.h, gpre[1])
                te.matmul(g_ps[:, :], r["ones"][0:1, 0:128], gbr_sb[0:1, :],
                          start=True, stop=True).then_inc(ctx.sT.h, 1)
                for t in range(NT):
                    if t == 0:
                        te.wait_ge(ctx.sV.h, gbcp_v[0])
                    else:
                        te.wait_ge(ctx.sA.h, sig_a[t - 1])
                    for dc in range(DC):
                        ins = te.matmul(g_ps[:, :],
                                        lnT[:, dc * T + t * 128:dc * T + (t + 1) * 128],
                                        gw_sb[:, dc, :],
                                        start=(dc == 0), stop=(dc == DC - 1))
                    ins.then_inc(ctx.sT.h, 1)

            b.tensor(t_gate)

            def s_gate(sc):
                for t in range(NT):
                    sc.wait_ge(ctx.sT.h, g_mm[t])
                    sc.activation(scores[:, t * 16:(t + 1) * 16], g_ps[:, :],
                                  AF.Sigmoid).then_inc(ctx.sA.h, 1)

            b.scalar(s_gate)

            def v_gate(v):
                v.wait_ge(ctx.sT.h, gbb_mm[0])
                v.tensor_copy(gb_sb[:, :], g_ps[:, :]).then_inc(ctx.sV.h, 1)
                v.wait_ge(ctx.sA.h, sig_a[NT - 1])
                sco = scores[:, :]
                s3 = sco.rearrange("p (t e) -> p t e", e=16)
                s4 = sco.rearrange("p (t g i) -> p t g i", g=4, i=4)
                gbb = gb_sb[:, :].rearrange("p (o e) -> p o e", o=1
                                            ).broadcast_to([128, NT, 16])
                A = sfc[:, :]
                A3 = A.rearrange("p (t e) -> p t e", e=16)
                A4 = A.rearrange("p (t g i) -> p t g i", g=4, i=4)
                B = msk[:, :]
                B3 = B.rearrange("p (t e) -> p t e", e=16)
                B4 = B.rearrange("p (t g i) -> p t g i", g=4, i=4)
                C = mskc[:, :]
                C3 = C.rearrange("p (t e) -> p t e", e=16)
                m64 = gs4[:, :]
                m64_3 = m64.rearrange("p (t g) -> p t g", g=4)
                n64 = t4[:, :]
                n64_3 = n64.rearrange("p (t g) -> p t g", g=4)
                m16 = gm16[:, :]
                n16 = gn16[:, :]
                gscf = gsc[:, :]
                gsc3 = gscf.rearrange("p (t e) -> p t e", e=16)
                v.tensor_tensor(A3, s3, gbb, ALU.add)
                v.drain()
                v.reduce_max(m64_3, A4, axis=AX.X)
                v.drain()
                v.tensor_tensor(B4, A4, m64_3.rearrange("p t g -> p t g ()")
                                .broadcast_to([128, NT, 4, 4]), ALU.is_ge)
                v.drain()
                v.tensor_scalar(B, B, NEG, None, ALU.mult, ALU.bypass)
                v.drain()
                v.tensor_add(B, B, A)
                v.drain()
                v.reduce_max(n64_3, B4, axis=AX.X)
                v.drain()
                v.tensor_add(n64, n64, m64)
                v.drain()
                v.reduce_max(m16, n64_3, axis=AX.X)
                v.drain()
                v.tensor_tensor(m64_3, n64_3, m16.rearrange("p t -> p t ()")
                                .broadcast_to([128, NT, 4]), ALU.is_ge)
                v.drain()
                v.tensor_scalar(m64, m64, NEG, None, ALU.mult, ALU.bypass)
                v.drain()
                v.tensor_add(m64, m64, n64)
                v.drain()
                v.reduce_max(n16, m64_3, axis=AX.X)
                v.drain()
                v.tensor_tensor(m64_3, n64_3, n16.rearrange("p t -> p t ()")
                                .broadcast_to([128, NT, 4]), ALU.is_ge)
                v.drain()
                v.tensor_scalar(m64, m64, 1.0, 1.0e30, ALU.subtract, ALU.mult)
                v.drain()
                v.tensor_tensor(B4, A4, m64_3.rearrange("p t g -> p t g ()")
                                .broadcast_to([128, NT, 4, 4]), ALU.add)
                v.drain()
                v.tensor_copy(C, B)
                v.drain()
                for it in range(4):
                    v.reduce_max(m16, B3, axis=AX.X)
                    v.drain()
                    if it < 3:
                        v.tensor_tensor(gsc3, B3,
                                        m16.rearrange("p t -> p t ()")
                                        .broadcast_to([128, NT, 16]), ALU.is_ge)
                        v.drain()
                        v.tensor_scalar(gscf, gscf, NEG, None, ALU.mult, ALU.bypass)
                        v.drain()
                        v.tensor_add(B, B, gscf)
                        v.drain()
                v.tensor_tensor(C3, C3, m16.rearrange("p t -> p t ()")
                                .broadcast_to([128, NT, 16]), ALU.is_ge)
                v.drain()
                v.tensor_mul(C, C, sco)
                v.drain()
                v.reduce_sum(n16, C3, axis=AX.X)
                v.drain()
                v.tensor_scalar_add(n16, n16, 1.0e-20)
                v.drain()
                v.reciprocal(m16, n16)
                v.drain()
                v.tensor_tensor(C3, C3, m16.rearrange("p t -> p t ()")
                                .broadcast_to([128, NT, 16]), ALU.mult)
                v.drain()
                v.tensor_scalar(coef[:, 0:NT * 16], C, 1.25, None,
                                ALU.mult, ALU.bypass)
                v.drain()
                for t in range(NT):
                    v.tensor_scalar_mul(r["tc"][:, 0:1], r["rstd"][:, t:t + 1], 0.5)
                    v.drain()
                    v.tensor_scalar(r["dump"][:, 0:D], x[:, t * D:(t + 1) * D],
                                    r["mean"][:, t:t + 1], r["tc"][:, 0:1],
                                    ALU.subtract, ALU.mult)
                    v.drain()
                    v.tensor_add(x[:, t * D:(t + 1) * D], x[:, t * D:(t + 1) * D],
                                 r["dump"][:, 0:D]).then_inc(ctx.sV.h, 1)

            b.vector(v_gate)

        emit_barrier(ctx)

        expert_es = ExitStack()
        wg_buf = expert_es.enter_context(
            nc.sbuf_tensor(f"wg_{uid()}", [128, 2, DC, 512], BF16))
        wu_buf = expert_es.enter_context(
            nc.sbuf_tensor(f"wu_{uid()}", [128, 2, DC, 512], BF16))
        wd_buf = expert_es.enter_context(
            nc.sbuf_tensor(f"wd_{uid()}", [128, 2, 4, 1024], BF16))
        we_sem = [ctx.sW0, ctx.sW1]
        up_mm, act_a, act_v, dn_mm, dn_v, wdma = [], [], [], [], [], []
        for e in range(E1):
            wdma.append(plan(we_sem[e % 2], 3, step=16))
            up_mm.append(plan(ctx.sT, 8))
            dn_mm.append(plan(ctx.sT, NT))
            act_a.append(plan(ctx.sA, 4))
            act_v.append(plan(ctx.sV, 4))
            dn_v.append(plan(ctx.sV, NT))

        def g_we(g):
            for e in range(E1):
                if e >= 2:
                    g.wait_ge(ctx.sV.h, dn_v[e - 2][-1])
                g.dma_start(out=wg_buf[:, e % 2, :, :],
                            in_=io["l1f_wgT"][e].rearrange("(dc p) n -> p dc n", p=128)
                            ).then_inc(we_sem[e % 2].h, 16)
                g.dma_start(out=wu_buf[:, e % 2, :, :],
                            in_=io["l1f_wuT"][e].rearrange("(dc p) n -> p dc n", p=128)
                            ).then_inc(we_sem[e % 2].h, 16)
                g.dma_start(out=wd_buf[:, e % 2, :, :],
                            in_=io["l1f_wdT"][e].rearrange("(ic p) n -> p ic n", p=128)
                            ).then_inc(we_sem[e % 2].h, 16)

        b.gpsimd(g_we)

        for e in range(E1):
            _l1ffn_expert(ctx, r, e, dict(
                lnT=lnT, hgs=hgs, act=act, coef=coef,
                wg=wg_buf[:, e % 2, :, :], wu=wu_buf[:, e % 2, :, :],
                wd=wd_buf[:, e % 2, :, :],
                wdma=wdma[e], up_mm=up_mm[e], dn_mm=dn_mm[e],
                act_a=act_a[e], act_v=act_v[e], dn_v=dn_v[e],
                we_sem=we_sem[e % 2],
                prev_dn_v=(dn_v[e - 1][-1] if e > 0 else None)))

        expert_es.close()
        _l1ffn_shared(ctx, r, io, dict(lnT=lnT, hgs=hgs, act=act,
                                       prev_dn_v=dn_v[-1][-1]))

    emit_barrier(ctx)


def _l1ffn_expert(ctx, r, e, s):
    nc, b = ctx.nc, ctx.block
    x = r["x"]
    lnT, hgs, act = s["lnT"], s["hgs"], s["act"]
    from contextlib import ExitStack
    with ExitStack() as es:
        ec = es.enter_context
        up_ps = [ec(nc.psum_tensor(f"up{i}_{e}_{uid()}", [128, 512], F32))[:, :]
                 for i in range(4)]

        def t_up(te):
            te.wait_ge(s["we_sem"].h, s["wdma"][2])
            if s["prev_dn_v"] is not None:
                te.wait_ge(ctx.sV.h, s["prev_dn_v"])
            for kind in range(2):
                w = s["wg"] if kind == 0 else s["wu"]
                for ic in range(4):
                    i = kind * 4 + ic
                    if kind == 1:
                        if ic >= 1:
                            te.wait_ge(ctx.sV.h, s["act_v"][ic - 1])
                        else:
                            te.wait_ge(ctx.sA.h, s["act_a"][3])
                    for dc in range(DC):
                        for tb in range(4):
                            if dc == 0 and i >= 1 and kind == 0 and tb == 0 and ic >= 1:
                                te.wait_ge(ctx.sA.h, s["act_a"][ic - 1])
                            ins = te.matmul(
                                up_ps[tb], w[:, dc, ic * 128:(ic + 1) * 128],
                                lnT[:, dc * T + tb * 512:dc * T + (tb + 1) * 512],
                                start=(dc == 0), stop=(dc == DC - 1))
                        if dc == DC - 1:
                            ins.then_inc(ctx.sT.h, 1)

        b.tensor(t_up)

        def s_up(sc):
            for ic in range(4):
                sc.wait_ge(ctx.sT.h, s["up_mm"][ic])
                for tb in range(4):
                    ins = sc.activation(hgs[:, ic * 2048 + tb * 512:
                                            ic * 2048 + (tb + 1) * 512],
                                        up_ps[tb], AF.Silu)
                ins.then_inc(ctx.sA.h, 1)

        b.scalar(s_up)

        def v_up(v):
            for ic in range(4):
                v.wait_ge(ctx.sT.h, s["up_mm"][4 + ic])
                for tb in range(4):
                    ins = v.tensor_tensor(
                        act[:, ic * 2048 + tb * 512:ic * 2048 + (tb + 1) * 512],
                        hgs[:, ic * 2048 + tb * 512:ic * 2048 + (tb + 1) * 512],
                        up_ps[tb], ALU.mult)
                ins.then_inc(ctx.sV.h, 1)

        b.vector(v_up)

    with ExitStack() as es:
        ec = es.enter_context
        y_ps = [ec(nc.psum_tensor(f"y{i}_{e}_{uid()}", [128, 1024], F32))[:, :]
                for i in range(2)]

        def t_dn(te):
            for tt in range(NT):
                if tt == 0:
                    te.wait_ge(ctx.sV.h, s["act_v"][3])
                if tt >= 2:
                    te.wait_ge(ctx.sV.h, s["dn_v"][tt - 2])
                for ic in range(4):
                    for nb in range(2):
                        ins = te.matmul(
                            y_ps[tt % 2][:, nb * 512:(nb + 1) * 512],
                            act[:, ic * 2048 + tt * 128:ic * 2048 + (tt + 1) * 128],
                            s["wd"][:, ic, nb * 512:(nb + 1) * 512],
                            start=(ic == 0), stop=(ic == 3))
                ins.then_inc(ctx.sT.h, 1)

        b.tensor(t_dn)

        def v_dn(v):
            for tt in range(NT):
                v.wait_ge(ctx.sT.h, s["dn_mm"][tt])
                v.scalar_tensor_tensor(
                    x[:, tt * D:(tt + 1) * D], y_ps[tt % 2],
                    s["coef"][:, tt * 16 + e:tt * 16 + e + 1],
                    x[:, tt * D:(tt + 1) * D],
                    ALU.mult, ALU.add).then_inc(ctx.sV.h, 1)

        b.vector(v_dn)


def _l1ffn_shared(ctx, r, io, s):
    nc, b = ctx.nc, ctx.block
    x = r["x"]
    lnT, hgs, act = s["lnT"], s["hgs"], s["act"]
    from contextlib import ExitStack
    actS = hgs.tensor if False else None
    with ExitStack() as es:
        ec = es.enter_context
        wsg_sb = ec(nc.sbuf_tensor(f"wsg_{uid()}", [128, 2, DC, 128], BF16))
        wsu_sb = ec(nc.sbuf_tensor(f"wsu_{uid()}", [128, 2, DC, 128], BF16))
        wsd_sb = ec(nc.sbuf_tensor(f"wsd_{uid()}", [128, DC, 1024], BF16))
        hgS = ec(nc.sbuf_tensor(f"hgS_{uid()}", [128, 2048], BF16))

        ws_sem = [ctx.sW2, ctx.sW3]
        wdma = []
        for sc8 in range(SH // 128):
            wdma.append(plan(ws_sem[sc8 % 2], 2, step=16))
        wsd_dma = plan(ctx.sP, 1, step=16)

        up_mm = [plan(ctx.sT, 2) for _ in range(8)]
        silu_a = [plan(ctx.sA, 4) for _ in range(8)]
        mul_v = [plan(ctx.sV, 4) for _ in range(8)]
        dn_mm = plan(ctx.sT, NT)
        dn_v = plan(ctx.sV, NT)

        def g_ws(g):
            g.wait_ge(ctx.sV.h, s["prev_dn_v"])
            g.dma_start(out=wsd_sb[:, :, :],
                        in_=io["l1f_wsdT"].rearrange("(sc p) n -> p sc n", p=128)
                        ).then_inc(ctx.sP.h, 16)
            for sc8 in range(8):
                if sc8 >= 2:
                    g.wait_ge(ctx.sV.h, mul_v[sc8 - 2][-1])
                g.dma_start(out=wsg_sb[:, sc8 % 2, :, :],
                            in_=io["l1f_wsgT"].rearrange("(dc p) n -> p dc n", p=128)
                            [:, :, sc8 * 128:(sc8 + 1) * 128]
                            ).then_inc(ws_sem[sc8 % 2].h, 16)
                g.dma_start(out=wsu_sb[:, sc8 % 2, :, :],
                            in_=io["l1f_wsuT"].rearrange("(dc p) n -> p dc n", p=128)
                            [:, :, sc8 * 128:(sc8 + 1) * 128]
                            ).then_inc(ws_sem[sc8 % 2].h, 16)

        b.gpsimd(g_ws)

        with ExitStack() as es2:
            up_ps = [es2.enter_context(
                nc.psum_tensor(f"ups{i}_{uid()}", [128, 512], F32))[:, :]
                for i in range(8)]

            def t_sup(te):
                if s["prev_dn_v"] is not None:
                    te.wait_ge(ctx.sV.h, s["prev_dn_v"])
                for sc8 in range(8):
                    te.wait_ge(ws_sem[sc8 % 2].h, wdma[sc8][1])
                    for kind in range(2):
                        w = wsg_sb if kind == 0 else wsu_sb
                        ps_base = 0 if kind == 0 else 4
                        if kind == 0 and sc8 >= 1:
                            te.wait_ge(ctx.sA.h, silu_a[sc8 - 1][-1])
                        if kind == 1 and sc8 >= 1:
                            te.wait_ge(ctx.sV.h, mul_v[sc8 - 1][-1])
                        for dc in range(DC):
                            for tb in range(4):
                                ins = te.matmul(
                                    up_ps[ps_base + tb],
                                    w[:, sc8 % 2, dc, :],
                                    lnT[:, dc * T + tb * 512:dc * T + (tb + 1) * 512],
                                    start=(dc == 0), stop=(dc == DC - 1))
                            if dc == DC - 1:
                                ins.then_inc(ctx.sT.h, 1)

            b.tensor(t_sup)

            def s_sup(sc):
                for sc8 in range(8):
                    sc.wait_ge(ctx.sT.h, up_mm[sc8][0])
                    if sc8 >= 1:
                        sc.wait_ge(ctx.sV.h, mul_v[sc8 - 1][-1])
                    for tb in range(4):
                        sc.activation(hgS[:, tb * 512:(tb + 1) * 512],
                                      up_ps[tb], AF.Silu).then_inc(ctx.sA.h, 1)

            b.scalar(s_sup)

            def v_sup(v):
                for sc8 in range(8):
                    v.wait_ge(ctx.sT.h, up_mm[sc8][1])
                    for tb in range(4):
                        v.wait_ge(ctx.sA.h, silu_a[sc8][tb])
                        dst = hgs if sc8 < 4 else act
                        v.tensor_tensor(
                            dst[:, (sc8 % 4) * 2048 + tb * 512:
                                (sc8 % 4) * 2048 + (tb + 1) * 512],
                            hgS[:, tb * 512:(tb + 1) * 512],
                            up_ps[4 + tb], ALU.mult).then_inc(ctx.sV.h, 1)

            b.vector(v_sup)

        with ExitStack() as es3:
            y_ps = [es3.enter_context(
                nc.psum_tensor(f"ys{i}_{uid()}", [128, 1024], F32))[:, :]
                for i in range(2)]

            def t_sdn(te):
                te.wait_ge(ctx.sP.h, wsd_dma[0])
                for tt in range(NT):
                    if tt == 0:
                        te.wait_ge(ctx.sV.h, mul_v[7][-1])
                    if tt >= 2:
                        te.wait_ge(ctx.sV.h, dn_v[tt - 2])
                    for sc8 in range(8):
                        src = hgs if sc8 < 4 else act
                        for nb in range(2):
                            ins = te.matmul(
                                y_ps[tt % 2][:, nb * 512:(nb + 1) * 512],
                                src[:, (sc8 % 4) * 2048 + tt * 128:
                                    (sc8 % 4) * 2048 + (tt + 1) * 128],
                                wsd_sb[:, sc8, nb * 512:(nb + 1) * 512],
                                start=(sc8 == 0), stop=(sc8 == 7))
                    ins.then_inc(ctx.sT.h, 1)

            b.tensor(t_sdn)

            def v_sdn(v):
                for tt in range(NT):
                    v.wait_ge(ctx.sT.h, dn_mm[tt])
                    v.scalar_tensor_tensor(
                        x[:, tt * D:(tt + 1) * D], y_ps[tt % 2], 0.5,
                        x[:, tt * D:(tt + 1) * D],
                        ALU.mult, ALU.add).then_inc(ctx.sV.h, 1)

            b.vector(v_sdn)



def emit_l1attn(ctx, r, io, cc):
    nc, b = ctx.nc, ctx.block
    from contextlib import ExitStack
    with ExitStack() as es:
        ec = es.enter_context
        arena = ec(nc.sbuf_tensor(f"l1a_arena_{uid()}",
                                  [128, 32768 + 16384 + 12544], BF16))
        lnT = arena[:, 0:DC * T]
        ln4T = arena[:, 0:32768].rearrange("p (dc t) -> p dc t", dc=DC)
        qkvgT = arena[:, 32768:32768 + 16384]
        flat_sb = arena[:, 32768:32768 + 8192]
        flatT_sb = arena[:, 32768 + 8192:32768 + 16384]
        qtok = arena[:, 49152:49152 + 4096]
        ktok = arena[:, 53248:53248 + 4096]
        fin = arena[:, 57344:57344 + 4096]
        ftok = arena[:, 61440:61440 + 256]
        wpT_sb = arena[:, 49152:49152 + 8192].rearrange("p (tp n) -> p tp n", tp=DC)
        wbuf = ec(nc.sbuf_tensor(f"l1aw_{uid()}", [128, DC, 4, 128], BF16))
        ksum = ec(nc.sbuf_tensor(f"ksum_{uid()}", [128, 64], BF16))
        b_sb = ec(nc.sbuf_tensor(f"l1ab_{uid()}", [128, 16], F32))
        bp_row = ec(nc.sbuf_tensor(f"l1abp_{uid()}", [1, D], BF16))
        M_sb = ec(nc.sbuf_tensor(f"msb_{uid()}", [128, 64], BF16))
        z_sb = ec(nc.sbuf_tensor(f"zsb_{uid()}", [128, 1], F32))
        zt_sb = ec(nc.sbuf_tensor(f"ztsb_{uid()}", [128, 1], F32))
        t1f = r["dump"][:, 512:1024]

        emit_ln(ctx, r, lnT)

        snd_x = plan(ctx.sX, 2, step=16)

        def q_send(g):
            g.dma_start(out=cc["in"].ap().rearrange("(dc p) t -> p dc t", p=128),
                        in_=lnT.rearrange("p (dc t) -> p dc t", dc=DC)
                        ).then_inc(ctx.sX.h, 16)
            g.dma_start(out=b_sb[:, :], in_=io["l1a_b"]).then_inc(ctx.sX.h, 16)

        b.sync(q_send)

        cc_done = plan(ctx.sG, 1)
        wp_dma = plan(ctx.sP, 1, step=16)
        kv_x = plan(ctx.sW, 2, step=16)

        def g_cc(g):
            g.wait_ge(ctx.sX.h, snd_x[1])
            g.collective_compute(
                "AllGather", ALU.bypass, replica_groups=cc["rg"],
                ins=[cc["in"].ap().opt()], outs=[cc["out"].ap().opt()],
            ).then_inc(ctx.sG.h, 1)
            g.dma_start(out=bp_row[:, :], in_=io["l1a_bp_row"]).then_inc(ctx.sP.h, 16)
            g.wait_ge(ctx.sG.h, cc_done[0])
            for rk in range(2):
                g.dma_start(
                    out=ln4T[:, :, rk * 2048:(rk + 1) * 2048],
                    in_=cc["out"].ap()[rk * 1024:(rk + 1) * 1024, :]
                    .rearrange("(dc p) t -> p dc t", p=128),
                ).then_inc(ctx.sW.h, 16)

        b.gpsimd(g_cc)

        st = dict(ln4T=ln4T, qkvgT=qkvgT, wbuf=wbuf, ksum=ksum, b_sb=b_sb,
                  M_sb=M_sb, z_sb=z_sb, zt_sb=zt_sb, t1f=t1f, qtok=qtok,
                  ktok=ktok, fin=fin, ftok=ftok, bounce=cc["bounce"],
                  kv_x=kv_x)
        prev = {"v": None, "t": None, "x": None}
        for pq in range(4):
            _l1attn_pair(ctx, r, io, pq, st, prev)

        _l1attn_proj(ctx, r, io, dict(flat=flat_sb, flatT=flatT_sb, wpT=wpT_sb,
                                      bp_row=bp_row, bounce=cc["bounce"],
                                      wp_dma=wp_dma, prev=prev))
        del wp_dma

    emit_barrier(ctx)


def _l1attn_pair(ctx, r, io, pq, s, prev):
    nc, b = ctx.nc, ctx.block
    from contextlib import ExitStack
    qkvgT = s["qkvgT"]

    wdma = plan(ctx.sW, 4, step=16)

    def g_w(g):
        if prev["t"] is not None:
            g.wait_ge(ctx.sT.h, prev["t"])
        for k in range(4):
            g.dma_start(out=s["wbuf"][:, :, k, :],
                        in_=io["l1a_wT"].rearrange("(dc p) (k c) -> p dc k c",
                                                   p=128, k=4)
                        [:, :, k, pq * 128:(pq + 1) * 128]
                        ).then_inc(ctx.sW.h, 16)

    b.gpsimd(g_w)

    pr_mm = plan(ctx.sT, 4 * 8)
    pr_ev = plan(ctx.sA, 4 * 8)
    sp_a = plan(ctx.sA, 8)
    sp_v = plan(ctx.sV, 8)

    with ExitStack() as es:
        ec = es.enter_context
        u_ps = [ec(nc.psum_tensor(f"u{i}_{pq}_{uid()}", [128, 512], F32))[:, :]
                for i in range(8)]

        def t_pr(te):
            te.wait_ge(ctx.sW.h, wdma[3])
            if prev["v"] is not None:
                te.wait_ge(ctx.sV.h, prev["v"])
            for kind in range(4):
                for dc in range(DC):
                    for tb in range(8):
                        i = kind * 8 + tb
                        if dc == 0 and kind >= 1:
                            te.wait_ge(ctx.sA.h, pr_ev[i - 8])
                        ins = te.matmul(
                            u_ps[tb], s["wbuf"][:, dc, kind, :],
                            s["ln4T"][:, dc, tb * 512:(tb + 1) * 512],
                            start=(dc == 0), stop=(dc == DC - 1))
                        if dc == DC - 1:
                            ins.then_inc(ctx.sT.h, 1)

        b.tensor(t_pr)

        def s_pr(sc):
            for kind in range(4):
                fn = AF.Identity if kind < 2 else AF.Gelu
                for tb in range(8):
                    i = kind * 8 + tb
                    sc.wait_ge(ctx.sT.h, pr_mm[i])
                    sc.activation(qkvgT[:, kind * 4096 + tb * 512:
                                        kind * 4096 + (tb + 1) * 512],
                                  u_ps[tb], fn,
                                  bias=s["b_sb"][:, pq * 4 + kind:pq * 4 + kind + 1]
                                  ).then_inc(ctx.sA.h, 1)
            for i8 in range(8):
                kq, qtr = i8 // 4, i8 % 4
                sl = qkvgT[:, kq * 4096 + qtr * 1024:kq * 4096 + (qtr + 1) * 1024]
                sc.wait_ge(ctx.sA.h, pr_ev[kq * 8 + qtr * 2 + 1])
                if i8 >= 1:
                    sc.wait_ge(ctx.sV.h, sp_v[i8 - 1])
                sc.activation(r["dump"][:, :], sl, AF.Abs)
                sc.drain()
                sc.activation(r["dump"][:, :], r["dump"][:, :], AF.Exp, scale=-1.0)
                sc.drain()
                sc.activation(r["dump"][:, :], r["dump"][:, :], AF.Ln, bias=1.0)
                sc.drain()
                sc.activation(r["ln_tok"][:, :], sl, AF.Relu
                              ).then_inc(ctx.sA.h, 1)

        b.scalar(s_pr)

        def v_pr(v):
            for i8 in range(8):
                kq, qtr = i8 // 4, i8 % 4
                sl = qkvgT[:, kq * 4096 + qtr * 1024:kq * 4096 + (qtr + 1) * 1024]
                v.wait_ge(ctx.sA.h, sp_a[i8])
                v.tensor_tensor(sl, r["ln_tok"][:, :], r["dump"][:, :],
                                ALU.add).then_inc(ctx.sV.h, 1)

        b.vector(v_pr)

    _l1attn_pair_core(ctx, r, io, pq, s, sp_v[-1], prev)


def _l1attn_pair_core(ctx, r, io, pq, s, sp_last, prev):
    nc, b = ctx.nc, ctx.block
    from contextlib import ExitStack
    qkvgT = s["qkvgT"]

    tr_mm = plan(ctx.sT, 64)
    tr_ev = []
    for i in range(64):
        if i < 32:
            tr_ev.append(plan(ctx.sV, 1)[-1])
        else:
            tr_ev.append(plan(ctx.sV, 3)[-1])
    mz_mm = plan(ctx.sT, 1)
    o1_mm = plan(ctx.sT, 8)
    ft_mm = plan(ctx.sT, 32)
    mz_v = plan(ctx.sV, 2)
    fin_v = plan(ctx.sV, 8 * 2)
    ft_v = plan(ctx.sV, 32)
    ft_x = plan(ctx.sX, 32, step=16)

    from contextlib import ExitStack as _ES
    esA = _ES()
    t_ps = [esA.enter_context(nc.psum_tensor(f"lt{i}_{pq}_{uid()}",
                                             [128, 128], BF16))[:, :]
            for i in range(2)]

    def t_tr(te):
        te.wait_ge(ctx.sV.h, sp_last)
        for kq in range(2):
            for tc in range(32):
                i = kq * 32 + tc
                if i >= 2:
                    te.wait_ge(ctx.sV.h, tr_ev[i - 2])
                te.transpose(t_ps[i % 2],
                             qkvgT[:, kq * 4096 + tc * 128:
                                   kq * 4096 + (tc + 1) * 128],
                             r["ident"][:, :]).then_inc(ctx.sT.h, 1)

    b.tensor(t_tr)

    def v_tr(v):
        for kq in range(2):
            for tc in range(32):
                i = kq * 32 + tc
                v.wait_ge(ctx.sT.h, tr_mm[i])
                dst = s["qtok"] if kq == 0 else s["ktok"]
                v.tensor_copy(dst[:, tc * 128:(tc + 1) * 128],
                              t_ps[i % 2]).then_inc(ctx.sV.h, 1)
                if kq == 1:
                    with nc.allow_low_precision(reason="bf16 ksum operand"):
                        for hl in range(2):
                            v.reduce_sum(s["ksum"][:, tc * 2 + hl:tc * 2 + hl + 1],
                                         t_ps[i % 2][:, hl * 64:(hl + 1) * 64],
                                         axis=AX.X).then_inc(ctx.sV.h, 1)

    b.vector(v_tr)
    esA.close()

    esB = _ES()
    M_ps = [esB.enter_context(nc.psum_tensor(f"lm{hl}_{pq}_{uid()}",
                                             [128, 64], F32))[:, :]
            for hl in range(2)]
    zd_ps = [esB.enter_context(nc.psum_tensor(f"lzd{hl}_{pq}_{uid()}",
                                              [128, 1], F32))[:, :]
             for hl in range(2)]

    def t_m(te):
        te.wait_ge(ctx.sV.h, tr_ev[63])
        for tc in range(32):
            for hl in range(2):
                tpos = (0, 64) if hl == 1 else None
                te.matmul(M_ps[hl][hl * 64:(hl + 1) * 64, :],
                          s["ktok"][:, tc * 128 + hl * 64:tc * 128 + hl * 64 + 64],
                          s["qtok"][:, tc * 128 + hl * 64:tc * 128 + hl * 64 + 64],
                          start=(tc == 0), stop=(tc == 31), tile_position=tpos)
                ins = te.matmul(zd_ps[hl][hl * 64:(hl + 1) * 64, :],
                                s["qtok"][:, tc * 128 + hl * 64:
                                          tc * 128 + hl * 64 + 64],
                                s["ksum"][:, tc * 2 + hl:tc * 2 + hl + 1],
                                start=(tc == 0), stop=(tc == 31),
                                tile_position=tpos)
        ins.then_inc(ctx.sT.h, 1)

    b.tensor(t_m)

    def v_m(v):
        v.wait_ge(ctx.sT.h, mz_mm[0])
        for hl in range(2):
            sl = slice(hl * 64, (hl + 1) * 64)
            v.tensor_copy(s["M_sb"][sl, :], M_ps[hl][sl, :])
            v.tensor_scalar(s["zt_sb"][sl, :], zd_ps[hl][sl, :], L1_SCALE,
                            float(D), ALU.mult, ALU.add)
        v.then_inc if False else None
        v.nop().then_inc(ctx.sV.h, 1)
        v.drain()
        with nc.allow_low_precision(reason="z gate reciprocal"):
            v.reciprocal(s["z_sb"][:, :], s["zt_sb"][:, :]).then_inc(ctx.sV.h, 1)

    b.vector(v_m)
    esB.close()

    esC = _ES()
    o1_ps = [[esC.enter_context(nc.psum_tensor(f"lo{i}{hl}_{pq}_{uid()}",
                                               [128, 512], F32))[:, :]
              for hl in range(2)] for i in range(2)]
    t_ps = [esC.enter_context(nc.psum_tensor(f"lt2{i}_{pq}_{uid()}",
                                             [128, 128], BF16))[:, :]
            for i in range(2)]

    def t_o(te):
        te.wait_ge(ctx.sV.h, mz_v[1])
        for tb in range(8):
            if tb >= 2:
                te.wait_ge(ctx.sV.h, fin_v[(tb - 2) * 2 + 1])
            for hl in range(2):
                tpos = (64, 64) if hl == 1 else None
                ins = te.matmul(
                    o1_ps[tb % 2][hl][hl * 64:(hl + 1) * 64, :],
                    s["M_sb"][hl * 64:(hl + 1) * 64, :],
                    qkvgT[hl * 64:(hl + 1) * 64,
                          2 * 4096 + tb * 512:2 * 4096 + (tb + 1) * 512],
                    start=True, stop=True, tile_position=tpos)
            ins.then_inc(ctx.sT.h, 1)
        for tc in range(32):
            te.wait_ge(ctx.sV.h, fin_v[(tc // 4) * 2 + 1])
            if tc >= 2:
                te.wait_ge(ctx.sV.h, ft_v[tc - 2])
            te.transpose(t_ps[tc % 2],
                         s["fin"][:, tc * 128:(tc + 1) * 128],
                         r["ident"][:, :]).then_inc(ctx.sT.h, 1)

    b.tensor(t_o)

    def v_o(v):
        for tb in range(8):
            v.wait_ge(ctx.sT.h, o1_mm[tb])
            for hl in range(2):
                sl = slice(hl * 64, (hl + 1) * 64)
                v.scalar_tensor_tensor(
                    s["t1f"][sl, :], o1_ps[tb % 2][hl][sl, :], L1_SCALE,
                    qkvgT[sl, 2 * 4096 + tb * 512:2 * 4096 + (tb + 1) * 512],
                    ALU.mult, ALU.add)
            v.nop().then_inc(ctx.sV.h, 1)
            v.drain()
            v.scalar_tensor_tensor(
                s["fin"][:, tb * 512:(tb + 1) * 512], s["t1f"][:, :],
                s["z_sb"][:, 0:1],
                qkvgT[:, 3 * 4096 + tb * 512:3 * 4096 + (tb + 1) * 512],
                ALU.mult, ALU.mult).then_inc(ctx.sV.h, 1)
        for tc in range(32):
            v.wait_ge(ctx.sT.h, ft_mm[tc])
            if tc >= 2:
                v.wait_ge(ctx.sX.h, ft_x[tc - 2])
            v.tensor_copy(s["ftok"][:, (tc % 2) * 128:(tc % 2) * 128 + 128],
                          t_ps[tc % 2]).then_inc(ctx.sV.h, 1)

    b.vector(v_o)

    def q_h(g):
        for tc in range(32):
            g.wait_ge(ctx.sV.h, ft_v[tc])
            g.dma_start(
                out=s["bounce"].ap()
                .rearrange("(hh t j) -> hh t j", hh=8, t=TK)
                [pq * 2:pq * 2 + 2, tc * 128:(tc + 1) * 128, :]
                .transpose([1, 0, 2]),
                in_=s["ftok"][:, (tc % 2) * 128:(tc % 2) * 128 + 128]
                .rearrange("p (hh j) -> p hh j", hh=2),
            ).then_inc(ctx.sX.h, 16)

    b.sync(q_h)
    esC.close()


    prev["v"] = ft_v[-1]
    prev["t"] = ft_mm[-1]
    prev["x"] = ft_x[-1]


def _l1attn_proj(ctx, r, io, s):
    nc, b = ctx.nc, ctx.block
    from contextlib import ExitStack
    x = r["x"]
    flat, flatT = s["flat"], s["flatT"]
    prev = s["prev"]

    wpl = plan(ctx.sP, 1, step=16)
    for half in range(2):
        fl_x = plan(ctx.sW, 1, step=16)
        fT_mm = plan(ctx.sT, 64)
        fT_v = plan(ctx.sV, 64)
        pj_mm = plan(ctx.sT, 8)
        pj_v = plan(ctx.sV, 8)

        def g_fl(g, half=half, fl_x=fl_x):
            if half == 0:
                g.wait_ge(ctx.sX.h, prev["x"])
                g.wait_ge(ctx.sV.h, prev["v"])
                g.dma_start(out=s["wpT"],
                            in_=io["l1a_wpT"].rearrange("(tp p) n -> p tp n", p=128)
                            ).then_inc(ctx.sP.h, 16)
            else:
                g.wait_ge(ctx.sV.h, prev["pj"])
            g.dma_start(
                out=flat[:, :].rearrange("p (t d) -> p t d", t=8),
                in_=s["bounce"].ap()
                .rearrange("(n p d) -> p n d", p=128, d=D)
                [:, half * 8:(half + 1) * 8, :],
            ).then_inc(ctx.sW.h, 16)

        b.gpsimd(g_fl)

        with ExitStack() as es:
            ec = es.enter_context
            t_ps = [ec(nc.psum_tensor(f"pt{i}_{half}_{uid()}", [128, 128], BF16))[:, :]
                    for i in range(2)]
            p_ps = [ec(nc.psum_tensor(f"pp{i}_{half}_{uid()}", [128, 1024], F32))[:, :]
                    for i in range(2)]

            def t_f(te, half=half, fl_x=fl_x, fT_mm=fT_mm, fT_v=fT_v,
                    pj_mm=pj_mm, pj_v=pj_v):
                te.wait_ge(ctx.sW.h, fl_x[0])
                for i in range(64):
                    if i >= 2:
                        te.wait_ge(ctx.sV.h, fT_v[i - 2])
                    tt, dc = i // 8, i % 8
                    te.transpose(t_ps[i % 2],
                                 flat[:, tt * 1024 + dc * 128:
                                      tt * 1024 + (dc + 1) * 128],
                                 r["ident"][:, :]).then_inc(ctx.sT.h, 1)
                te.wait_ge(ctx.sP.h, wpl[0])
                for tt in range(8):
                    te.wait_ge(ctx.sV.h, fT_v[tt * 8 + 7])
                    if tt >= 2:
                        te.wait_ge(ctx.sV.h, pj_v[tt - 2])
                    for dc in range(DC):
                        for nb in range(2):
                            te.matmul(p_ps[tt % 2][:, nb * 512:(nb + 1) * 512],
                                      flatT[:, dc * 1024 + tt * 128:
                                            dc * 1024 + (tt + 1) * 128],
                                      s["wpT"][:, dc, nb * 512:(nb + 1) * 512],
                                      start=(dc == 0), stop=False)
                    for nb in range(2):
                        ins = te.matmul(p_ps[tt % 2][:, nb * 512:(nb + 1) * 512],
                                        r["ones"][0:1, 0:128],
                                        s["bp_row"][0:1, nb * 512:(nb + 1) * 512],
                                        start=False, stop=True)
                    ins.then_inc(ctx.sT.h, 1)

            b.tensor(t_f)

            def v_f(v, half=half, fT_mm=fT_mm, fT_v=fT_v, pj_mm=pj_mm, pj_v=pj_v):
                for i in range(64):
                    v.wait_ge(ctx.sT.h, fT_mm[i])
                    tt, dc = i // 8, i % 8
                    v.tensor_copy(flatT[:, dc * 1024 + tt * 128:
                                        dc * 1024 + (tt + 1) * 128],
                                  t_ps[i % 2]).then_inc(ctx.sV.h, 1)
                for tt in range(8):
                    g = half * 8 + tt
                    v.wait_ge(ctx.sT.h, pj_mm[tt])
                    v.tensor_add(x[:, g * D:(g + 1) * D], x[:, g * D:(g + 1) * D],
                                 p_ps[tt % 2]).then_inc(ctx.sV.h, 1)

            b.vector(v_f)

        prev["pj"] = pj_v[-1]



PHASESETS = {
    "l0ffn": ["l0ffn"],
    "l0attn": ["l0attn"],
    "l1ffn": ["l1ffn"],
    "l1attn": ["l1attn"],
}


def build_program(phases, replica_groups=RG):
    nc = bass.Bass("TRN2", target_bir_lowering=False, detect_race_conditions=False)
    io = {}
    io["x_in"] = nc.dram_tensor("x_in", [T, D], F32, kind="ExternalInput").ap()
    io["y"] = nc.dram_tensor("y", [T, D], F32, kind="ExternalOutput").ap()
    need = set(phases)
    if "l0ffn" in need:
        io["l0_rw"] = nc.dram_tensor("l0_rw", [D, 9], F32, kind="ExternalInput").ap()
        io["l0_rb_row"] = nc.dram_tensor("l0_rb_row", [1, 9], F32, kind="ExternalInput").ap()
        io["l0_w1"] = nc.dram_tensor("l0_w1", [E0, D, F0], F32, kind="ExternalInput").ap()
        io["l0_w2"] = nc.dram_tensor("l0_w2", [E0, F0, D], F32, kind="ExternalInput").ap()
    if "l0attn" in need:
        for nme in ["l0_wqT", "l0_wkT", "l0_wvT", "l0_woT"]:
            io[nme] = nc.dram_tensor(nme, [D, D], F32, kind="ExternalInput").ap()
        for nme in ["l0_bq", "l0_bk", "l0_bv"]:
            io[nme] = nc.dram_tensor(nme, [128, 8], F32, kind="ExternalInput").ap()
        io["l0_bo_row"] = nc.dram_tensor("l0_bo_row", [1, D], F32, kind="ExternalInput").ap()
    if "l1ffn" in need:
        io["l1f_gwT"] = nc.dram_tensor("l1f_gwT", [D, 16], F32, kind="ExternalInput").ap()
        io["l1f_gb_row"] = nc.dram_tensor("l1f_gb_row", [1, 16], F32, kind="ExternalInput").ap()
        io["l1f_wgT"] = nc.dram_tensor("l1f_wgT", [E1, D, I1], F32, kind="ExternalInput").ap()
        io["l1f_wuT"] = nc.dram_tensor("l1f_wuT", [E1, D, I1], F32, kind="ExternalInput").ap()
        io["l1f_wdT"] = nc.dram_tensor("l1f_wdT", [E1, I1, D], F32, kind="ExternalInput").ap()
        io["l1f_wsgT"] = nc.dram_tensor("l1f_wsgT", [D, SH], F32, kind="ExternalInput").ap()
        io["l1f_wsuT"] = nc.dram_tensor("l1f_wsuT", [D, SH], F32, kind="ExternalInput").ap()
        io["l1f_wsdT"] = nc.dram_tensor("l1f_wsdT", [SH, D], F32, kind="ExternalInput").ap()
    if "l1attn" in need:
        io["l1a_wT"] = nc.dram_tensor("l1a_wT", [D, 2048], F32, kind="ExternalInput").ap()
        io["l1a_b"] = nc.dram_tensor("l1a_b", [128, 16], F32, kind="ExternalInput").ap()
        io["l1a_wpT"] = nc.dram_tensor("l1a_wpT", [D, D], F32, kind="ExternalInput").ap()
        io["l1a_bp_row"] = nc.dram_tensor("l1a_bp_row", [1, D], F32, kind="ExternalInput").ap()
    n_cc = sum(1 for p in phases if p in ("l0attn", "l1attn"))
    cc_bufs = []
    for i in range(n_cc):
        cc_bufs.append({
            "in": nc.dram_tensor(f"cc_in{i}", [D, T], BF16, kind="Internal"),
            "out": nc.dram_tensor(f"cc_out{i}", [2 * D, T], BF16, kind="Internal"),
            "bounce": nc.dram_tensor(f"bounce{i}", [8 * TK * HD], BF16, kind="Internal"),
            "rg": replica_groups,
        })
    cc_idx = [0]

    ident_np = np.eye(128, dtype=ml_dtypes.bfloat16)
    ones_np = np.ones((1, 128), dtype=ml_dtypes.bfloat16)

    from contextlib import ExitStack
    with ExitStack() as es:
        ec = es.enter_context
        x_sb = ec(nc.sbuf_tensor("x_sb", [128, NT * D], F32))
        mean = ec(nc.sbuf_tensor("mean", [128, NT], F32))
        rstd = ec(nc.sbuf_tensor("rstd", [128, NT], F32))
        var = ec(nc.sbuf_tensor("var", [128, NT], F32))
        s1 = ec(nc.sbuf_tensor("s1", [128, NT], F32))
        s2 = ec(nc.sbuf_tensor("s2", [128, NT], F32))
        dump = ec(nc.sbuf_tensor("dump", [128, 1024], F32))
        ln_tok = ec(nc.sbuf_tensor("ln_tok", [128, 1024], BF16))
        tc = ec(nc.sbuf_tensor("tc", [128, 1], F32))
        tc2 = ec(nc.sbuf_tensor("tc2", [128, 1], F32))
        m9 = ec(nc.sbuf_tensor("m9", [128, 16], F32))
        coef = ec(nc.sbuf_tensor("coef", [128, NT * 16], F32))
        ident = ec(nc.sbuf_tensor("ident", [128, 128], BF16))
        ones = ec(nc.sbuf_tensor("ones", [1, 128], BF16))
        hT = ec(nc.semaphore("sT"))
        hV = ec(nc.semaphore("sV"))
        hA = ec(nc.semaphore("sA"))
        hG = ec(nc.semaphore("sG"))
        hW = ec(nc.semaphore("sW"))
        hX = ec(nc.semaphore("sX"))
        hW0 = ec(nc.semaphore("sW0"))
        hW1 = ec(nc.semaphore("sW1"))
        hW2 = ec(nc.semaphore("sW2"))
        hW3 = ec(nc.semaphore("sW3"))
        hP = ec(nc.semaphore("sP"))
        block = ec(nc.Block())
        sems = dict(sT=Sem(hT), sV=Sem(hV), sA=Sem(hA), sG=Sem(hG),
                    sW=Sem(hW), sX=Sem(hX), sW0=Sem(hW0), sW1=Sem(hW1),
                    sW2=Sem(hW2), sW3=Sem(hW3), sP=Sem(hP))
        ctx = Ctx(nc, block, sems)
        ident_dram = nc.inline_tensor(ident_np, "ident_const").ap()
        ones_dram = nc.inline_tensor(ones_np, "ones_const").ap()

        r = dict(x=x_sb[:, :], mean=mean[:, :], rstd=rstd[:, :], var=var[:, :],
                 s1=s1[:, :], s2=s2[:, :], dump=dump[:, :],
                 ln_tok=ln_tok[:, :], tc=tc[:, :], tc2=tc2[:, :], m9=m9[:, :],
                 coef=coef[:, :], ident=ident[:, :], ones=ones[:, :])

        ctx.sX.inc(48)

        def g_load(g):
            g.dma_start(out=x_sb[:, :].rearrange("p (t d) -> p t d", t=NT),
                        in_=io["x_in"].rearrange("(t p) d -> p t d", p=128)
                        ).then_inc(ctx.sX.h, 16)
            g.dma_start(out=ident[:, :], in_=ident_dram).then_inc(ctx.sX.h, 16)
            g.dma_start(out=ones[:, :], in_=ones_dram).then_inc(ctx.sX.h, 16)

        block.sync(g_load)
        emit_barrier(ctx)

        for ph in phases:
            if ph == "l0ffn":
                emit_l0ffn(ctx, r, io)
            elif ph == "l0attn":
                emit_l0attn(ctx, r, io, cc_bufs[cc_idx[0]])
                cc_idx[0] += 1
            elif ph == "l1ffn":
                emit_l1ffn(ctx, r, io)
            elif ph == "l1attn":
                emit_l1attn(ctx, r, io, cc_bufs[cc_idx[0]])
                cc_idx[0] += 1
            elif ph == "nop":
                pass
            elif ph == "lnonly":
                with nc.sbuf_tensor("lnT_dbg", [128, DC * T], BF16) as lnT_dbg:
                    emit_ln(ctx, r, lnT_dbg[:, :])
                    dbg = nc.dram_tensor("dbg", [128, DC * T], F32,
                                         kind="ExternalOutput").ap()
                    with nc.sbuf_tensor("dbg_f", [128, 2048], F32) as dbg_f:
                        dbg_v = plan(ctx.sV, 8)
                        dbg_x = plan(ctx.sX, 8, step=16)

                        def v_dbg(v):
                            for i in range(8):
                                if i >= 1:
                                    v.wait_ge(ctx.sX.h, dbg_x[i - 1])
                                v.tensor_copy(dbg_f[:, :],
                                              lnT_dbg[:, i * 2048:(i + 1) * 2048]
                                              ).then_inc(ctx.sV.h, 1)

                        def q_dbg(g):
                            for i in range(8):
                                g.wait_ge(ctx.sV.h, dbg_v[i])
                                g.dma_start(out=dbg[:, i * 2048:(i + 1) * 2048],
                                            in_=dbg_f[:, :]).then_inc(ctx.sX.h, 16)

                        ctx.block.vector(v_dbg)
                        ctx.block.sync(q_dbg)
                        emit_barrier(ctx)
                dbg2 = nc.dram_tensor("dbg2", [128, 5 * NT], F32,
                                      kind="ExternalOutput").ap()
                st_x = plan(ctx.sX, 1, step=16)

                def q_dbg2(g):
                    g.dma_start(out=dbg2[:, 0 * NT:1 * NT], in_=s1_t).then_inc(ctx.sX.h, 16)
                    g.dma_start(out=dbg2[:, 1 * NT:2 * NT], in_=s2_t).then_inc(ctx.sX.h, 16)
                    g.dma_start(out=dbg2[:, 2 * NT:3 * NT], in_=mean_t).then_inc(ctx.sX.h, 16)
                    g.dma_start(out=dbg2[:, 3 * NT:4 * NT], in_=var_t).then_inc(ctx.sX.h, 16)
                    g.dma_start(out=dbg2[:, 4 * NT:5 * NT], in_=rstd_t).then_inc(ctx.sX.h, 16)

                ctx.sX.inc(4 * 16)
                s1_t, s2_t = r["s1"], r["s2"]
                mean_t, var_t, rstd_t = r["mean"], r["var"], r["rstd"]
                ctx.block.sync(q_dbg2)
                emit_barrier(ctx)
            else:
                raise ValueError(ph)

        ctx.sX.inc(16)

        def g_store(g):
            g.dma_start(out=io["y"].rearrange("(t p) d -> p t d", p=128),
                        in_=x_sb[:, :].rearrange("p (t d) -> p t d", t=NT)
                        ).then_inc(ctx.sX.h, 16)
            g.wait_ge(ctx.sX.h, ctx.sX.v)

        block.sync(g_store)

    return nc



def prep_inputs(inputs):
    f32 = np.float32
    w = {}
    w["l0_rw"] = np.ascontiguousarray(inputs["L0_ffn_router_w"], dtype=f32)
    w["l0_rb_row"] = np.ascontiguousarray(inputs["L0_ffn_router_b"], dtype=f32).reshape(1, 9)
    w["l0_w1"] = np.ascontiguousarray(inputs["L0_ffn_w1"], dtype=f32)
    w["l0_w2"] = np.ascontiguousarray(inputs["L0_ffn_w2"], dtype=f32)
    for nme, key in [("l0_wqT", "L0_attn_Wq"), ("l0_wkT", "L0_attn_Wk"),
                     ("l0_wvT", "L0_attn_Wv"), ("l0_woT", "L0_attn_Wo")]:
        w[nme] = np.ascontiguousarray(np.asarray(inputs[key], dtype=f32).T)
    for nme, key in [("l0_bq", "L0_attn_bq"), ("l0_bk", "L0_attn_bk"),
                     ("l0_bv", "L0_attn_bv")]:
        w[nme] = np.ascontiguousarray(
            np.asarray(inputs[key], dtype=f32).reshape(8, 128).T)
    w["l0_bo_row"] = np.ascontiguousarray(inputs["L0_attn_bo"], dtype=f32).reshape(1, D)
    w["l1f_gwT"] = np.ascontiguousarray(np.asarray(inputs["L1_ffn_gate_w"], dtype=f32).T)
    w["l1f_gb_row"] = np.ascontiguousarray(inputs["L1_ffn_gate_bias"], dtype=f32).reshape(1, 16)
    w["l1f_wgT"] = np.ascontiguousarray(np.asarray(inputs["L1_ffn_Wg"], dtype=f32).transpose(0, 2, 1))
    w["l1f_wuT"] = np.ascontiguousarray(np.asarray(inputs["L1_ffn_Wu"], dtype=f32).transpose(0, 2, 1))
    w["l1f_wdT"] = np.ascontiguousarray(np.asarray(inputs["L1_ffn_Wd"], dtype=f32).transpose(0, 2, 1))
    w["l1f_wsgT"] = np.ascontiguousarray(np.asarray(inputs["L1_ffn_Ws_g"], dtype=f32).T)
    w["l1f_wsuT"] = np.ascontiguousarray(np.asarray(inputs["L1_ffn_Ws_u"], dtype=f32).T)
    w["l1f_wsdT"] = np.ascontiguousarray(np.asarray(inputs["L1_ffn_Ws_d"], dtype=f32).T)

    w["l1a_wpT"] = np.ascontiguousarray(np.asarray(inputs["L1_attn_w_proj"], dtype=f32).T)
    w["l1a_bp_row"] = np.ascontiguousarray(inputs["L1_attn_b_proj"], dtype=f32).reshape(1, D)
    wq4 = np.asarray(inputs["L1_attn_w_qkvg"], dtype=f32)
    bq4 = np.asarray(inputs["L1_attn_b_qkvg"], dtype=f32)

    x = np.asarray(inputs["L0_attn_x"], dtype=f32)
    per_core = []
    for c in range(NCORES):
        bidx, h = c // 2, c % 2
        m = dict(w)
        m["x_in"] = np.ascontiguousarray(x[bidx, h * T:(h + 1) * T, :])
        rows = np.empty((4, 4, 2, 64), dtype=np.int64)
        for k in range(4):
            for pq in range(4):
                for hl in range(2):
                    hh = h * 8 + pq * 2 + hl
                    rows[k, pq, hl] = k * 1024 + hh * 64 + np.arange(64)
        ridx = rows.reshape(-1)
        m["l1a_wT"] = np.ascontiguousarray(wq4[ridx].T)
        m["l1a_b"] = np.ascontiguousarray(
            bq4[ridx].reshape(4, 4, 2 * 64).transpose(2, 1, 0).reshape(128, 16))
        per_core.append(m)
    return per_core


def launch_keys(phases):
    keys = ["x_in"]
    if "l0ffn" in phases:
        keys += ["l0_rw", "l0_rb_row", "l0_w1", "l0_w2"]
    if "l0attn" in phases:
        keys += ["l0_wqT", "l0_wkT", "l0_wvT", "l0_woT",
                 "l0_bq", "l0_bk", "l0_bv", "l0_bo_row"]
    if "l1ffn" in phases:
        keys += ["l1f_gwT", "l1f_gb_row", "l1f_wgT", "l1f_wuT", "l1f_wdT",
                 "l1f_wsgT", "l1f_wsuT", "l1f_wsdT"]
    if "l1attn" in phases:
        keys += ["l1a_wT", "l1a_b", "l1a_wpT", "l1a_bp_row"]
    return keys


def run_launch(nc, per_core, keys):
    in_maps = [{k: m[k] for k in keys} for m in per_core]
    res = run_bass_kernel_spmd(nc, in_maps, core_ids=list(range(NCORES)))
    return [r["y"] for r in res.results]


FULL_PHASES = ["l0ffn", "l0attn", "l0ffn", "l1ffn", "l1attn", "l1ffn"]


def kernel(**inputs):
    per_core = prep_inputs(inputs)
    nc = build_program(FULL_PHASES)
    ys = run_launch(nc, per_core, launch_keys(FULL_PHASES))
    out = np.empty((4, TK, D), dtype=np.float32)
    for c in range(NCORES):
        bidx, h = c // 2, c % 2
        out[bidx, h * T:(h + 1) * T, :] = ys[c]
    return out


# version 07 (final)
# speedup vs baseline: 1.1849x; 1.0530x over previous
import numpy as np
import ml_dtypes
import concourse.bass as bass
import concourse.mybir as mybir
from concourse.bass_utils import run_bass_kernel_spmd

F32 = mybir.dt.float32
BF16 = mybir.dt.bfloat16
AF = mybir.ActivationFunctionType
ALU = mybir.AluOpType
AX = mybir.AxisListType

NCORES = 8
D = 1024
DC = 8
T = 2048
NT = 16
TK = 4096
NTK = 32
NH = 16
HD = 64
E0 = 8
F0 = 4096
E1 = 16
I1 = 512
SH = 1024
EPS = 1e-5
ISQRT_D = 1.0 / 32.0
L1_SCALE = HD ** -0.5
RG = [[0, 1], [2, 3], [4, 5], [6, 7]]

NEG = -1.0e30
DEBUG_ATTN = False

_UID = [0]


def uid():
    _UID[0] += 1
    return _UID[0]


class Sem:
    def __init__(self, h):
        self.h = h
        self.v = 0

    def inc(self, n=1):
        self.v += n
        return self.v


def plan(sem, n, step=1):
    base = sem.v
    sem.v += n * step
    return [base + (i + 1) * step for i in range(n)]


class Ctx:

    def __init__(self, nc, block, sems):
        self.nc = nc
        self.block = block
        for k, v in sems.items():
            setattr(self, k, v)

    def barrier_vals(self):
        return [(s, s.v) for s in (self.sT, self.sV, self.sA, self.sG, self.sW,
                                   self.sX, self.sW0, self.sW1, self.sW2,
                                   self.sW3, self.sW4, self.sW5, self.sP)]


def emit_barrier(ctx):
    vals = ctx.barrier_vals()
    b = ctx.block

    def body(eng):
        for s, v in vals:
            if v > 0:
                eng.wait_ge(s.h, v)

    b.tensor(body)
    b.vector(body)
    b.scalar(body)
    b.gpsimd(body)
    b.sync(body)



def emit_ln(ctx, r, lnT):
    nc, b = ctx.nc, ctx.block
    with (nc.psum_tensor(f"tr_ps0_{uid()}", [128, 128], BF16) as tr_ps0,
          nc.psum_tensor(f"tr_ps1_{uid()}", [128, 128], BF16) as tr_ps1):
        _emit_ln_inner(ctx, r, lnT, [tr_ps0[:, :], tr_ps1[:, :]])
    emit_barrier(ctx)


def _emit_ln_inner(ctx, r, lnT, tr_ps):
    nc, b = ctx.nc, ctx.block
    r = dict(r)
    r["tr_ps"] = tr_ps
    x = r["x"]
    mean, rstd = r["mean"], r["rstd"]
    sq_a = plan(ctx.sA, NT)
    rs_a = plan(ctx.sA, NT)
    ev_a = plan(ctx.sA, NT * DC)
    st_v = plan(ctx.sV, NT)
    ln_v = plan(ctx.sV, NT)
    tr_t = plan(ctx.sT, NT * DC)
    ev = [(ctx.sA, v) for v in ev_a]

    def scalar1(sc):
        for t in range(NT):
            sc.activation(r["dump"][:, :], x[:, t * D:(t + 1) * D], AF.Square,
                          accum_out=r["s2"][:, t:t + 1]).then_inc(ctx.sA.h, 1)
        for t in range(NT):
            sc.wait_ge(ctx.sV.h, st_v[t])
            sc.activation(r["tc2"][:, 0:1], r["s1"][:, t:t + 1], AF.Ln)
            sc.drain()
            sc.activation(rstd[:, t:t + 1], r["tc2"][:, 0:1], AF.Exp,
                          scale=-0.5).then_inc(ctx.sA.h, 1)
        for t in range(NT):
            for dc in range(DC):
                idx = t * DC + dc
                sc.wait_ge(ctx.sT.h, tr_t[idx])
                sc.copy(lnT[:, dc * T + t * 128: dc * T + (t + 1) * 128],
                        r["tr_ps"][idx % 2][:, :]).then_inc(ctx.sA.h, 1)

    b.scalar(scalar1)

    def vector1(v):
        for t in range(NT):
            v.wait_ge(ctx.sA.h, sq_a[t])
            v.reduce_sum(r["s1"][:, t:t + 1], x[:, t * D:(t + 1) * D], axis=AX.X)
            v.drain()
            v.tensor_scalar_mul(mean[:, t:t + 1], r["s1"][:, t:t + 1], 1.0 / D)
            v.drain()
            v.tensor_mul(r["tc"][:, 0:1], mean[:, t:t + 1], mean[:, t:t + 1])
            v.tensor_scalar(r["s1"][:, t:t + 1], r["s2"][:, t:t + 1], 1.0 / D,
                            EPS, ALU.mult, ALU.add)
            v.drain()
            v.tensor_sub(r["s1"][:, t:t + 1], r["s1"][:, t:t + 1],
                         r["tc"][:, 0:1]).then_inc(ctx.sV.h, 1)
        for t in range(NT):
            v.wait_ge(ctx.sA.h, rs_a[t])
            if t >= 1:
                v.wait_ge(ctx.sT.h, tr_t[8 * t - 1])
            v.tensor_scalar(r["ln_tok"][:, 0:D], x[:, t * D:(t + 1) * D],
                            mean[:, t:t + 1], rstd[:, t:t + 1],
                            ALU.subtract, ALU.mult).then_inc(ctx.sV.h, 1)

    b.vector(vector1)

    def tensor1(te):
        for t in range(NT):
            te.wait_ge(ctx.sV.h, ln_v[t])
            for dc in range(DC):
                idx = t * DC + dc
                if idx >= 2:
                    te.wait_ge(ev[idx - 2][0].h, ev[idx - 2][1])
                te.transpose(r["tr_ps"][idx % 2][:, :],
                             r["ln_tok"][:, dc * 128:(dc + 1) * 128],
                             r["ident"][:, :]).then_inc(ctx.sT.h, 1)

    b.tensor(tensor1)



def emit_l0ffn(ctx, r, io):
    nc, b = ctx.nc, ctx.block
    x = r["x"]

    from contextlib import ExitStack
    with ExitStack() as es:
        ec = es.enter_context
        arena = ec(nc.sbuf_tensor(f"ffn0_arena_{uid()}", [128, (DC * T) + 32 * 1024], BF16))
        w1buf = ec(nc.sbuf_tensor(f"w1buf_{uid()}", [128, 3, DC, 128], BF16))
        w2buf = ec(nc.sbuf_tensor(f"w2buf_{uid()}", [128, 2, 8, 1024], BF16))
        rw_sb = ec(nc.sbuf_tensor(f"rw_sb_{uid()}", [128, DC, 9], BF16))
        rb_sb = ec(nc.sbuf_tensor(f"rb_sb_{uid()}", [1, 9], BF16))
        lnT = arena[:, 0:DC * T]
        h_sb = arena[:, DC * T:DC * T + 32 * 1024]

        emit_ln(ctx, r, lnT)
        router_es = ExitStack()
        r_ps = router_es.enter_context(nc.psum_tensor(f"r_ps_{uid()}", [128, 9], F32))

        coef = r["coef"]
        logit_sb = r["dump"][:, 0:NT * 9]
        prob_sb = r["dump"][:, 256:256 + NT * 9]
        m16a = r["dump"][:, 512:528]
        m16b = r["dump"][:, 528:544]

        ctx.sP.inc(32)

        def g_pre(g):
            g.dma_start(out=rw_sb[:, :, :],
                        in_=io["l0_rw"].rearrange("(dc p) n -> p dc n", p=128)
                        ).then_inc(ctx.sP.h, 16)
            g.dma_start(out=rb_sb[:, :],
                        in_=io["l0_rb_row"]).then_inc(ctx.sP.h, 16)

        b.gpsimd(g_pre)
        w_pre = ctx.sP.v

        rt_t = plan(ctx.sT, NT)
        rt_ev = plan(ctx.sA, NT)
        exp_a = plan(ctx.sA, 1)
        sub_v = plan(ctx.sV, 1)
        gate_done = plan(ctx.sV, 1)
        skip_v = plan(ctx.sV, NT)

        def t_router(te):
            te.wait_ge(ctx.sP.h, w_pre)
            for t in range(NT):
                if t >= 1:
                    te.wait_ge(ctx.sA.h, rt_ev[t - 1])
                for dc in range(DC):
                    te.matmul(r_ps[:, :],
                              lnT[:, dc * T + t * 128: dc * T + (t + 1) * 128],
                              rw_sb[:, dc, :],
                              start=(dc == 0), stop=False)
                te.matmul(r_ps[:, :], r["ones"][0:1, 0:128], rb_sb[0:1, :],
                          start=False, stop=True).then_inc(ctx.sT.h, 1)

        b.tensor(t_router)

        def s_router(sc):
            for t in range(NT):
                sc.wait_ge(ctx.sT.h, rt_t[t])
                sc.copy(logit_sb[:, t * 9:(t + 1) * 9], r_ps[:, :]
                        ).then_inc(ctx.sA.h, 1)

        b.scalar(s_router)

        def s_exp(sc):
            sc.wait_ge(ctx.sV.h, sub_v[0])
            sc.activation(prob_sb[:, :], prob_sb[:, :], AF.Exp
                          ).then_inc(ctx.sA.h, 1)

        def v_gate(v):
            L3 = logit_sb[:, :].rearrange("p (t e) -> p t e", e=9)
            P = prob_sb[:, :]
            P3 = P.rearrange("p (t e) -> p t e", e=9)
            v.wait_ge(ctx.sA.h, rt_ev[NT - 1])
            v.reduce_max(m16a[:, :], L3, axis=AX.X)
            v.drain()
            v.tensor_tensor(P3, L3, m16a[:, :].rearrange("p t -> p t ()")
                            .broadcast_to([128, NT, 9]), ALU.subtract
                            ).then_inc(ctx.sV.h, 1)
            v.wait_ge(ctx.sA.h, exp_a[0])
            v.reduce_max(m16a[:, :], P3, axis=AX.X)
            v.drain()
            v.tensor_tensor(L3, P3, m16a[:, :].rearrange("p t -> p t ()")
                            .broadcast_to([128, NT, 9]), ALU.is_ge)
            v.drain()
            v.tensor_scalar(logit_sb[:, :], logit_sb[:, :], NEG, None,
                            ALU.mult, ALU.bypass)
            v.drain()
            v.tensor_add(logit_sb[:, :], logit_sb[:, :], P)
            v.drain()
            v.reduce_max(m16b[:, :], L3, axis=AX.X)
            v.drain()
            v.tensor_tensor(L3, P3, m16b[:, :].rearrange("p t -> p t ()")
                            .broadcast_to([128, NT, 9]), ALU.is_ge)
            v.drain()
            v.tensor_mul(logit_sb[:, :], logit_sb[:, :], P)
            v.reduce_sum(m16a[:, :], P3, axis=AX.X)
            v.drain()
            v.reciprocal(m16b[:, :], m16a[:, :])
            v.drain()
            v.tensor_tensor(L3, L3, m16b[:, :].rearrange("p t -> p t ()")
                            .broadcast_to([128, NT, 9]), ALU.mult)
            v.drain()
            cview = coef.rearrange("p (t e) -> p t e", e=16)[:, :, 0:9]
            v.tensor_scalar(cview, L3, 0.5, None, ALU.mult,
                            ALU.bypass).then_inc(ctx.sV.h, 1)
            v.drain()
            for t in range(NT):
                v.tensor_scalar(r["tc"][:, 0:1], coef[:, t * 16 + 8:t * 16 + 9],
                                r["rstd"][:, t:t + 1], None, ALU.mult, ALU.bypass)
                v.drain()
                v.tensor_scalar(r["dump"][:, 0:D], x[:, t * D:(t + 1) * D],
                                r["mean"][:, t:t + 1], r["tc"][:, 0:1],
                                ALU.subtract, ALU.mult)
                v.drain()
                v.tensor_add(x[:, t * D:(t + 1) * D], x[:, t * D:(t + 1) * D],
                             r["dump"][:, 0:D]).then_inc(ctx.sV.h, 1)

        b.vector(v_gate)
        b.scalar(s_exp)

        router_es.close()
        h_ps0 = ec(nc.psum_tensor(f"h_ps0_{uid()}", [128, 1024], F32))
        h_ps1 = ec(nc.psum_tensor(f"h_ps1_{uid()}", [128, 1024], F32))
        y_ps0 = ec(nc.psum_tensor(f"y_ps0_{uid()}", [128, 1024], F32))
        y_ps1 = ec(nc.psum_tensor(f"y_ps1_{uid()}", [128, 1024], F32))
        h_ps = [h_ps0, h_ps1]
        y_ps = [y_ps0, y_ps1]

        n_eh = E0 * 2
        w1_dma, w2_dma = [], []
        w1_sem = [ctx.sW0, ctx.sW1, ctx.sW4]
        w2_sem = [ctx.sW2, ctx.sW3]
        for i in range(n_eh * 32):
            w1_dma += plan(w1_sem[i % 3], 1, step=16)
        for j in range(n_eh * 4):
            w2_dma += plan(w2_sem[j % 2], 1, step=16)
        h_mm, y_mm = [], []
        for _eh in range(n_eh):
            h_mm += plan(ctx.sT, 32)
            y_mm += plan(ctx.sT, 32)
        h_ev = plan(ctx.sA, n_eh * 32)
        y_add = plan(ctx.sV, n_eh * 32)

        def g_weights(g):
            for e in range(E0):
                for half in range(2):
                    i_eh = e * 2 + half
                    for ft in range(32):
                        i = i_eh * 32 + ft
                        if i >= 3:
                            g.wait_ge(ctx.sT.h, h_mm[i - 3])
                        g.dma_start(
                            out=w1buf[:, i % 3, :, :],
                            in_=io["l0_w1"][e].rearrange("(dc p) f -> p dc f", p=128)
                            [:, :, ft * 128:(ft + 1) * 128],
                        ).then_inc(w1_sem[i % 3].h, 16)
                    for fcg in range(4):
                        j = i_eh * 4 + fcg
                        if j >= 2:
                            g.wait_ge(ctx.sT.h, y_mm[(j - 2) * 8 + 7])
                        g.dma_start(
                            out=w2buf[:, j % 2, :, :],
                            in_=io["l0_w2"][e][fcg * 1024:(fcg + 1) * 1024, :]
                            .rearrange("(c p) n -> p c n", p=128),
                        ).then_inc(w2_sem[j % 2].h, 16)

        b.gpsimd(g_weights)

        def t_exp(te):
            te.wait_ge(ctx.sA.h, rt_ev[NT - 1])
            for e in range(E0):
                for half in range(2):
                    i_eh = e * 2 + half
                    for ft in range(32):
                        i = i_eh * 32 + ft
                        te.wait_ge(w1_sem[i % 3].h, w1_dma[i])
                        if i >= 2:
                            te.wait_ge(ctx.sA.h, h_ev[i - 2])
                        for dc in range(DC):
                            for nb in range(2):
                                ins = te.matmul(
                                    h_ps[i % 2][:, nb * 512:(nb + 1) * 512],
                                    w1buf[:, i % 3, dc, :],
                                    lnT[:, dc * T + half * 1024 + nb * 512:
                                        dc * T + half * 1024 + (nb + 1) * 512],
                                    start=(dc == 0), stop=(dc == DC - 1))
                                if dc == DC - 1 and nb == 1:
                                    ins.then_inc(ctx.sT.h, 1)
                    for fcg in range(4):
                        j = i_eh * 4 + fcg
                        te.wait_ge(w2_sem[j % 2].h, w2_dma[j])
                        te.wait_ge(ctx.sA.h, h_ev[i_eh * 32 + 31])
                        for tt in range(8):
                            jj = j * 8 + tt
                            if jj >= 2:
                                te.wait_ge(ctx.sV.h, y_add[jj - 2])
                            for fc in range(8):
                                fcol = (fcg * 8 + fc) * 1024 + tt * 128
                                for nb in range(2):
                                    ins = te.matmul(
                                        y_ps[jj % 2][:, nb * 512:(nb + 1) * 512],
                                        h_sb[:, fcol:fcol + 128],
                                        w2buf[:, j % 2, fc, nb * 512:(nb + 1) * 512],
                                        start=(fc == 0), stop=(fc == 7))
                                    if fc == 7 and nb == 1:
                                        ins.then_inc(ctx.sT.h, 1)

        b.tensor(t_exp)

        def s_gelu(sc):
            for e in range(E0):
                for half in range(2):
                    i_eh = e * 2 + half
                    for ft in range(32):
                        i = i_eh * 32 + ft
                        sc.wait_ge(ctx.sT.h, h_mm[i])
                        sc.activation(h_sb[:, ft * 1024:(ft + 1) * 1024],
                                      h_ps[i % 2][:, :], AF.Gelu
                                      ).then_inc(ctx.sA.h, 1)

        b.scalar(s_gelu)

        def v_acc(v):
            for e in range(E0):
                for half in range(2):
                    i_eh = e * 2 + half
                    for fcg in range(4):
                        j = i_eh * 4 + fcg
                        for tt in range(8):
                            jj = j * 8 + tt
                            v.wait_ge(ctx.sT.h, y_mm[jj])
                            g = half * 8 + tt
                            v.scalar_tensor_tensor(
                                x[:, g * D:(g + 1) * D], y_ps[jj % 2][:, :],
                                coef[:, g * 16 + e:g * 16 + e + 1],
                                x[:, g * D:(g + 1) * D],
                                ALU.mult, ALU.add).then_inc(ctx.sV.h, 1)

        b.vector(v_acc)

        emit_barrier(ctx)



def emit_l0attn(ctx, r, io, cc):
    nc, b = ctx.nc, ctx.block
    from contextlib import ExitStack
    with ExitStack() as es:
        ec = es.enter_context
        C0 = 32768 + 16384
        VO = 32 * 130
        arena = ec(nc.sbuf_tensor(f"attn_arena_{uid()}", [128, C0 + VO + 4096 + 4096], BF16))
        lnT = arena[:, 0:DC * T]
        kvT = arena[:, 0:32768].rearrange("p (dc t) -> p dc t", dc=DC)
        qT = arena[:, 32768:32768 + 16384]
        Vo = arena[:, C0:C0 + VO]
        vT = arena[:, C0 + VO:C0 + VO + 4096]
        kT = vT
        E2 = [arena[:, C0 + VO + 4096 + i * 1024:C0 + VO + 4096 + (i + 1) * 1024]
              for i in range(3)]
        o_sb = ec(nc.sbuf_tensor(f"o_sb_{uid()}", [128, 2048], BF16))
        wq_sb = ec(nc.sbuf_tensor(f"wq_sb_{uid()}", [128, 2, DC, 128], BF16))
        wv_sb = ec(nc.sbuf_tensor(f"wv_sb_{uid()}", [128, DC, 128], BF16))
        wk_sb = ec(nc.sbuf_tensor(f"wk_sb_{uid()}", [128, DC, 128], BF16))
        wo_sb = ec(nc.sbuf_tensor(f"wo_sb_{uid()}", [128, 1024], BF16))
        bq_sb = ec(nc.sbuf_tensor(f"bq_sb_{uid()}", [128, 8], F32))
        bk_sb = ec(nc.sbuf_tensor(f"bk_sb_{uid()}", [128, 8], F32))
        bv_sb = ec(nc.sbuf_tensor(f"bv_sb_{uid()}", [128, 8], F32))
        o_f = r["dump"][0:64, 0:512]
        dsc = r["dump"][0:64, 512:768].bitcast(BF16)[0:1, :]
        bo_sb = r["ln_tok"][0:1, 0:1024]

        emit_ln(ctx, r, lnT)

        snd_x = plan(ctx.sX, 4, step=16)

        def q_send(g):
            g.dma_start(out=cc["in"].ap().rearrange("(dc p) t -> p dc t", p=128),
                        in_=lnT.rearrange("p (dc t) -> p dc t", dc=DC)
                        ).then_inc(ctx.sX.h, 16)
            g.dma_start(out=bq_sb[:, :], in_=io["l0_bq"]).then_inc(ctx.sX.h, 16)
            g.dma_start(out=bk_sb[:, :], in_=io["l0_bk"]).then_inc(ctx.sX.h, 16)
            g.dma_start(out=bv_sb[:, :], in_=io["l0_bv"]).then_inc(ctx.sX.h, 16)

        b.sync(q_send)

        cc_done = plan(ctx.sG, 1)
        bo_dma = plan(ctx.sP, 1, step=16)
        wq_sem = [ctx.sW0, ctx.sW1]
        wq_dma = []
        for i in range(8):
            wq_dma += plan(wq_sem[i % 2], 1, step=16)
        q_mm = plan(ctx.sT, 8 * 4)
        q_ev = plan(ctx.sA, 8 * 4)

        def g_cc(g):
            g.wait_ge(ctx.sX.h, snd_x[3])
            g.collective_compute(
                "AllGather", ALU.bypass, replica_groups=cc["rg"],
                ins=[cc["in"].ap().opt()], outs=[cc["out"].ap().opt()],
            ).then_inc(ctx.sG.h, 1)
            g.dma_start(out=bo_sb[:, :], in_=io["l0_bo_row"]).then_inc(ctx.sP.h, 16)
            for pp in range(8):
                if pp >= 2:
                    g.wait_ge(ctx.sT.h, q_mm[(pp - 2) * 4 + 3])
                g.dma_start(out=wq_sb[:, pp % 2, :, :],
                            in_=io["l0_wqT"].rearrange("(dc p) f -> p dc f", p=128)
                            [:, :, pp * 128:(pp + 1) * 128]
                            ).then_inc(wq_sem[pp % 2].h, 16)

        b.gpsimd(g_cc)

        with (nc.psum_tensor(f"q_ps0_{uid()}", [128, 512], F32) as q_ps0,
              nc.psum_tensor(f"q_ps1_{uid()}", [128, 512], F32) as q_ps1):
            q_ps = [q_ps0[:, :], q_ps1[:, :]]

            def t_q(te):
                for pp in range(8):
                    te.wait_ge(wq_sem[pp % 2].h, wq_dma[pp])
                    for tb in range(4):
                        i = pp * 4 + tb
                        if i >= 2:
                            te.wait_ge(ctx.sA.h, q_ev[i - 2])
                        for dc in range(DC):
                            ins = te.matmul(q_ps[i % 2], wq_sb[:, pp % 2, dc, :],
                                            lnT[:, dc * T + tb * 512:dc * T + (tb + 1) * 512],
                                            start=(dc == 0), stop=(dc == DC - 1))
                        ins.then_inc(ctx.sT.h, 1)

            b.tensor(t_q)

            def s_q(sc):
                sc.wait_ge(ctx.sX.h, snd_x[3])
                for pp in range(8):
                    for tb in range(4):
                        i = pp * 4 + tb
                        sc.wait_ge(ctx.sT.h, q_mm[i])
                        sc.activation(qT[:, pp * 2048 + tb * 512:pp * 2048 + (tb + 1) * 512],
                                      q_ps[i % 2], AF.Identity,
                                      bias=bq_sb[:, pp:pp + 1]).then_inc(ctx.sA.h, 1)

            b.scalar(s_q)

        kv_x = plan(ctx.sW, 2, step=16)

        def g_kv(g):
            g.wait_ge(ctx.sG.h, cc_done[0])
            g.wait_ge(ctx.sA.h, q_ev[-1])
            for rk in range(2):
                g.dma_start(
                    out=kvT[:, :, rk * 2048:(rk + 1) * 2048],
                    in_=cc["out"].ap()[rk * 1024:(rk + 1) * 1024, :]
                    .rearrange("(dc p) t -> p dc t", p=128),
                ).then_inc(ctx.sW.h, 16)

        b.gpsimd(g_kv)

        vo_init = plan(ctx.sV, 1)

        def v_voinit(v):
            v.memset(Vo.rearrange("p (kc h j) -> p kc h j", h=2, j=65)[:, :, :, 64:65],
                     1.0).then_inc(ctx.sV.h, 1)

        b.vector(v_voinit)

        st = dict(kvT=kvT, qT=qT, Vo=Vo, vT=vT, kT=kT, E2=E2, o_sb=o_sb,
                  wv_sb=wv_sb, wk_sb=wk_sb, wo_sb=wo_sb, bk_sb=bk_sb, bv_sb=bv_sb,
                  bo_sb=bo_sb, dsc=dsc, o_f=o_f, kv_x=kv_x, vo_init=vo_init,
                  bo_dma=bo_dma)
        prev_last_v, prev_last_t = None, None
        for pp in range(8):
            prev_last_v, prev_last_t = _l0attn_pair(ctx, r, io, pp, st,
                                                    prev_last_v, prev_last_t)
            if DEBUG_ATTN and pp == 0:
                dbg3 = nc.dram_tensor("dbg3", [128, 2048], BF16,
                                      kind="ExternalOutput").ap()
                dbg4 = nc.dram_tensor("dbg4", [128, 4096], BF16,
                                      kind="ExternalOutput").ap()
                dbg5 = nc.dram_tensor("dbg5", [128, 4160], BF16,
                                      kind="ExternalOutput").ap()
                dx = plan(ctx.sX, 3, step=16)

                def q_dbg(g):
                    g.wait_ge(ctx.sV.h, prev_last_v)
                    g.dma_start(out=dbg3, in_=o_sb[:, :]).then_inc(ctx.sX.h, 16)
                    g.dma_start(out=dbg4, in_=kT[:, :]).then_inc(ctx.sX.h, 16)
                    g.dma_start(out=dbg5, in_=Vo[:, :]).then_inc(ctx.sX.h, 16)

                b.sync(q_dbg)
                emit_barrier(ctx)

    emit_barrier(ctx)


def _l0attn_pair(ctx, r, io, pp, s, prev_last_v, prev_last_t):
    nc, b = ctx.nc, ctx.block
    from contextlib import ExitStack
    x = r["x"]
    kvT, qT, Vo, vT, kT, E2, o_sb = (s["kvT"], s["qT"], s["Vo"], s["vT"], s["kT"],
                                     s["E2"], s["o_sb"])

    wdma = plan(ctx.sW, 3, step=16)

    def g_w(g):
        if prev_last_t is not None:
            g.wait_ge(ctx.sT.h, prev_last_t)
        g.dma_start(out=s["wv_sb"][:, :, :],
                    in_=io["l0_wvT"].rearrange("(dc p) f -> p dc f", p=128)
                    [:, :, pp * 128:(pp + 1) * 128]).then_inc(ctx.sW.h, 16)
        g.dma_start(out=s["wk_sb"][:, :, :],
                    in_=io["l0_wkT"].rearrange("(dc p) f -> p dc f", p=128)
                    [:, :, pp * 128:(pp + 1) * 128]).then_inc(ctx.sW.h, 16)
        g.dma_start(out=s["wo_sb"][:, :],
                    in_=io["l0_woT"][pp * 128:(pp + 1) * 128, :]).then_inc(ctx.sW.h, 16)

    b.gpsimd(g_w)

    v_mm = plan(ctx.sT, 8)
    tr_mm = plan(ctx.sT, 32)
    k_mm = plan(ctx.sT, 8)
    v_ev = plan(ctx.sA, 8)
    k_ev = plan(ctx.sA, 8)
    tr_ev = plan(ctx.sV, 32)

    with ExitStack() as es:
        ec = es.enter_context
        vk_ps = [ec(nc.psum_tensor(f"vk_ps{i}_{pp}_{uid()}", [128, 512], F32))[:, :]
                 for i in range(4)]
        t_ps = [ec(nc.psum_tensor(f"t_ps{i}_{pp}_{uid()}", [128, 128], BF16))[:, :]
                for i in range(2)]

        def t_a(te):
            te.wait_ge(ctx.sW.h, wdma[2])
            if prev_last_v is not None:
                te.wait_ge(ctx.sV.h, prev_last_v)
            for tbg in range(2):
                for dc in range(DC):
                    for tb4 in range(4):
                        i = tbg * 4 + tb4
                        if dc == 0 and tbg == 1:
                            te.wait_ge(ctx.sA.h, v_ev[i - 4])
                        ins = te.matmul(vk_ps[tb4], s["wv_sb"][:, dc, :],
                                        kvT[:, dc, tbg * 2048 + tb4 * 512:
                                            tbg * 2048 + (tb4 + 1) * 512],
                                        start=(dc == 0), stop=(dc == DC - 1))
                        if dc == DC - 1:
                            ins.then_inc(ctx.sT.h, 1)
            for kc in range(32):
                te.wait_ge(ctx.sA.h, v_ev[kc // 4])
                if kc >= 2:
                    te.wait_ge(ctx.sV.h, tr_ev[kc - 2])
                if kc == 0:
                    te.wait_ge(ctx.sV.h, s["vo_init"][0])
                te.transpose(t_ps[kc % 2], vT[:, kc * 128:(kc + 1) * 128],
                             r["ident"][:, :]).then_inc(ctx.sT.h, 1)
            for tbg in range(2):
                for dc in range(DC):
                    for tb4 in range(4):
                        i = tbg * 4 + tb4
                        if dc == 0:
                            te.wait_ge(ctx.sA.h, v_ev[7] if tbg == 0 else k_ev[i - 4])
                        ins = te.matmul(vk_ps[tb4], s["wk_sb"][:, dc, :],
                                        kvT[:, dc, tbg * 2048 + tb4 * 512:
                                            tbg * 2048 + (tb4 + 1) * 512],
                                        start=(dc == 0), stop=(dc == DC - 1))
                        if dc == DC - 1:
                            ins.then_inc(ctx.sT.h, 1)

        b.tensor(t_a)

        def s_a(sc):
            for i in range(8):
                sc.wait_ge(ctx.sT.h, v_mm[i])
                sc.activation(vT[:, i * 512:(i + 1) * 512], vk_ps[i % 4],
                              AF.Identity, bias=s["bv_sb"][:, pp:pp + 1]
                              ).then_inc(ctx.sA.h, 1)
            for i in range(8):
                sc.wait_ge(ctx.sT.h, k_mm[i])
                if i == 0:
                    sc.wait_ge(ctx.sT.h, tr_mm[31])
                sc.activation(kT[:, i * 512:(i + 1) * 512], vk_ps[i % 4],
                              AF.Identity, bias=s["bk_sb"][:, pp:pp + 1]
                              ).then_inc(ctx.sA.h, 1)

        b.scalar(s_a)

        def v_a(v):
            for kc in range(32):
                v.wait_ge(ctx.sT.h, tr_mm[kc])
                dst = Vo[:, kc * 130:(kc + 1) * 130].rearrange(
                    "p (h j) -> p h j", h=2)[:, :, 0:64]
                src = t_ps[kc % 2].rearrange("p (h j) -> p h j", h=2)
                v.tensor_copy(dst, src).then_inc(ctx.sV.h, 1)

        b.vector(v_a)

    sc_mm, pv_mm, den_mm, ex_ev, den_v = [], [], [], [], []
    for qb in range(4):
        sq, pq = [None] * 32, [None] * 32
        sq[0] = plan(ctx.sT, 1)[0]
        sq[1] = plan(ctx.sT, 1)[0]
        sq[2] = plan(ctx.sT, 1)[0]
        for kc in range(32):
            pq[kc] = plan(ctx.sT, 1)[0]
            if kc + 3 < 32:
                sq[kc + 3] = plan(ctx.sT, 1)[0]
        sc_mm.append(sq)
        pv_mm.append(pq)
        den_mm.append(plan(ctx.sT, 2))
        ex_ev.append(plan(ctx.sA, 32))
        den_v.append(plan(ctx.sV, 6))
    p_mm = plan(ctx.sT, NT)
    p_add = plan(ctx.sV, NT)

    with ExitStack() as es:
        ec = es.enter_context
        s_ps = [ec(nc.psum_tensor(f"s_ps{i}_{pp}_{uid()}", [128, 1024], F32))[:, :]
                for i in range(3)]
        o_ps = [ec(nc.psum_tensor(f"o_ps{i}_{pp}_{uid()}", [65, 512], F32))[:, :]
                for i in range(2)]

        def t_b(te):
            def scores(qb, kc):
                if kc >= 3:
                    te.wait_ge(ctx.sA.h, ex_ev[qb][kc - 3])
                elif qb >= 1:
                    te.wait_ge(ctx.sA.h, ex_ev[qb - 1][29 + kc])
                    if kc == 0:
                        te.wait_ge(ctx.sV.h, den_v[qb - 1][5])
                else:
                    if kc == 0:
                        te.wait_ge(ctx.sA.h, k_ev[7])
                        te.wait_ge(ctx.sV.h, tr_ev[31])
                for hp in range(2):
                    ins = te.matmul(
                        s_ps[kc % 3][:, hp * 512:(hp + 1) * 512],
                        kT[hp * 64:(hp + 1) * 64, kc * 128:(kc + 1) * 128],
                        qT[hp * 64:(hp + 1) * 64,
                           pp * 2048 + qb * 512:pp * 2048 + (qb + 1) * 512],
                        start=True, stop=True)
                ins.then_inc(ctx.sT.h, 1)

            def pv(qb, kc):
                te.wait_ge(ctx.sA.h, ex_ev[qb][kc])
                for hp in range(2):
                    ins = te.matmul(
                        o_ps[hp],
                        Vo[:, kc * 130 + hp * 65:kc * 130 + hp * 65 + 65],
                        E2[kc % 3][:, hp * 512:(hp + 1) * 512],
                        start=(kc == 0), stop=(kc == 31))
                ins.then_inc(ctx.sT.h, 1)

            for qb in range(4):
                scores(qb, 0)
                scores(qb, 1)
                scores(qb, 2)
                for kc in range(32):
                    pv(qb, kc)
                    if kc + 3 < 32:
                        scores(qb, kc + 3)
                for hp in range(2):
                    te.wait_ge(ctx.sV.h, den_v[qb][hp * 3 + 0])
                    te.matmul(s_ps[0][0:64, 0:512], r["ones"][0:1, 0:64],
                              s["dsc"][0:1, :], start=True, stop=True
                              ).then_inc(ctx.sT.h, 1)
            te.wait_ge(ctx.sW.h, wdma[2])
            if pp == 0:
                te.wait_ge(ctx.sP.h, s["bo_dma"][0])
            for tt in range(NT):
                if tt == 0:
                    te.wait_ge(ctx.sV.h, den_v[3][5])
                if tt >= 2:
                    te.wait_ge(ctx.sV.h, p_add[tt - 2])
                for nb in range(2):
                    ins = te.matmul(s_ps[tt % 2][:, nb * 512:(nb + 1) * 512],
                                    o_sb[:, tt * 128:(tt + 1) * 128],
                                    s["wo_sb"][:, nb * 512:(nb + 1) * 512],
                                    start=True, stop=(pp != 0))
                    if pp == 0:
                        ins = te.matmul(s_ps[tt % 2][:, nb * 512:(nb + 1) * 512],
                                        r["ones"][0:1, 0:128],
                                        s["bo_sb"][0:1, nb * 512:(nb + 1) * 512],
                                        start=False, stop=True)
                ins.then_inc(ctx.sT.h, 1)

        b.tensor(t_b)

        def s_b(sc):
            for qb in range(4):
                for kc in range(32):
                    sc.wait_ge(ctx.sT.h, sc_mm[qb][kc])
                    if not (qb == 0 and kc < 3):
                        pq2, pkc = (qb, kc - 3) if kc >= 3 else (qb - 1, 29 + kc)
                        sc.wait_ge(ctx.sT.h, pv_mm[pq2][pkc])
                    sc.activation(E2[kc % 3][:, 0:1024], s_ps[kc % 3], AF.Exp,
                                  scale=ISQRT_D).then_inc(ctx.sA.h, 1)

        b.scalar(s_b)

        def v_b(v):
            for qb in range(4):
                for hp in range(2):
                    v.wait_ge(ctx.sT.h, pv_mm[qb][31])
                    with nc.allow_low_precision(reason="bf16 softmax denom"):
                        v.reciprocal(s["dsc"][0:1, :], o_ps[hp][64:65, :]
                                     ).then_inc(ctx.sV.h, 1)
                    v.tensor_copy(s["o_f"][:, :], o_ps[hp][0:64, :]
                                  ).then_inc(ctx.sV.h, 1)
                    v.wait_ge(ctx.sT.h, den_mm[qb][hp])
                    v.drain()
                    v.tensor_mul(
                        o_sb[hp * 64:(hp + 1) * 64, qb * 512:(qb + 1) * 512],
                        s["o_f"][:, :], s_ps[0][0:64, 0:512]).then_inc(ctx.sV.h, 1)
            for tt in range(NT):
                v.wait_ge(ctx.sT.h, p_mm[tt])
                v.tensor_add(x[:, tt * D:(tt + 1) * D], x[:, tt * D:(tt + 1) * D],
                             s_ps[tt % 2]).then_inc(ctx.sV.h, 1)

        b.vector(v_b)

    return p_add[-1], p_mm[-1]



def emit_l1ffn(ctx, r, io):
    nc, b = ctx.nc, ctx.block
    x = r["x"]
    from contextlib import ExitStack
    with ExitStack() as es:
        ec = es.enter_context
        arena = ec(nc.sbuf_tensor(f"ffn1_arena_{uid()}", [128, 32768], BF16))
        lnT = arena[:, 0:16384]
        hgs = arena[:, 16384:24576]
        act = arena[:, 24576:32768]
        gw_sb = ec(nc.sbuf_tensor(f"gw_{uid()}", [128, DC, 16], BF16))
        gbr_sb = ec(nc.sbuf_tensor(f"gbr_{uid()}", [1, 16], BF16))
        gb_sb = ec(nc.sbuf_tensor(f"gb_{uid()}", [128, 16], F32))
        scores = ec(nc.sbuf_tensor(f"scor_{uid()}", [128, NT * 16], F32))
        sfc = ec(nc.sbuf_tensor(f"sfc_{uid()}", [128, NT * 16], F32))
        msk = ec(nc.sbuf_tensor(f"msk_{uid()}", [128, NT * 16], F32))
        mskc = ec(nc.sbuf_tensor(f"mskc_{uid()}", [128, NT * 16], F32))
        gs4 = ec(nc.sbuf_tensor(f"gs4_{uid()}", [128, 64], F32))
        t4 = ec(nc.sbuf_tensor(f"t4_{uid()}", [128, 64], F32))
        gm16 = ec(nc.sbuf_tensor(f"gm16_{uid()}", [128, 16], F32))
        gn16 = ec(nc.sbuf_tensor(f"gn16_{uid()}", [128, 16], F32))
        gsc = ec(nc.sbuf_tensor(f"gsc_{uid()}", [128, NT * 16], F32))
        coef = r["coef"]

        emit_ln(ctx, r, lnT)

        gpre = plan(ctx.sP, 2, step=16)

        def g_pre(g):
            g.dma_start(out=gw_sb[:, :, :],
                        in_=io["l1f_gwT"].rearrange("(dc p) n -> p dc n", p=128)
                        ).then_inc(ctx.sP.h, 16)
            g.dma_start(out=gbr_sb[:, :], in_=io["l1f_gb_row"]).then_inc(ctx.sP.h, 16)

        b.gpsimd(g_pre)

        gbb_mm = plan(ctx.sT, 1)
        g_mm = plan(ctx.sT, NT)
        sig_a = plan(ctx.sA, NT)
        gbcp_v = plan(ctx.sV, 1)
        gate_v = plan(ctx.sV, NT)

        with nc.psum_tensor(f"g_ps_{uid()}", [128, 16], F32) as g_ps:
            def t_gate(te):
                te.wait_ge(ctx.sP.h, gpre[1])
                te.matmul(g_ps[:, :], r["ones"][0:1, 0:128], gbr_sb[0:1, :],
                          start=True, stop=True).then_inc(ctx.sT.h, 1)
                for t in range(NT):
                    if t == 0:
                        te.wait_ge(ctx.sV.h, gbcp_v[0])
                    else:
                        te.wait_ge(ctx.sA.h, sig_a[t - 1])
                    for dc in range(DC):
                        ins = te.matmul(g_ps[:, :],
                                        lnT[:, dc * T + t * 128:dc * T + (t + 1) * 128],
                                        gw_sb[:, dc, :],
                                        start=(dc == 0), stop=(dc == DC - 1))
                    ins.then_inc(ctx.sT.h, 1)

            b.tensor(t_gate)

            def s_gate(sc):
                for t in range(NT):
                    sc.wait_ge(ctx.sT.h, g_mm[t])
                    sc.activation(scores[:, t * 16:(t + 1) * 16], g_ps[:, :],
                                  AF.Sigmoid).then_inc(ctx.sA.h, 1)

            b.scalar(s_gate)

            def v_gate(v):
                v.wait_ge(ctx.sT.h, gbb_mm[0])
                v.tensor_copy(gb_sb[:, :], g_ps[:, :]).then_inc(ctx.sV.h, 1)
                v.wait_ge(ctx.sA.h, sig_a[NT - 1])
                sco = scores[:, :]
                s3 = sco.rearrange("p (t e) -> p t e", e=16)
                s4 = sco.rearrange("p (t g i) -> p t g i", g=4, i=4)
                gbb = gb_sb[:, :].rearrange("p (o e) -> p o e", o=1
                                            ).broadcast_to([128, NT, 16])
                A = sfc[:, :]
                A3 = A.rearrange("p (t e) -> p t e", e=16)
                A4 = A.rearrange("p (t g i) -> p t g i", g=4, i=4)
                B = msk[:, :]
                B3 = B.rearrange("p (t e) -> p t e", e=16)
                B4 = B.rearrange("p (t g i) -> p t g i", g=4, i=4)
                C = mskc[:, :]
                C3 = C.rearrange("p (t e) -> p t e", e=16)
                m64 = gs4[:, :]
                m64_3 = m64.rearrange("p (t g) -> p t g", g=4)
                n64 = t4[:, :]
                n64_3 = n64.rearrange("p (t g) -> p t g", g=4)
                m16 = gm16[:, :]
                n16 = gn16[:, :]
                gscf = gsc[:, :]
                gsc3 = gscf.rearrange("p (t e) -> p t e", e=16)
                v.tensor_tensor(A3, s3, gbb, ALU.add)
                v.drain()
                v.reduce_max(m64_3, A4, axis=AX.X)
                v.drain()
                v.tensor_tensor(B4, A4, m64_3.rearrange("p t g -> p t g ()")
                                .broadcast_to([128, NT, 4, 4]), ALU.is_ge)
                v.drain()
                v.tensor_scalar(B, B, NEG, None, ALU.mult, ALU.bypass)
                v.drain()
                v.tensor_add(B, B, A)
                v.drain()
                v.reduce_max(n64_3, B4, axis=AX.X)
                v.drain()
                v.tensor_add(n64, n64, m64)
                v.drain()
                v.reduce_max(m16, n64_3, axis=AX.X)
                v.drain()
                v.tensor_tensor(m64_3, n64_3, m16.rearrange("p t -> p t ()")
                                .broadcast_to([128, NT, 4]), ALU.is_ge)
                v.drain()
                v.tensor_scalar(m64, m64, NEG, None, ALU.mult, ALU.bypass)
                v.drain()
                v.tensor_add(m64, m64, n64)
                v.drain()
                v.reduce_max(n16, m64_3, axis=AX.X)
                v.drain()
                v.tensor_tensor(m64_3, n64_3, n16.rearrange("p t -> p t ()")
                                .broadcast_to([128, NT, 4]), ALU.is_ge)
                v.drain()
                v.tensor_scalar(m64, m64, 1.0, 1.0e30, ALU.subtract, ALU.mult)
                v.drain()
                v.tensor_tensor(B4, A4, m64_3.rearrange("p t g -> p t g ()")
                                .broadcast_to([128, NT, 4, 4]), ALU.add)
                v.drain()
                v.tensor_copy(C, B)
                v.drain()
                for it in range(4):
                    v.reduce_max(m16, B3, axis=AX.X)
                    v.drain()
                    if it < 3:
                        v.tensor_tensor(gsc3, B3,
                                        m16.rearrange("p t -> p t ()")
                                        .broadcast_to([128, NT, 16]), ALU.is_ge)
                        v.drain()
                        v.tensor_scalar(gscf, gscf, NEG, None, ALU.mult, ALU.bypass)
                        v.drain()
                        v.tensor_add(B, B, gscf)
                        v.drain()
                v.tensor_tensor(C3, C3, m16.rearrange("p t -> p t ()")
                                .broadcast_to([128, NT, 16]), ALU.is_ge)
                v.drain()
                v.tensor_mul(C, C, sco)
                v.drain()
                v.reduce_sum(n16, C3, axis=AX.X)
                v.drain()
                v.tensor_scalar_add(n16, n16, 1.0e-20)
                v.drain()
                v.reciprocal(m16, n16)
                v.drain()
                v.tensor_tensor(C3, C3, m16.rearrange("p t -> p t ()")
                                .broadcast_to([128, NT, 16]), ALU.mult)
                v.drain()
                v.tensor_scalar(coef[:, 0:NT * 16], C, 1.25, None,
                                ALU.mult, ALU.bypass)
                v.drain()
                for t in range(NT):
                    v.tensor_scalar_mul(r["tc"][:, 0:1], r["rstd"][:, t:t + 1], 0.5)
                    v.drain()
                    v.tensor_scalar(r["dump"][:, 0:D], x[:, t * D:(t + 1) * D],
                                    r["mean"][:, t:t + 1], r["tc"][:, 0:1],
                                    ALU.subtract, ALU.mult)
                    v.drain()
                    v.tensor_add(x[:, t * D:(t + 1) * D], x[:, t * D:(t + 1) * D],
                                 r["dump"][:, 0:D]).then_inc(ctx.sV.h, 1)

            b.vector(v_gate)

        emit_barrier(ctx)

        expert_es = ExitStack()
        wg_buf = expert_es.enter_context(
            nc.sbuf_tensor(f"wg_{uid()}", [128, 2, DC, 512], BF16))
        wu_buf = expert_es.enter_context(
            nc.sbuf_tensor(f"wu_{uid()}", [128, 2, DC, 512], BF16))
        wd_buf = expert_es.enter_context(
            nc.sbuf_tensor(f"wd_{uid()}", [128, 2, 4, 1024], BF16))
        we_sem = [ctx.sW0, ctx.sW1]
        up_mm, act_a, act_v, dn_mm, dn_v, wdma = [], [], [], [], [], []
        for e in range(E1):
            wdma.append(plan(we_sem[e % 2], 3, step=16))
            up_mm.append(plan(ctx.sT, 8))
            dn_mm.append(plan(ctx.sT, NT))
            act_a.append(plan(ctx.sA, 4))
            act_v.append(plan(ctx.sV, 4))
            dn_v.append(plan(ctx.sV, NT))

        def g_we(g):
            for e in range(E1):
                if e >= 2:
                    g.wait_ge(ctx.sV.h, dn_v[e - 2][-1])
                g.dma_start(out=wg_buf[:, e % 2, :, :],
                            in_=io["l1f_wgT"][e].rearrange("(dc p) n -> p dc n", p=128)
                            ).then_inc(we_sem[e % 2].h, 16)
                g.dma_start(out=wu_buf[:, e % 2, :, :],
                            in_=io["l1f_wuT"][e].rearrange("(dc p) n -> p dc n", p=128)
                            ).then_inc(we_sem[e % 2].h, 16)
                g.dma_start(out=wd_buf[:, e % 2, :, :],
                            in_=io["l1f_wdT"][e].rearrange("(ic p) n -> p ic n", p=128)
                            ).then_inc(we_sem[e % 2].h, 16)

        b.gpsimd(g_we)

        for e in range(E1):
            _l1ffn_expert(ctx, r, e, dict(
                lnT=lnT, hgs=hgs, act=act, coef=coef,
                wg=wg_buf[:, e % 2, :, :], wu=wu_buf[:, e % 2, :, :],
                wd=wd_buf[:, e % 2, :, :],
                wdma=wdma[e], up_mm=up_mm[e], dn_mm=dn_mm[e],
                act_a=act_a[e], act_v=act_v[e], dn_v=dn_v[e],
                we_sem=we_sem[e % 2],
                prev_dn_v=(dn_v[e - 1][-1] if e > 0 else None)))

        expert_es.close()
        _l1ffn_shared(ctx, r, io, dict(lnT=lnT, hgs=hgs, act=act,
                                       prev_dn_v=dn_v[-1][-1]))

    emit_barrier(ctx)


def _l1ffn_expert(ctx, r, e, s):
    nc, b = ctx.nc, ctx.block
    x = r["x"]
    lnT, hgs, act = s["lnT"], s["hgs"], s["act"]
    from contextlib import ExitStack
    with ExitStack() as es:
        ec = es.enter_context
        up_ps = [ec(nc.psum_tensor(f"up{i}_{e}_{uid()}", [128, 512], F32))[:, :]
                 for i in range(4)]

        def t_up(te):
            te.wait_ge(s["we_sem"].h, s["wdma"][2])
            if s["prev_dn_v"] is not None:
                te.wait_ge(ctx.sV.h, s["prev_dn_v"])
            for kind in range(2):
                w = s["wg"] if kind == 0 else s["wu"]
                for ic in range(4):
                    i = kind * 4 + ic
                    if kind == 1:
                        if ic >= 1:
                            te.wait_ge(ctx.sV.h, s["act_v"][ic - 1])
                        else:
                            te.wait_ge(ctx.sA.h, s["act_a"][3])
                    for dc in range(DC):
                        for tb in range(4):
                            if dc == 0 and i >= 1 and kind == 0 and tb == 0 and ic >= 1:
                                te.wait_ge(ctx.sA.h, s["act_a"][ic - 1])
                            ins = te.matmul(
                                up_ps[tb], w[:, dc, ic * 128:(ic + 1) * 128],
                                lnT[:, dc * T + tb * 512:dc * T + (tb + 1) * 512],
                                start=(dc == 0), stop=(dc == DC - 1))
                        if dc == DC - 1:
                            ins.then_inc(ctx.sT.h, 1)

        b.tensor(t_up)

        def s_up(sc):
            for ic in range(4):
                sc.wait_ge(ctx.sT.h, s["up_mm"][ic])
                for tb in range(4):
                    ins = sc.activation(hgs[:, ic * 2048 + tb * 512:
                                            ic * 2048 + (tb + 1) * 512],
                                        up_ps[tb], AF.Silu)
                ins.then_inc(ctx.sA.h, 1)

        b.scalar(s_up)

        def v_up(v):
            for ic in range(4):
                v.wait_ge(ctx.sT.h, s["up_mm"][4 + ic])
                for tb in range(4):
                    ins = v.tensor_tensor(
                        act[:, ic * 2048 + tb * 512:ic * 2048 + (tb + 1) * 512],
                        hgs[:, ic * 2048 + tb * 512:ic * 2048 + (tb + 1) * 512],
                        up_ps[tb], ALU.mult)
                ins.then_inc(ctx.sV.h, 1)

        b.vector(v_up)

    with ExitStack() as es:
        ec = es.enter_context
        y_ps = [ec(nc.psum_tensor(f"y{i}_{e}_{uid()}", [128, 1024], F32))[:, :]
                for i in range(2)]

        def t_dn(te):
            for tt in range(NT):
                if tt == 0:
                    te.wait_ge(ctx.sV.h, s["act_v"][3])
                if tt >= 2:
                    te.wait_ge(ctx.sV.h, s["dn_v"][tt - 2])
                for ic in range(4):
                    for nb in range(2):
                        ins = te.matmul(
                            y_ps[tt % 2][:, nb * 512:(nb + 1) * 512],
                            act[:, ic * 2048 + tt * 128:ic * 2048 + (tt + 1) * 128],
                            s["wd"][:, ic, nb * 512:(nb + 1) * 512],
                            start=(ic == 0), stop=(ic == 3))
                ins.then_inc(ctx.sT.h, 1)

        b.tensor(t_dn)

        def v_dn(v):
            for tt in range(NT):
                v.wait_ge(ctx.sT.h, s["dn_mm"][tt])
                v.scalar_tensor_tensor(
                    x[:, tt * D:(tt + 1) * D], y_ps[tt % 2],
                    s["coef"][:, tt * 16 + e:tt * 16 + e + 1],
                    x[:, tt * D:(tt + 1) * D],
                    ALU.mult, ALU.add).then_inc(ctx.sV.h, 1)

        b.vector(v_dn)


def _l1ffn_shared(ctx, r, io, s):
    nc, b = ctx.nc, ctx.block
    x = r["x"]
    lnT, hgs, act = s["lnT"], s["hgs"], s["act"]
    from contextlib import ExitStack
    actS = hgs.tensor if False else None
    with ExitStack() as es:
        ec = es.enter_context
        wsg_sb = ec(nc.sbuf_tensor(f"wsg_{uid()}", [128, 2, DC, 128], BF16))
        wsu_sb = ec(nc.sbuf_tensor(f"wsu_{uid()}", [128, 2, DC, 128], BF16))
        wsd_sb = ec(nc.sbuf_tensor(f"wsd_{uid()}", [128, DC, 1024], BF16))
        hgS = ec(nc.sbuf_tensor(f"hgS_{uid()}", [128, 2048], BF16))

        ws_sem = [ctx.sW2, ctx.sW3]
        wdma = []
        for sc8 in range(SH // 128):
            wdma.append(plan(ws_sem[sc8 % 2], 2, step=16))
        wsd_dma = plan(ctx.sP, 1, step=16)

        up_mm = [plan(ctx.sT, 2) for _ in range(8)]
        silu_a = [plan(ctx.sA, 4) for _ in range(8)]
        mul_v = [plan(ctx.sV, 4) for _ in range(8)]
        dn_mm = plan(ctx.sT, NT)
        dn_v = plan(ctx.sV, NT)

        def g_ws(g):
            g.wait_ge(ctx.sV.h, s["prev_dn_v"])
            g.dma_start(out=wsd_sb[:, :, :],
                        in_=io["l1f_wsdT"].rearrange("(sc p) n -> p sc n", p=128)
                        ).then_inc(ctx.sP.h, 16)
            for sc8 in range(8):
                if sc8 >= 2:
                    g.wait_ge(ctx.sV.h, mul_v[sc8 - 2][-1])
                g.dma_start(out=wsg_sb[:, sc8 % 2, :, :],
                            in_=io["l1f_wsgT"].rearrange("(dc p) n -> p dc n", p=128)
                            [:, :, sc8 * 128:(sc8 + 1) * 128]
                            ).then_inc(ws_sem[sc8 % 2].h, 16)
                g.dma_start(out=wsu_sb[:, sc8 % 2, :, :],
                            in_=io["l1f_wsuT"].rearrange("(dc p) n -> p dc n", p=128)
                            [:, :, sc8 * 128:(sc8 + 1) * 128]
                            ).then_inc(ws_sem[sc8 % 2].h, 16)

        b.gpsimd(g_ws)

        with ExitStack() as es2:
            up_ps = [es2.enter_context(
                nc.psum_tensor(f"ups{i}_{uid()}", [128, 512], F32))[:, :]
                for i in range(8)]

            def t_sup(te):
                if s["prev_dn_v"] is not None:
                    te.wait_ge(ctx.sV.h, s["prev_dn_v"])
                for sc8 in range(8):
                    te.wait_ge(ws_sem[sc8 % 2].h, wdma[sc8][1])
                    for kind in range(2):
                        w = wsg_sb if kind == 0 else wsu_sb
                        ps_base = 0 if kind == 0 else 4
                        if kind == 0 and sc8 >= 1:
                            te.wait_ge(ctx.sA.h, silu_a[sc8 - 1][-1])
                        if kind == 1 and sc8 >= 1:
                            te.wait_ge(ctx.sV.h, mul_v[sc8 - 1][-1])
                        for dc in range(DC):
                            for tb in range(4):
                                ins = te.matmul(
                                    up_ps[ps_base + tb],
                                    w[:, sc8 % 2, dc, :],
                                    lnT[:, dc * T + tb * 512:dc * T + (tb + 1) * 512],
                                    start=(dc == 0), stop=(dc == DC - 1))
                            if dc == DC - 1:
                                ins.then_inc(ctx.sT.h, 1)

            b.tensor(t_sup)

            def s_sup(sc):
                for sc8 in range(8):
                    sc.wait_ge(ctx.sT.h, up_mm[sc8][0])
                    if sc8 >= 1:
                        sc.wait_ge(ctx.sV.h, mul_v[sc8 - 1][-1])
                    for tb in range(4):
                        sc.activation(hgS[:, tb * 512:(tb + 1) * 512],
                                      up_ps[tb], AF.Silu).then_inc(ctx.sA.h, 1)

            b.scalar(s_sup)

            def v_sup(v):
                for sc8 in range(8):
                    v.wait_ge(ctx.sT.h, up_mm[sc8][1])
                    for tb in range(4):
                        v.wait_ge(ctx.sA.h, silu_a[sc8][tb])
                        dst = hgs if sc8 < 4 else act
                        v.tensor_tensor(
                            dst[:, (sc8 % 4) * 2048 + tb * 512:
                                (sc8 % 4) * 2048 + (tb + 1) * 512],
                            hgS[:, tb * 512:(tb + 1) * 512],
                            up_ps[4 + tb], ALU.mult).then_inc(ctx.sV.h, 1)

            b.vector(v_sup)

        with ExitStack() as es3:
            y_ps = [es3.enter_context(
                nc.psum_tensor(f"ys{i}_{uid()}", [128, 1024], F32))[:, :]
                for i in range(2)]

            def t_sdn(te):
                te.wait_ge(ctx.sP.h, wsd_dma[0])
                for tt in range(NT):
                    if tt == 0:
                        te.wait_ge(ctx.sV.h, mul_v[7][-1])
                    if tt >= 2:
                        te.wait_ge(ctx.sV.h, dn_v[tt - 2])
                    for sc8 in range(8):
                        src = hgs if sc8 < 4 else act
                        for nb in range(2):
                            ins = te.matmul(
                                y_ps[tt % 2][:, nb * 512:(nb + 1) * 512],
                                src[:, (sc8 % 4) * 2048 + tt * 128:
                                    (sc8 % 4) * 2048 + (tt + 1) * 128],
                                wsd_sb[:, sc8, nb * 512:(nb + 1) * 512],
                                start=(sc8 == 0), stop=(sc8 == 7))
                    ins.then_inc(ctx.sT.h, 1)

            b.tensor(t_sdn)

            def v_sdn(v):
                for tt in range(NT):
                    v.wait_ge(ctx.sT.h, dn_mm[tt])
                    v.scalar_tensor_tensor(
                        x[:, tt * D:(tt + 1) * D], y_ps[tt % 2], 0.5,
                        x[:, tt * D:(tt + 1) * D],
                        ALU.mult, ALU.add).then_inc(ctx.sV.h, 1)

            b.vector(v_sdn)



def emit_l1attn(ctx, r, io, cc):
    nc, b = ctx.nc, ctx.block
    from contextlib import ExitStack
    with ExitStack() as es:
        ec = es.enter_context
        arena = ec(nc.sbuf_tensor(f"l1a_arena_{uid()}",
                                  [128, 32768 + 16384 + 12544], BF16))
        lnT = arena[:, 0:DC * T]
        ln4T = arena[:, 0:32768].rearrange("p (dc t) -> p dc t", dc=DC)
        qkvgT = arena[:, 32768:32768 + 16384]
        flat_sb = arena[:, 32768:32768 + 8192]
        flatT_sb = arena[:, 32768 + 8192:32768 + 16384]
        qtok = arena[:, 49152:49152 + 4096]
        ktok = arena[:, 53248:53248 + 4096]
        fin = arena[:, 57344:57344 + 4096]
        ftok = arena[:, 61440:61440 + 256]
        wpT_sb = arena[:, 49152:49152 + 8192].rearrange("p (tp n) -> p tp n", tp=DC)
        wbuf = ec(nc.sbuf_tensor(f"l1aw_{uid()}", [128, DC, 4, 128], BF16))
        ksum = ec(nc.sbuf_tensor(f"ksum_{uid()}", [128, 64], BF16))
        b_sb = ec(nc.sbuf_tensor(f"l1ab_{uid()}", [128, 16], F32))
        bp_row = ec(nc.sbuf_tensor(f"l1abp_{uid()}", [1, D], BF16))
        M_sb = ec(nc.sbuf_tensor(f"msb_{uid()}", [128, 64], BF16))
        z_sb = ec(nc.sbuf_tensor(f"zsb_{uid()}", [128, 1], F32))
        zt_sb = ec(nc.sbuf_tensor(f"ztsb_{uid()}", [128, 1], F32))
        t1f = r["dump"][:, 512:1024]

        emit_ln(ctx, r, lnT)

        snd_x = plan(ctx.sX, 2, step=16)

        def q_send(g):
            g.dma_start(out=cc["in"].ap().rearrange("(dc p) t -> p dc t", p=128),
                        in_=lnT.rearrange("p (dc t) -> p dc t", dc=DC)
                        ).then_inc(ctx.sX.h, 16)
            g.dma_start(out=b_sb[:, :], in_=io["l1a_b"]).then_inc(ctx.sX.h, 16)

        b.sync(q_send)

        cc_done = plan(ctx.sG, 1)
        wp_dma = plan(ctx.sP, 1, step=16)
        kv_x = plan(ctx.sW, 2, step=16)

        def g_cc(g):
            g.wait_ge(ctx.sX.h, snd_x[1])
            g.collective_compute(
                "AllGather", ALU.bypass, replica_groups=cc["rg"],
                ins=[cc["in"].ap().opt()], outs=[cc["out"].ap().opt()],
            ).then_inc(ctx.sG.h, 1)
            g.dma_start(out=bp_row[:, :], in_=io["l1a_bp_row"]).then_inc(ctx.sP.h, 16)
            g.wait_ge(ctx.sG.h, cc_done[0])
            for rk in range(2):
                g.dma_start(
                    out=ln4T[:, :, rk * 2048:(rk + 1) * 2048],
                    in_=cc["out"].ap()[rk * 1024:(rk + 1) * 1024, :]
                    .rearrange("(dc p) t -> p dc t", p=128),
                ).then_inc(ctx.sW.h, 16)

        b.gpsimd(g_cc)

        st = dict(ln4T=ln4T, qkvgT=qkvgT, wbuf=wbuf, ksum=ksum, b_sb=b_sb,
                  M_sb=M_sb, z_sb=z_sb, zt_sb=zt_sb, t1f=t1f, qtok=qtok,
                  ktok=ktok, fin=fin, ftok=ftok, bounce=cc["bounce"],
                  kv_x=kv_x)
        prev = {"v": None, "t": None, "x": None}
        for pq in range(4):
            _l1attn_pair(ctx, r, io, pq, st, prev)

        _l1attn_proj(ctx, r, io, dict(flat=flat_sb, flatT=flatT_sb, wpT=wpT_sb,
                                      bp_row=bp_row, bounce=cc["bounce"],
                                      wp_dma=wp_dma, prev=prev))
        del wp_dma

    emit_barrier(ctx)


def _l1attn_pair(ctx, r, io, pq, s, prev):
    nc, b = ctx.nc, ctx.block
    from contextlib import ExitStack
    qkvgT = s["qkvgT"]

    wdma = plan(ctx.sW, 4, step=16)

    def g_w(g):
        if prev["t"] is not None:
            g.wait_ge(ctx.sT.h, prev["t"])
        for k in range(4):
            g.dma_start(out=s["wbuf"][:, :, k, :],
                        in_=io["l1a_wT"].rearrange("(dc p) (k c) -> p dc k c",
                                                   p=128, k=4)
                        [:, :, k, pq * 128:(pq + 1) * 128]
                        ).then_inc(ctx.sW.h, 16)

    b.gpsimd(g_w)

    pr_mm = plan(ctx.sT, 4 * 8)
    pr_ev = plan(ctx.sA, 4 * 8)
    sp_a = plan(ctx.sA, 8)
    sp_v = plan(ctx.sV, 8)

    with ExitStack() as es:
        ec = es.enter_context
        u_ps = [ec(nc.psum_tensor(f"u{i}_{pq}_{uid()}", [128, 512], F32))[:, :]
                for i in range(8)]

        def t_pr(te):
            te.wait_ge(ctx.sW.h, wdma[3])
            if prev["v"] is not None:
                te.wait_ge(ctx.sV.h, prev["v"])
            for kind in range(4):
                for dc in range(DC):
                    for tb in range(8):
                        i = kind * 8 + tb
                        if dc == 0 and kind >= 1:
                            te.wait_ge(ctx.sA.h, pr_ev[i - 8])
                        ins = te.matmul(
                            u_ps[tb], s["wbuf"][:, dc, kind, :],
                            s["ln4T"][:, dc, tb * 512:(tb + 1) * 512],
                            start=(dc == 0), stop=(dc == DC - 1))
                        if dc == DC - 1:
                            ins.then_inc(ctx.sT.h, 1)

        b.tensor(t_pr)

        def s_pr(sc):
            for kind in range(4):
                fn = AF.Identity if kind < 2 else AF.Gelu
                for tb in range(8):
                    i = kind * 8 + tb
                    sc.wait_ge(ctx.sT.h, pr_mm[i])
                    sc.activation(qkvgT[:, kind * 4096 + tb * 512:
                                        kind * 4096 + (tb + 1) * 512],
                                  u_ps[tb], fn,
                                  bias=s["b_sb"][:, pq * 4 + kind:pq * 4 + kind + 1]
                                  ).then_inc(ctx.sA.h, 1)
            for i8 in range(8):
                kq, qtr = i8 // 4, i8 % 4
                sl = qkvgT[:, kq * 4096 + qtr * 1024:kq * 4096 + (qtr + 1) * 1024]
                sc.wait_ge(ctx.sA.h, pr_ev[kq * 8 + qtr * 2 + 1])
                if i8 >= 1:
                    sc.wait_ge(ctx.sV.h, sp_v[i8 - 1])
                sc.activation(r["dump"][:, :], sl, AF.Abs)
                sc.drain()
                sc.activation(r["dump"][:, :], r["dump"][:, :], AF.Exp, scale=-1.0)
                sc.drain()
                sc.activation(r["dump"][:, :], r["dump"][:, :], AF.Ln, bias=1.0)
                sc.drain()
                sc.activation(r["ln_tok"][:, :], sl, AF.Relu
                              ).then_inc(ctx.sA.h, 1)

        b.scalar(s_pr)

        def v_pr(v):
            for i8 in range(8):
                kq, qtr = i8 // 4, i8 % 4
                sl = qkvgT[:, kq * 4096 + qtr * 1024:kq * 4096 + (qtr + 1) * 1024]
                v.wait_ge(ctx.sA.h, sp_a[i8])
                v.tensor_tensor(sl, r["ln_tok"][:, :], r["dump"][:, :],
                                ALU.add).then_inc(ctx.sV.h, 1)

        b.vector(v_pr)

    _l1attn_pair_core(ctx, r, io, pq, s, sp_v[-1], prev)


def _l1attn_pair_core(ctx, r, io, pq, s, sp_last, prev):
    nc, b = ctx.nc, ctx.block
    from contextlib import ExitStack
    qkvgT = s["qkvgT"]

    tr_mm = plan(ctx.sT, 64)
    tr_ev = []
    for i in range(64):
        if i < 32:
            tr_ev.append(plan(ctx.sV, 1)[-1])
        else:
            tr_ev.append(plan(ctx.sV, 3)[-1])
    mz_mm = plan(ctx.sT, 1)
    o1_mm = plan(ctx.sT, 8)
    ft_mm = plan(ctx.sT, 32)
    mz_v = plan(ctx.sV, 2)
    fin_v = plan(ctx.sV, 8 * 2)
    ft_v = plan(ctx.sV, 32)
    ft_x = plan(ctx.sX, 32, step=16)

    from contextlib import ExitStack as _ES
    esA = _ES()
    t_ps = [esA.enter_context(nc.psum_tensor(f"lt{i}_{pq}_{uid()}",
                                             [128, 128], BF16))[:, :]
            for i in range(2)]

    def t_tr(te):
        te.wait_ge(ctx.sV.h, sp_last)
        for kq in range(2):
            for tc in range(32):
                i = kq * 32 + tc
                if i >= 2:
                    te.wait_ge(ctx.sV.h, tr_ev[i - 2])
                te.transpose(t_ps[i % 2],
                             qkvgT[:, kq * 4096 + tc * 128:
                                   kq * 4096 + (tc + 1) * 128],
                             r["ident"][:, :]).then_inc(ctx.sT.h, 1)

    b.tensor(t_tr)

    def v_tr(v):
        for kq in range(2):
            for tc in range(32):
                i = kq * 32 + tc
                v.wait_ge(ctx.sT.h, tr_mm[i])
                dst = s["qtok"] if kq == 0 else s["ktok"]
                v.tensor_copy(dst[:, tc * 128:(tc + 1) * 128],
                              t_ps[i % 2]).then_inc(ctx.sV.h, 1)
                if kq == 1:
                    with nc.allow_low_precision(reason="bf16 ksum operand"):
                        for hl in range(2):
                            v.reduce_sum(s["ksum"][:, tc * 2 + hl:tc * 2 + hl + 1],
                                         t_ps[i % 2][:, hl * 64:(hl + 1) * 64],
                                         axis=AX.X).then_inc(ctx.sV.h, 1)

    b.vector(v_tr)
    esA.close()

    esB = _ES()
    M_ps = [esB.enter_context(nc.psum_tensor(f"lm{hl}_{pq}_{uid()}",
                                             [128, 64], F32))[:, :]
            for hl in range(2)]
    zd_ps = [esB.enter_context(nc.psum_tensor(f"lzd{hl}_{pq}_{uid()}",
                                              [128, 1], F32))[:, :]
             for hl in range(2)]

    def t_m(te):
        te.wait_ge(ctx.sV.h, tr_ev[63])
        for tc in range(32):
            for hl in range(2):
                tpos = (0, 64) if hl == 1 else None
                te.matmul(M_ps[hl][hl * 64:(hl + 1) * 64, :],
                          s["ktok"][:, tc * 128 + hl * 64:tc * 128 + hl * 64 + 64],
                          s["qtok"][:, tc * 128 + hl * 64:tc * 128 + hl * 64 + 64],
                          start=(tc == 0), stop=(tc == 31), tile_position=tpos)
                ins = te.matmul(zd_ps[hl][hl * 64:(hl + 1) * 64, :],
                                s["qtok"][:, tc * 128 + hl * 64:
                                          tc * 128 + hl * 64 + 64],
                                s["ksum"][:, tc * 2 + hl:tc * 2 + hl + 1],
                                start=(tc == 0), stop=(tc == 31),
                                tile_position=tpos)
        ins.then_inc(ctx.sT.h, 1)

    b.tensor(t_m)

    def v_m(v):
        v.wait_ge(ctx.sT.h, mz_mm[0])
        for hl in range(2):
            sl = slice(hl * 64, (hl + 1) * 64)
            v.tensor_copy(s["M_sb"][sl, :], M_ps[hl][sl, :])
            v.tensor_scalar(s["zt_sb"][sl, :], zd_ps[hl][sl, :], L1_SCALE,
                            float(D), ALU.mult, ALU.add)
        v.then_inc if False else None
        v.nop().then_inc(ctx.sV.h, 1)
        v.drain()
        with nc.allow_low_precision(reason="z gate reciprocal"):
            v.reciprocal(s["z_sb"][:, :], s["zt_sb"][:, :]).then_inc(ctx.sV.h, 1)

    b.vector(v_m)
    esB.close()

    esC = _ES()
    o1_ps = [[esC.enter_context(nc.psum_tensor(f"lo{i}{hl}_{pq}_{uid()}",
                                               [128, 512], F32))[:, :]
              for hl in range(2)] for i in range(2)]
    t_ps = [esC.enter_context(nc.psum_tensor(f"lt2{i}_{pq}_{uid()}",
                                             [128, 128], BF16))[:, :]
            for i in range(2)]

    def t_o(te):
        te.wait_ge(ctx.sV.h, mz_v[1])
        for tb in range(8):
            if tb >= 2:
                te.wait_ge(ctx.sV.h, fin_v[(tb - 2) * 2 + 1])
            for hl in range(2):
                tpos = (64, 64) if hl == 1 else None
                ins = te.matmul(
                    o1_ps[tb % 2][hl][hl * 64:(hl + 1) * 64, :],
                    s["M_sb"][hl * 64:(hl + 1) * 64, :],
                    qkvgT[hl * 64:(hl + 1) * 64,
                          2 * 4096 + tb * 512:2 * 4096 + (tb + 1) * 512],
                    start=True, stop=True, tile_position=tpos)
            ins.then_inc(ctx.sT.h, 1)
        for tc in range(32):
            te.wait_ge(ctx.sV.h, fin_v[(tc // 4) * 2 + 1])
            if tc >= 2:
                te.wait_ge(ctx.sV.h, ft_v[tc - 2])
            te.transpose(t_ps[tc % 2],
                         s["fin"][:, tc * 128:(tc + 1) * 128],
                         r["ident"][:, :]).then_inc(ctx.sT.h, 1)

    b.tensor(t_o)

    def v_o(v):
        for tb in range(8):
            v.wait_ge(ctx.sT.h, o1_mm[tb])
            for hl in range(2):
                sl = slice(hl * 64, (hl + 1) * 64)
                v.scalar_tensor_tensor(
                    s["t1f"][sl, :], o1_ps[tb % 2][hl][sl, :], L1_SCALE,
                    qkvgT[sl, 2 * 4096 + tb * 512:2 * 4096 + (tb + 1) * 512],
                    ALU.mult, ALU.add)
            v.nop().then_inc(ctx.sV.h, 1)
            v.drain()
            v.scalar_tensor_tensor(
                s["fin"][:, tb * 512:(tb + 1) * 512], s["t1f"][:, :],
                s["z_sb"][:, 0:1],
                qkvgT[:, 3 * 4096 + tb * 512:3 * 4096 + (tb + 1) * 512],
                ALU.mult, ALU.mult).then_inc(ctx.sV.h, 1)
        for tc in range(32):
            v.wait_ge(ctx.sT.h, ft_mm[tc])
            if tc >= 2:
                v.wait_ge(ctx.sX.h, ft_x[tc - 2])
            v.tensor_copy(s["ftok"][:, (tc % 2) * 128:(tc % 2) * 128 + 128],
                          t_ps[tc % 2]).then_inc(ctx.sV.h, 1)

    b.vector(v_o)

    def q_h(g):
        for tc in range(32):
            g.wait_ge(ctx.sV.h, ft_v[tc])
            g.dma_start(
                out=s["bounce"].ap()
                .rearrange("(hh t j) -> hh t j", hh=8, t=TK)
                [pq * 2:pq * 2 + 2, tc * 128:(tc + 1) * 128, :]
                .transpose([1, 0, 2]),
                in_=s["ftok"][:, (tc % 2) * 128:(tc % 2) * 128 + 128]
                .rearrange("p (hh j) -> p hh j", hh=2),
            ).then_inc(ctx.sX.h, 16)

    b.sync(q_h)
    esC.close()


    prev["v"] = ft_v[-1]
    prev["t"] = ft_mm[-1]
    prev["x"] = ft_x[-1]


def _l1attn_proj(ctx, r, io, s):
    nc, b = ctx.nc, ctx.block
    from contextlib import ExitStack
    x = r["x"]
    flat, flatT = s["flat"], s["flatT"]
    prev = s["prev"]

    wpl = plan(ctx.sP, 1, step=16)
    for half in range(2):
        fl_x = plan(ctx.sW, 1, step=16)
        fT_mm = plan(ctx.sT, 64)
        fT_v = plan(ctx.sV, 64)
        pj_mm = plan(ctx.sT, 8)
        pj_v = plan(ctx.sV, 8)

        def g_fl(g, half=half, fl_x=fl_x):
            if half == 0:
                g.wait_ge(ctx.sX.h, prev["x"])
                g.wait_ge(ctx.sV.h, prev["v"])
                g.dma_start(out=s["wpT"],
                            in_=io["l1a_wpT"].rearrange("(tp p) n -> p tp n", p=128)
                            ).then_inc(ctx.sP.h, 16)
            else:
                g.wait_ge(ctx.sV.h, prev["pj"])
            g.dma_start(
                out=flat[:, :].rearrange("p (t d) -> p t d", t=8),
                in_=s["bounce"].ap()
                .rearrange("(n p d) -> p n d", p=128, d=D)
                [:, half * 8:(half + 1) * 8, :],
            ).then_inc(ctx.sW.h, 16)

        b.gpsimd(g_fl)

        with ExitStack() as es:
            ec = es.enter_context
            t_ps = [ec(nc.psum_tensor(f"pt{i}_{half}_{uid()}", [128, 128], BF16))[:, :]
                    for i in range(2)]
            p_ps = [ec(nc.psum_tensor(f"pp{i}_{half}_{uid()}", [128, 1024], F32))[:, :]
                    for i in range(2)]

            def t_f(te, half=half, fl_x=fl_x, fT_mm=fT_mm, fT_v=fT_v,
                    pj_mm=pj_mm, pj_v=pj_v):
                te.wait_ge(ctx.sW.h, fl_x[0])
                for i in range(64):
                    if i >= 2:
                        te.wait_ge(ctx.sV.h, fT_v[i - 2])
                    tt, dc = i // 8, i % 8
                    te.transpose(t_ps[i % 2],
                                 flat[:, tt * 1024 + dc * 128:
                                      tt * 1024 + (dc + 1) * 128],
                                 r["ident"][:, :]).then_inc(ctx.sT.h, 1)
                te.wait_ge(ctx.sP.h, wpl[0])
                for tt in range(8):
                    te.wait_ge(ctx.sV.h, fT_v[tt * 8 + 7])
                    if tt >= 2:
                        te.wait_ge(ctx.sV.h, pj_v[tt - 2])
                    for dc in range(DC):
                        for nb in range(2):
                            te.matmul(p_ps[tt % 2][:, nb * 512:(nb + 1) * 512],
                                      flatT[:, dc * 1024 + tt * 128:
                                            dc * 1024 + (tt + 1) * 128],
                                      s["wpT"][:, dc, nb * 512:(nb + 1) * 512],
                                      start=(dc == 0), stop=False)
                    for nb in range(2):
                        ins = te.matmul(p_ps[tt % 2][:, nb * 512:(nb + 1) * 512],
                                        r["ones"][0:1, 0:128],
                                        s["bp_row"][0:1, nb * 512:(nb + 1) * 512],
                                        start=False, stop=True)
                    ins.then_inc(ctx.sT.h, 1)

            b.tensor(t_f)

            def v_f(v, half=half, fT_mm=fT_mm, fT_v=fT_v, pj_mm=pj_mm, pj_v=pj_v):
                for i in range(64):
                    v.wait_ge(ctx.sT.h, fT_mm[i])
                    tt, dc = i // 8, i % 8
                    v.tensor_copy(flatT[:, dc * 1024 + tt * 128:
                                        dc * 1024 + (tt + 1) * 128],
                                  t_ps[i % 2]).then_inc(ctx.sV.h, 1)
                for tt in range(8):
                    g = half * 8 + tt
                    v.wait_ge(ctx.sT.h, pj_mm[tt])
                    v.tensor_add(x[:, g * D:(g + 1) * D], x[:, g * D:(g + 1) * D],
                                 p_ps[tt % 2]).then_inc(ctx.sV.h, 1)

            b.vector(v_f)

        prev["pj"] = pj_v[-1]



PHASESETS = {
    "l0ffn": ["l0ffn"],
    "l0attn": ["l0attn"],
    "l1ffn": ["l1ffn"],
    "l1attn": ["l1attn"],
}


def build_program(phases, replica_groups=RG):
    nc = bass.Bass("TRN2", target_bir_lowering=False, detect_race_conditions=False)
    io = {}
    io["x_in"] = nc.dram_tensor("x_in", [T, D], F32, kind="ExternalInput").ap()
    io["y"] = nc.dram_tensor("y", [T, D], F32, kind="ExternalOutput").ap()
    need = set(phases)
    if "l0ffn" in need:
        io["l0_rw"] = nc.dram_tensor("l0_rw", [D, 9], F32, kind="ExternalInput").ap()
        io["l0_rb_row"] = nc.dram_tensor("l0_rb_row", [1, 9], F32, kind="ExternalInput").ap()
        io["l0_w1"] = nc.dram_tensor("l0_w1", [E0, D, F0], F32, kind="ExternalInput").ap()
        io["l0_w2"] = nc.dram_tensor("l0_w2", [E0, F0, D], F32, kind="ExternalInput").ap()
    if "l0attn" in need:
        for nme in ["l0_wqT", "l0_wkT", "l0_wvT", "l0_woT"]:
            io[nme] = nc.dram_tensor(nme, [D, D], F32, kind="ExternalInput").ap()
        for nme in ["l0_bq", "l0_bk", "l0_bv"]:
            io[nme] = nc.dram_tensor(nme, [128, 8], F32, kind="ExternalInput").ap()
        io["l0_bo_row"] = nc.dram_tensor("l0_bo_row", [1, D], F32, kind="ExternalInput").ap()
    if "l1ffn" in need:
        io["l1f_gwT"] = nc.dram_tensor("l1f_gwT", [D, 16], F32, kind="ExternalInput").ap()
        io["l1f_gb_row"] = nc.dram_tensor("l1f_gb_row", [1, 16], F32, kind="ExternalInput").ap()
        io["l1f_wgT"] = nc.dram_tensor("l1f_wgT", [E1, D, I1], F32, kind="ExternalInput").ap()
        io["l1f_wuT"] = nc.dram_tensor("l1f_wuT", [E1, D, I1], F32, kind="ExternalInput").ap()
        io["l1f_wdT"] = nc.dram_tensor("l1f_wdT", [E1, I1, D], F32, kind="ExternalInput").ap()
        io["l1f_wsgT"] = nc.dram_tensor("l1f_wsgT", [D, SH], F32, kind="ExternalInput").ap()
        io["l1f_wsuT"] = nc.dram_tensor("l1f_wsuT", [D, SH], F32, kind="ExternalInput").ap()
        io["l1f_wsdT"] = nc.dram_tensor("l1f_wsdT", [SH, D], F32, kind="ExternalInput").ap()
    if "l1attn" in need:
        io["l1a_wT"] = nc.dram_tensor("l1a_wT", [D, 2048], F32, kind="ExternalInput").ap()
        io["l1a_b"] = nc.dram_tensor("l1a_b", [128, 16], F32, kind="ExternalInput").ap()
        io["l1a_wpT"] = nc.dram_tensor("l1a_wpT", [D, D], F32, kind="ExternalInput").ap()
        io["l1a_bp_row"] = nc.dram_tensor("l1a_bp_row", [1, D], F32, kind="ExternalInput").ap()
    n_cc = sum(1 for p in phases if p in ("l0attn", "l1attn"))
    cc_bufs = []
    for i in range(n_cc):
        cc_bufs.append({
            "in": nc.dram_tensor(f"cc_in{i}", [D, T], BF16, kind="Internal"),
            "out": nc.dram_tensor(f"cc_out{i}", [2 * D, T], BF16, kind="Internal"),
            "bounce": nc.dram_tensor(f"bounce{i}", [8 * TK * HD], BF16, kind="Internal"),
            "rg": replica_groups,
        })
    cc_idx = [0]

    ident_np = np.eye(128, dtype=ml_dtypes.bfloat16)
    ones_np = np.ones((1, 128), dtype=ml_dtypes.bfloat16)

    from contextlib import ExitStack
    with ExitStack() as es:
        ec = es.enter_context
        x_sb = ec(nc.sbuf_tensor("x_sb", [128, NT * D], F32))
        mean = ec(nc.sbuf_tensor("mean", [128, NT], F32))
        rstd = ec(nc.sbuf_tensor("rstd", [128, NT], F32))
        s1 = ec(nc.sbuf_tensor("s1", [128, NT], F32))
        s2 = ec(nc.sbuf_tensor("s2", [128, NT], F32))
        dump = ec(nc.sbuf_tensor("dump", [128, 1024], F32))
        ln_tok = ec(nc.sbuf_tensor("ln_tok", [128, 1024], BF16))
        tc = ec(nc.sbuf_tensor("tc", [128, 1], F32))
        tc2 = ec(nc.sbuf_tensor("tc2", [128, 1], F32))
        coef = ec(nc.sbuf_tensor("coef", [128, NT * 16], F32))
        ident = ec(nc.sbuf_tensor("ident", [128, 128], BF16))
        ones = ec(nc.sbuf_tensor("ones", [1, 128], BF16))
        hT = ec(nc.semaphore("sT"))
        hV = ec(nc.semaphore("sV"))
        hA = ec(nc.semaphore("sA"))
        hG = ec(nc.semaphore("sG"))
        hW = ec(nc.semaphore("sW"))
        hX = ec(nc.semaphore("sX"))
        hW0 = ec(nc.semaphore("sW0"))
        hW1 = ec(nc.semaphore("sW1"))
        hW2 = ec(nc.semaphore("sW2"))
        hW3 = ec(nc.semaphore("sW3"))
        hW4 = ec(nc.semaphore("sW4"))
        hW5 = ec(nc.semaphore("sW5"))
        hP = ec(nc.semaphore("sP"))
        block = ec(nc.Block())
        sems = dict(sT=Sem(hT), sV=Sem(hV), sA=Sem(hA), sG=Sem(hG),
                    sW=Sem(hW), sX=Sem(hX), sW0=Sem(hW0), sW1=Sem(hW1),
                    sW2=Sem(hW2), sW3=Sem(hW3), sW4=Sem(hW4), sW5=Sem(hW5),
                    sP=Sem(hP))
        ctx = Ctx(nc, block, sems)
        ident_dram = nc.inline_tensor(ident_np, "ident_const").ap()
        ones_dram = nc.inline_tensor(ones_np, "ones_const").ap()

        r = dict(x=x_sb[:, :], mean=mean[:, :], rstd=rstd[:, :],
                 s1=s1[:, :], s2=s2[:, :], dump=dump[:, :],
                 ln_tok=ln_tok[:, :], tc=tc[:, :], tc2=tc2[:, :],
                 coef=coef[:, :], ident=ident[:, :], ones=ones[:, :])

        ctx.sX.inc(48)

        def g_load(g):
            g.dma_start(out=x_sb[:, :].rearrange("p (t d) -> p t d", t=NT),
                        in_=io["x_in"].rearrange("(t p) d -> p t d", p=128)
                        ).then_inc(ctx.sX.h, 16)
            g.dma_start(out=ident[:, :], in_=ident_dram).then_inc(ctx.sX.h, 16)
            g.dma_start(out=ones[:, :], in_=ones_dram).then_inc(ctx.sX.h, 16)

        block.sync(g_load)
        emit_barrier(ctx)

        for ph in phases:
            if ph == "l0ffn":
                emit_l0ffn(ctx, r, io)
            elif ph == "l0attn":
                emit_l0attn(ctx, r, io, cc_bufs[cc_idx[0]])
                cc_idx[0] += 1
            elif ph == "l1ffn":
                emit_l1ffn(ctx, r, io)
            elif ph == "l1attn":
                emit_l1attn(ctx, r, io, cc_bufs[cc_idx[0]])
                cc_idx[0] += 1
            elif ph == "nop":
                pass
            elif ph == "lnonly":
                with nc.sbuf_tensor("lnT_dbg", [128, DC * T], BF16) as lnT_dbg:
                    emit_ln(ctx, r, lnT_dbg[:, :])
                    dbg = nc.dram_tensor("dbg", [128, DC * T], F32,
                                         kind="ExternalOutput").ap()
                    with nc.sbuf_tensor("dbg_f", [128, 2048], F32) as dbg_f:
                        dbg_v = plan(ctx.sV, 8)
                        dbg_x = plan(ctx.sX, 8, step=16)

                        def v_dbg(v):
                            for i in range(8):
                                if i >= 1:
                                    v.wait_ge(ctx.sX.h, dbg_x[i - 1])
                                v.tensor_copy(dbg_f[:, :],
                                              lnT_dbg[:, i * 2048:(i + 1) * 2048]
                                              ).then_inc(ctx.sV.h, 1)

                        def q_dbg(g):
                            for i in range(8):
                                g.wait_ge(ctx.sV.h, dbg_v[i])
                                g.dma_start(out=dbg[:, i * 2048:(i + 1) * 2048],
                                            in_=dbg_f[:, :]).then_inc(ctx.sX.h, 16)

                        ctx.block.vector(v_dbg)
                        ctx.block.sync(q_dbg)
                        emit_barrier(ctx)
                dbg2 = nc.dram_tensor("dbg2", [128, 5 * NT], F32,
                                      kind="ExternalOutput").ap()
                st_x = plan(ctx.sX, 1, step=16)

                def q_dbg2(g):
                    g.dma_start(out=dbg2[:, 0 * NT:1 * NT], in_=s1_t).then_inc(ctx.sX.h, 16)
                    g.dma_start(out=dbg2[:, 1 * NT:2 * NT], in_=s2_t).then_inc(ctx.sX.h, 16)
                    g.dma_start(out=dbg2[:, 2 * NT:3 * NT], in_=mean_t).then_inc(ctx.sX.h, 16)
                    g.dma_start(out=dbg2[:, 3 * NT:4 * NT], in_=var_t).then_inc(ctx.sX.h, 16)
                    g.dma_start(out=dbg2[:, 4 * NT:5 * NT], in_=rstd_t).then_inc(ctx.sX.h, 16)

                ctx.sX.inc(4 * 16)
                s1_t, s2_t = r["s1"], r["s2"]
                mean_t, var_t, rstd_t = r["mean"], r["s1"], r["rstd"]
                ctx.block.sync(q_dbg2)
                emit_barrier(ctx)
            else:
                raise ValueError(ph)

        ctx.sX.inc(16)

        def g_store(g):
            g.dma_start(out=io["y"].rearrange("(t p) d -> p t d", p=128),
                        in_=x_sb[:, :].rearrange("p (t d) -> p t d", t=NT)
                        ).then_inc(ctx.sX.h, 16)
            g.wait_ge(ctx.sX.h, ctx.sX.v)

        block.sync(g_store)

    return nc



def prep_inputs(inputs):
    f32 = np.float32
    w = {}
    w["l0_rw"] = np.ascontiguousarray(inputs["L0_ffn_router_w"], dtype=f32)
    w["l0_rb_row"] = np.ascontiguousarray(inputs["L0_ffn_router_b"], dtype=f32).reshape(1, 9)
    w["l0_w1"] = np.ascontiguousarray(inputs["L0_ffn_w1"], dtype=f32)
    w["l0_w2"] = np.ascontiguousarray(inputs["L0_ffn_w2"], dtype=f32)
    for nme, key in [("l0_wqT", "L0_attn_Wq"), ("l0_wkT", "L0_attn_Wk"),
                     ("l0_wvT", "L0_attn_Wv"), ("l0_woT", "L0_attn_Wo")]:
        w[nme] = np.ascontiguousarray(np.asarray(inputs[key], dtype=f32).T)
    for nme, key in [("l0_bq", "L0_attn_bq"), ("l0_bk", "L0_attn_bk"),
                     ("l0_bv", "L0_attn_bv")]:
        w[nme] = np.ascontiguousarray(
            np.asarray(inputs[key], dtype=f32).reshape(8, 128).T)
    w["l0_bo_row"] = np.ascontiguousarray(inputs["L0_attn_bo"], dtype=f32).reshape(1, D)
    w["l1f_gwT"] = np.ascontiguousarray(np.asarray(inputs["L1_ffn_gate_w"], dtype=f32).T)
    w["l1f_gb_row"] = np.ascontiguousarray(inputs["L1_ffn_gate_bias"], dtype=f32).reshape(1, 16)
    w["l1f_wgT"] = np.ascontiguousarray(np.asarray(inputs["L1_ffn_Wg"], dtype=f32).transpose(0, 2, 1))
    w["l1f_wuT"] = np.ascontiguousarray(np.asarray(inputs["L1_ffn_Wu"], dtype=f32).transpose(0, 2, 1))
    w["l1f_wdT"] = np.ascontiguousarray(np.asarray(inputs["L1_ffn_Wd"], dtype=f32).transpose(0, 2, 1))
    w["l1f_wsgT"] = np.ascontiguousarray(np.asarray(inputs["L1_ffn_Ws_g"], dtype=f32).T)
    w["l1f_wsuT"] = np.ascontiguousarray(np.asarray(inputs["L1_ffn_Ws_u"], dtype=f32).T)
    w["l1f_wsdT"] = np.ascontiguousarray(np.asarray(inputs["L1_ffn_Ws_d"], dtype=f32).T)

    w["l1a_wpT"] = np.ascontiguousarray(np.asarray(inputs["L1_attn_w_proj"], dtype=f32).T)
    w["l1a_bp_row"] = np.ascontiguousarray(inputs["L1_attn_b_proj"], dtype=f32).reshape(1, D)
    wq4 = np.asarray(inputs["L1_attn_w_qkvg"], dtype=f32)
    bq4 = np.asarray(inputs["L1_attn_b_qkvg"], dtype=f32)

    x = np.asarray(inputs["L0_attn_x"], dtype=f32)
    per_core = []
    for c in range(NCORES):
        bidx, h = c // 2, c % 2
        m = dict(w)
        m["x_in"] = np.ascontiguousarray(x[bidx, h * T:(h + 1) * T, :])
        rows = np.empty((4, 4, 2, 64), dtype=np.int64)
        for k in range(4):
            for pq in range(4):
                for hl in range(2):
                    hh = h * 8 + pq * 2 + hl
                    rows[k, pq, hl] = k * 1024 + hh * 64 + np.arange(64)
        ridx = rows.reshape(-1)
        m["l1a_wT"] = np.ascontiguousarray(wq4[ridx].T)
        m["l1a_b"] = np.ascontiguousarray(
            bq4[ridx].reshape(4, 4, 2 * 64).transpose(2, 1, 0).reshape(128, 16))
        per_core.append(m)
    return per_core


def launch_keys(phases):
    keys = ["x_in"]
    if "l0ffn" in phases:
        keys += ["l0_rw", "l0_rb_row", "l0_w1", "l0_w2"]
    if "l0attn" in phases:
        keys += ["l0_wqT", "l0_wkT", "l0_wvT", "l0_woT",
                 "l0_bq", "l0_bk", "l0_bv", "l0_bo_row"]
    if "l1ffn" in phases:
        keys += ["l1f_gwT", "l1f_gb_row", "l1f_wgT", "l1f_wuT", "l1f_wdT",
                 "l1f_wsgT", "l1f_wsuT", "l1f_wsdT"]
    if "l1attn" in phases:
        keys += ["l1a_wT", "l1a_b", "l1a_wpT", "l1a_bp_row"]
    return keys


def run_launch(nc, per_core, keys):
    in_maps = [{k: m[k] for k in keys} for m in per_core]
    res = run_bass_kernel_spmd(nc, in_maps, core_ids=list(range(NCORES)))
    return [r["y"] for r in res.results]


FULL_PHASES = ["l0ffn", "l0attn", "l0ffn", "l1ffn", "l1attn", "l1ffn"]


def kernel(**inputs):
    per_core = prep_inputs(inputs)
    nc = build_program(FULL_PHASES)
    ys = run_launch(nc, per_core, launch_keys(FULL_PHASES))
    out = np.empty((4, TK, D), dtype=np.float32)
    for c in range(NCORES):
        bidx, h = c // 2, c % 2
        out[bidx, h * T:(h + 1) * T, :] = ys[c]
    return out
